# version 11
# baseline (speedup 1.0000x reference)
import contextlib
import os
import numpy as np
import ml_dtypes
import concourse.bass as bass
import concourse.mybir as mybir
from concourse.bass_utils import run_bass_kernel_spmd

F32 = mybir.dt.float32
BF16 = mybir.dt.bfloat16
I32 = mybir.dt.int32
AF = mybir.ActivationFunctionType
OP = mybir.AluOpType
BF = ml_dtypes.bfloat16

D = 2048
R = 2560
H = 16384
T = 2048
DC, RC, HC = D // 128, R // 128, H // 128
NTAU = 4
KB_G = D // 2
GSL = H // 8
DSL = D // 8
TILE = 128 * 128

GC1 = 0.7978845608028654
GC3 = 0.044715 * GC1
LN_EPS = 1e-5

ENGS = ("sync", "gpsimd", "vector", "scalar", "tensor")


class Sem:
    def __init__(self, name):
        self.name = name
        self.h = None
        self.n = 0

    def inc(self, k=1):
        self.n += k
        return self.n


class Plan:
    def __init__(self):
        self.q = {e: [] for e in ENGS}
        self.lastw = {e: {} for e in ENGS}

    def emit(self, eng, fn):
        self.q[eng].append(fn)

    def wait(self, eng, sem, val=None):
        v = sem.n if val is None else val
        if v <= 0:
            return
        if self.lastw[eng].get(sem.name, 0) >= v:
            return
        self.lastw[eng][sem.name] = v
        self.q[eng].append(lambda E, s=sem, vv=v: E.wait_ge(s.h, vv))

    def ins(self, eng, build, inc=None, inck=1):
        if inc is None:
            self.emit(eng, build)
        else:
            inc.inc(inck)
            self.emit(eng, lambda E, b=build, s=inc, k=inck: b(E).then_inc(s.h, k))


class Arena:

    def __init__(self, handle, base, cap):
        self.t = handle
        self.base = base
        self.cap = cap
        self.cur = base

    def reset(self):
        self.cur = self.base

    def alloc(self, n_f32):
        n_f32 = (n_f32 + 1) & ~1
        off = self.cur
        self.cur += n_f32
        assert self.cur <= self.cap, f"arena overflow: {self.cur} > {self.cap}"
        return off

    def f32(self, off, cols, parts=(0, 128)):
        return self.t[parts[0]:parts[1], off:off + cols]

    def bf16(self, off, cols, parts=(0, 128)):
        assert cols % 2 == 0
        return self.t[parts[0]:parts[1], off:off + cols // 2].bitcast(BF16)

    def i32(self, off, cols, parts=(0, 128)):
        return self.t[parts[0]:parts[1], off:off + cols].bitcast(I32)


def ceil_div(a, b):
    return (a + b - 1) // b



class _Cut(Exception):
    pass


def build_program():
    PH = int(os.environ.get("ATH_PH", "9"))

    def cut(n):
        if PH < n:
            raise _Cut()

    nc = bass.Bass("TRN2", target_bir_lowering=False, debug=False)

    def ein(name, shape, dt):
        return nc.dram_tensor(name, shape, dt, kind="ExternalInput").ap()

    d_x = ein("x_fm", [D, T], F32)
    d_w1T = ein("w1T", [D, R], F32)
    d_convT = ein("convT", [4, R, R], F32)
    d_waT = ein("waT", [R, R], F32)
    d_wxT = ein("wxT", [R, R], F32)
    d_w2T = ein("w2T", [R, D], F32)
    d_vecs = ein("vecs", [128, 128], F32)
    d_ones = ein("onesb", [128, 130], BF16)
    d_ident = ein("ident", [128, 128], BF16)
    d_gpk = ein("gpk", [GSL, KB_G], I32)
    d_gsc = ein("gsc", [GSL, D // 32], F32)
    d_upk = ein("upk", [GSL, KB_G], I32)
    d_usc = ein("usc", [GSL, D // 32], F32)
    d_dpk = ein("dpk", [DSL, H // 2], I32)
    d_dsc = ein("dsc", [DSL, H // 32], F32)

    d_out = nc.dram_tensor("out_fm", [D, T], F32, kind="ExternalOutput").ap()

    ccA_in = [nc.dram_tensor(f"ccA_in{q}", [1024, 2048], BF16) for q in range(4)]
    ccA_out = [nc.dram_tensor(f"ccA_out{q}", [8192, 2048], BF16) for q in range(4)]
    ccB_in = nc.dram_tensor("ccB_in", [2048, 2048], BF16)
    ccB_out = nc.dram_tensor("ccB_out", [16384, 2048], BF16)
    attn_bf = nc.dram_tensor("attn_bf", [3040 * TILE], BF16)
    x1_buf = nc.dram_tensor("x1_buf", [D, T], F32)
    x2_buf = nc.dram_tensor("x2_buf", [D, T], F32)
    inter_buf = nc.dram_tensor("inter_buf", [H // 128, 128, T], BF16)
    hc_buf = nc.dram_tensor("hc_buf", [RC, 128, T], BF16)
    hl_buf = nc.dram_tensor("hl_buf", [RC, 128, T], BF16)
    ac_buf = nc.dram_tensor("ac_buf", [RC, 128, T], BF16)
    e1_in = nc.dram_tensor("e1_in", [128, 160], BF16)
    e1_out = nc.dram_tensor("e1_out", [1024, 160], BF16)
    e2_in = nc.dram_tensor("e2_in", [128, 40], BF16)
    e2_out = nc.dram_tensor("e2_out", [1024, 40], BF16)
    dbg_gu = (nc.dram_tensor("dbg_gu", [2, 128, 2048], F32, kind="ExternalOutput")
              if os.environ.get("ATH_DBG") else None)

    W1_BASE = 0
    CV_BASE = W1_BASE + 20 * 16
    WA_BASE = CV_BASE + 20 * 80
    WX_BASE = WA_BASE + 400
    W2_BASE = WX_BASE + 400
    N_TILES = W2_BASE + 320
    assert N_TILES == 3040

    def attn_tile_ap(tile_idx, ntiles=1):
        a = attn_bf.ap()[tile_idx * TILE:(tile_idx + ntiles) * TILE]
        return a.rearrange("(t p c) -> p t c", p=128, c=128)

    es = contextlib.ExitStack()
    with es:
        ar_t = es.enter_context(nc.sbuf_tensor("arena", [128, 48000], F32))
        pA = es.enter_context(nc.psum_tensor("pA", [128, 2048], F32))
        pB = es.enter_context(nc.psum_tensor("pB", [128, 2048], F32))

        sems = {}
        SEM_NAMES = ("sLD", "sGP", "sPE", "sDV", "sSC", "sCC", "sST",
                     "sQ0", "sQ1", "sSa", "sSb", "sSc", "sSd", "sS5a", "sS5b",
                     "sR0", "sR1", "sW0", "sW1", "sI0", "sI1", "sDX", "sDS",
                     "sT0", "sT1", "sHC", "sHS", "sHLS", "sHL",
                     "sX0", "sX1", "sO0", "sO1", "sE1L", "sE1S", "sE2L", "sE2S")
        for nm in SEM_NAMES:
            sems[nm] = Sem(nm)
            sems[nm].h = es.enter_context(nc.semaphore(nm))
        (sLD, sGP, sPE, sDV, sSC, sCC, sST,
         sQ0, sQ1, sSa, sSb, sSc, sSd, sS5a, sS5b,
         sR0, sR1, sW0, sW1, sI0, sI1, sDX, sDS,
         sT0, sT1, sHC, sHS, sHLS, sHL,
         sX0, sX1, sO0, sO1, sE1L, sE1S, sE2L, sE2S) = (sems[n] for n in SEM_NAMES)
        sQ = [sQ0, sQ1]
        sR = [sR0, sR1]
        sW = [sW0, sW1]
        sI = [sI0, sI1]
        sT = [sT0, sT1]
        sX = [sX0, sX1]
        sO = [sO0, sO1]
        sS5 = [sS5a, sS5b]

        V_B1, V_CB, V_BA, V_BX, V_LAM, V_B2 = 0, 20, 40, 60, 80, 100

        d_msk = ein("msk", [128, 32], F32)

        CONST_F32 = 0
        ar = Arena(ar_t, 0, 48000)
        A_VEC = ar.alloc(128)
        A_MSK = ar.alloc(32)
        A_ONE = ar.alloc(66)
        A_ID = ar.alloc(64)
        A_ZER = ar.alloc(512)
        A_SP = ar.alloc(20)
        A0 = ar.cur

        def vec_col(c0, n=1):
            return ar.f32(A_VEC + c0, n)

        ones_col = ar.bf16(A_ONE, 2)[:, 0:1]
        ones_row = ar.bf16(A_ONE, 130)[0:1, 1:129]
        ident = ar.bf16(A_ID, 128)
        zeros512 = ar.f32(A_ZER, 512)

        CONSTS = [GC3, GC1, 0.5, 7.5, -4.0, -6.0, -1.0, 1.0]
        for i, v in enumerate(CONSTS):
            nc.const_aps.register(F32, v, ar_t[:, A_VEC + 116 + i:A_VEC + 117 + i])

        plan = Plan()

        def dma(qeng, out_ap, in_ap, sem, deps=()):
            for (s, v) in deps:
                plan.wait(qeng, s, v)
            plan.ins(qeng, lambda E, o=out_ap, i=in_ap: E.dma_start(out=o, in_=i),
                     inc=sem, inck=16)
            return sem.n

        def comp(eng, build, sem=None, deps=()):
            for (s, v) in deps:
                plan.wait(eng, s, v)
            plan.ins(eng, build, inc=sem)
            return None if sem is None else sem.n

        def _emit_all():
            n0 = dma("sync", ar.f32(A_VEC, 128), d_vecs, sLD)
            dma("sync", ar.f32(A_MSK, 32), d_msk, sLD)
            dma("sync", ar.bf16(A_ONE, 130), d_ones, sLD)
            dma("sync", ar.bf16(A_ID, 128), d_ident, sLD)
            ld_consts = sLD.n
            comp("vector", lambda E: E.memset(zeros512, 0.0), sem=sDV)

            SP_T1 = A0
            lam = ar.f32(A_VEC + V_LAM, 20)
            t1 = ar.f32(SP_T1, 20)
            t2 = ar.f32(SP_T1 + 20, 20)
            comp("scalar", lambda E: E.activation(t1, lam, AF.Abs), deps=[(sLD, ld_consts)])
            comp("scalar", lambda E: E.activation(t1, t1, AF.Exp, scale=-1.0))
            comp("scalar", lambda E: E.activation(t1, t1, AF.Ln, bias=1.0), sem=sSC)
            comp("vector", lambda E: E.tensor_scalar_max(t2, lam, 0.0),
                 deps=[(sLD, ld_consts)])
            comp("vector", lambda E: E.tensor_add(ar.f32(A_SP, 20), t2, t1),
                 sem=sDV, deps=[(sSC, sSC.n)])
            sp_ready = (sDV, sDV.n)

            cut(1)
            ar.cur = A0 + 64
            DQ = []
            for par in range(2):
                d = {}
                d["pk"] = ar.alloc(1024)
                d["lo"] = ar.alloc(1024)
                d["hi"] = ar.alloc(1024)
                d["c"] = ar.alloc(2048)
                d["s"] = ar.alloc(2048)
                d["sc"] = ar.alloc(64)
                d["t"] = ar.alloc(2048)
                d["r4"] = ar.alloc(2048)
                d["r6"] = ar.alloc(2048)
                d["m"] = ar.alloc(2048)
                d["dq"] = ar.alloc(1024)
                d["st"] = ar.alloc(1024)
                DQ.append(d)
            assert ar.cur < 48000

            jobs = []
            for q in range(4):
                for i in range(4 * q, 4 * q + 4):
                    jobs.append(("g", d_gpk, d_gsc, i * 128, 0))
                    jobs.append(("u", d_upk, d_usc, i * 128, 0))
            for rr in range(2):
                for j in range(8):
                    jobs.append(("d", d_dpk, d_dsc, rr * 128, j * 1024))

            job_ld = []
            job_done = []
            ccA_done = []

            for ji, (kind, dpk_, dsc_, row0, kb0) in enumerate(jobs):
                d = DQ[ji % 2]
                deps = []
                if ji >= 2:
                    deps = [job_done[ji - 2]]
                qsem = sQ[ji % 2]
                v = dma("sync", ar.i32(d["pk"], 1024), dpk_[row0:row0 + 128, kb0:kb0 + 1024],
                        qsem, deps=deps)
                v = dma("sync", ar.f32(d["sc"], 64),
                        dsc_[row0:row0 + 128, kb0 // 16:kb0 // 16 + 64], qsem)
                job_ld.append((qsem, v))

                pk = ar.i32(d["pk"], 1024)
                lo = ar.i32(d["lo"], 1024)
                hi = ar.i32(d["hi"], 1024)
                cc = ar.f32(d["c"], 2048)
                ss = ar.f32(d["s"], 2048)
                tt_ = ar.f32(d["t"], 2048)
                r4 = ar.f32(d["r4"], 2048)
                r6 = ar.f32(d["r6"], 2048)
                mm_ = ar.f32(d["m"], 2048)
                dq = ar.bf16(d["dq"], 2048)
                stg = ar.bf16(d["st"], 2048)

                ldv = job_ld[ji]
                comp("vector", lambda E, a=lo, b=pk: E.tensor_single_scalar(a, b, 15, OP.bitwise_and), deps=[ldv])
                comp("vector", lambda E, a=hi, b=pk: E.tensor_single_scalar(a, b, 4, OP.logical_shift_right), sem=sDV)
                c3 = cc.rearrange("p (k two) -> p k two", two=2)
                plan.wait("gpsimd", sDV, sDV.n)
                comp("gpsimd", lambda E, a=c3[:, :, 0], b=lo: E.tensor_copy(a, b))
                comp("gpsimd", lambda E, a=c3[:, :, 1], b=hi: E.tensor_copy(a, b), sem=sGP)
                v_c = sGP.n
                comp("scalar", lambda E, a=ss, b=cc: E.activation(a, b, AF.Sign, bias=7.5, scale=-1.0),
                     sem=sSC, deps=[(sGP, v_c)])
                v_sgn = sSC.n
                comp("vector", lambda E, a=tt_, b=ss, c_=cc: E.scalar_tensor_tensor(a, b, 4.0, c_, OP.mult, OP.add),
                     deps=[(sSC, v_sgn), (sGP, v_c)])
                comp("vector", lambda E, a=tt_: E.tensor_scalar(a, a, -4.0, None, OP.add), sem=sDV)
                v_t = sDV.n
                comp("scalar", lambda E, a=r4, b=tt_: E.activation(a, b, AF.Relu, bias=-4.0),
                     deps=[(sDV, v_t)])
                comp("scalar", lambda E, a=r6, b=tt_: E.activation(a, b, AF.Relu, bias=-6.0), sem=sSC)
                v_r = sSC.n
                comp("vector", lambda E, a=r4, b=tt_: E.tensor_add(a, a, b), deps=[(sSC, v_r)])
                comp("vector", lambda E, a=mm_, b=r4, c_=r6: E.scalar_tensor_tensor(a, b, 0.5, c_, OP.mult, OP.add))
                comp("vector", lambda E, a=mm_, b=ss: E.tensor_mul(a, a, b))
                scb = ar.f32(d["sc"], 64).unsqueeze(2).broadcast_to([128, 64, 32])
                m3 = mm_.rearrange("p (blk j) -> p blk j", j=32)
                d3 = dq.rearrange("p (blk j) -> p blk j", j=32)
                comp("vector", lambda E, a=d3, b=m3, c_=scb: E.tensor_tensor(a, b, c_, OP.mult), sem=sDV)
                v_dq = sDV.n

                for kc in range(16):
                    bank = kc % 4
                    pt = pA[:, bank * 512:bank * 512 + 64].bitcast(BF16)[:, 0:128]
                    depsT = [(sDV, v_dq)]
                    if kc >= 4:
                        depsT.append((sSC, sSC.n))
                    comp("tensor", lambda E, o=pt, i=dq[:, kc * 128:(kc + 1) * 128]:
                         E.transpose(o, i, ident), sem=sPE, deps=depsT)
                    v_tr = sPE.n
                    comp("scalar", lambda E, o=stg[:, kc * 128:(kc + 1) * 128], i=pt:
                         E.activation(o, i, AF.Copy), sem=sSC, deps=[(sPE, v_tr)])
                v_stg = sSC.n

                if kind in ("g", "u"):
                    i_ch = row0 // 128
                    q = i_ch // 4
                    j = i_ch % 4
                    off = ((0 if kind == "g" else 4) + j) * 16 * TILE
                    dst = ccA_in[q].ap().rearrange("a b -> (a b)")[off:off + 16 * TILE]
                    dst = dst.rearrange("(t p c) -> p t c", p=128, c=128)
                else:
                    kc0 = kb0 // 64
                    dh = row0 // 128
                    base = ccB_in.ap().rearrange("a b -> (a b)")
                    dst = base[(kc0 * 2 + dh) * TILE:(kc0 * 2 + dh) * TILE + 15 * 2 * TILE + TILE]
                    dst = dst.rearrange("(t x) -> t x", x=TILE)[::2]
                    dst = dst.rearrange("t (p c) -> p t c", p=128, c=128)
                src = stg.rearrange("p (t c) -> p t c", c=128)
                if ji < 32:
                    ssem = [sSa, sSb][ji % 2]
                else:
                    ssem = [sSc, sSd][ji % 2]
                v = dma("gpsimd", dst, src, ssem, deps=[(sSC, v_stg)])
                job_done.append((ssem, v))
                if ji < 32 and ji % 8 == 7:
                    qq = ji // 8
                    plan.wait("gpsimd", job_done[ji][0], job_done[ji][1])
                    plan.wait("gpsimd", job_done[ji - 1][0], job_done[ji - 1][1])
                    plan.ins("gpsimd", lambda E, q_=qq: E.collective_compute(
                        "AllGather", OP.bypass, replica_groups=[list(range(8))],
                        ins=[ccA_in[q_].ap().opt()], outs=[ccA_out[q_].ap().opt()]),
                        inc=sCC)
                    ccA_done.append(sCC.n)

            plan.wait("gpsimd", sSc, sSc.n)
            plan.wait("gpsimd", sSd, sSd.n)
            plan.ins("gpsimd", lambda E: E.collective_compute(
                "AllGather", OP.bypass, replica_groups=[list(range(8))],
                ins=[ccB_in.ap().opt()], outs=[ccB_out.ap().opt()]), inc=sCC)
            ccB_done = sCC.n

            cut(2)
            P05 = {"ji": 0}
            CW_BASE = 48000 - 3842
            CW = [{"src": CW_BASE, "dst": CW_BASE + 2560}]

            cast_jobs = []
            for kc in range(16):
                cast_jobs.append((d_w1T[kc * 128:(kc + 1) * 128, :], 20,
                                  [W1_BASE + r * 16 + kc for r in range(20)]))
            for tap in range(4):
                for ic in range(20):
                    cast_jobs.append((d_convT[tap, ic * 128:(ic + 1) * 128, :], 20,
                                      [CV_BASE + (o * 4 + tap) * 20 + ic for o in range(20)]))
            for dc in range(20):
                cast_jobs.append((d_waT[dc * 128:(dc + 1) * 128, :], 20,
                                  [WA_BASE + e * 20 + dc for e in range(20)]))
            for dc in range(20):
                cast_jobs.append((d_wxT[dc * 128:(dc + 1) * 128, :], 20,
                                  [WX_BASE + e * 20 + dc for e in range(20)]))
            for rc_ in range(20):
                cast_jobs.append((d_w2T[rc_ * 128:(rc_ + 1) * 128, :], 16,
                                  [W2_BASE + dd * 20 + rc_ for dd in range(16)]))
            cj_done = []

            def emit_p05_job():
                ji = P05["ji"]
                if ji >= len(cast_jobs):
                    return False
                P05["ji"] = ji + 1
                src_ap, ncol, tidx = cast_jobs[ji]
                d = CW[0]
                deps = [cj_done[ji - 1]] if ji >= 1 else []
                v = dma("sync", ar.f32(d["src"], ncol * 128), src_ap, sQ[ji % 2], deps=deps)
                comp("vector", lambda E, o=ar.bf16(d["dst"], ncol * 128),
                     i=ar.f32(d["src"], ncol * 128):
                     E.tensor_copy(o, i), sem=sDV, deps=[(sQ[ji % 2], v)])
                v_c = sDV.n
                stride = (tidx[1] - tidx[0]) * TILE if ncol > 1 else TILE
                base = attn_bf.ap()[tidx[0] * TILE: tidx[0] * TILE + (ncol - 1) * stride + TILE]
                dst = base.rearrange("(t x) -> t x", x=TILE)[::stride // TILE]
                dst = dst.rearrange("t (p c) -> p t c", p=128, c=128)
                srcv = ar.bf16(d["dst"], ncol * 128).rearrange("p (t c) -> p t c", c=128)
                v = dma("gpsimd", dst, srcv, sS5[ji % 2], deps=[(sDV, v_c)])
                cj_done.append((sS5[ji % 2], v))
                return True

            def emit_p05_rest():
                while emit_p05_job():
                    pass
                P05['done'] = [(sS5a, sS5a.n), (sS5b, sS5b.n)]


            def emit_layernorm(src_dram, A_LN_, sc, extra_deps=(), chunk_deps=None):
                plan.wait("tensor", sDV, sDV.n)
                plan.wait("tensor", sSC, sSC.n)
                roll = [sc["roll"] + i * 2048 for i in range(2)]
                sqb = [sc["sqb"] + i * 1024 for i in range(2)]
                xb = [sc["xb"] + i * 1024 for i in range(2)]
                r_mean = ar.f32(sc["rows"], 2048, (0, 1))
                r_var = ar.f32(sc["rows"] + 2048, 2048, (0, 1))
                r_t = ar.f32(sc["rows"] + 4096, 2048, (0, 1))
                hv = sc["rows"] + 6144
                rstd_hi = ar.bf16(hv, 2048, (0, 1))
                rstd_lo = ar.bf16(hv + 1024, 2048, (0, 1))
                mr_hi = ar.bf16(hv + 2048, 2048, (0, 1))
                mr_lo = ar.bf16(hv + 3072, 2048, (0, 1))

                pe_marks = []
                for c in range(16):
                    d = c % 2
                    deps = list(extra_deps) if c == 0 else []
                    if chunk_deps is not None:
                        deps.append(chunk_deps(c))
                    if c >= 2:
                        deps.append((sPE, pe_marks[c - 2]))
                    v = dma("sync", ar.f32(roll[d], 2048), src_dram[c * 128:(c + 1) * 128, :],
                            sR[d], deps=deps)
                    comp("scalar", lambda E, o=ar.bf16(sqb[d], 2048), i=ar.f32(roll[d], 2048):
                         E.activation(o, i, AF.Square), deps=[(sR[d], v)])
                    comp("scalar", lambda E, o=ar.bf16(xb[d], 2048), i=ar.f32(roll[d], 2048):
                         E.activation(o, i, AF.Copy), sem=sSC)
                    v_s = sSC.n
                    plan.wait("tensor", sSC, v_s)
                    for tau in range(4):
                        comp("tensor", lambda E, o=pA[0:1, tau * 512:(tau + 1) * 512],
                             i=ar.bf16(xb[d], 2048)[:, tau * 512:(tau + 1) * 512],
                             st=(c == 0), sp=(c == 15):
                             E.matmul(o, ones_col, i, start=st, stop=sp))
                    for tau in range(4):
                        last = (tau == 3)
                        comp("tensor", lambda E, o=pB[0:1, tau * 512:(tau + 1) * 512],
                             i=ar.bf16(sqb[d], 2048)[:, tau * 512:(tau + 1) * 512],
                             st=(c == 0), sp=(c == 15):
                             E.matmul(o, ones_col, i, start=st, stop=sp),
                             sem=(sPE if last else None))
                    pe_marks.append(sPE.n)
                stats_done = sPE.n

                comp("vector", lambda E: E.tensor_scalar(r_mean, pA[0:1, :], 1.0 / D, None, OP.mult),
                     deps=[(sPE, stats_done)])
                comp("vector", lambda E: E.tensor_scalar(r_var, pB[0:1, :], 1.0 / D, None, OP.mult))
                comp("vector", lambda E: E.scalar_tensor_tensor(r_t, r_mean, -1.0, r_mean, OP.mult, OP.mult))
                comp("vector", lambda E: E.tensor_add(r_var, r_var, r_t))
                comp("vector", lambda E: E.tensor_scalar(r_var, r_var, LN_EPS, None, OP.add), sem=sDV)
                v_rows = sDV.n
                comp("scalar", lambda E: E.activation(r_t, r_var, AF.Ln), sem=sSC, deps=[(sDV, v_rows)])
                comp("vector", lambda E: E.tensor_scalar(r_t, r_t, -0.5, None, OP.mult), sem=sDV,
                     deps=[(sSC, sSC.n)])
                comp("scalar", lambda E: E.activation(r_var, r_t, AF.Exp), deps=[(sDV, sDV.n)])
                comp("scalar", lambda E: E.activation(rstd_hi, r_var, AF.Copy), sem=sSC)
                v_hi1 = sSC.n
                comp("vector", lambda E: E.tensor_mul(r_t, r_mean, r_var), deps=[(sSC, v_hi1)])
                comp("vector", lambda E: E.tensor_tensor(r_mean, r_var, rstd_hi, OP.subtract))
                comp("vector", lambda E: E.tensor_copy(rstd_lo, r_mean), sem=sDV)
                comp("scalar", lambda E: E.activation(mr_hi, r_t, AF.Copy), sem=sSC,
                     deps=[(sDV, sDV.n)])
                comp("vector", lambda E: E.tensor_tensor(r_mean, r_t, mr_hi, OP.subtract),
                     deps=[(sSC, sSC.n)])
                comp("vector", lambda E: E.tensor_copy(mr_lo, r_mean), sem=sDV)
                v_lo = sDV.n

                plan.wait("tensor", sDV, v_lo)
                for tau in range(4):
                    sl = slice(tau * 512, (tau + 1) * 512)
                    comp("tensor", lambda E, o=pA[:, sl], i=rstd_hi[:, sl]:
                         E.matmul(o, ones_row, i, start=True, stop=False))
                    comp("tensor", lambda E, o=pA[:, sl], i=rstd_lo[:, sl]:
                         E.matmul(o, ones_row, i, start=False, stop=True))
                    comp("tensor", lambda E, o=pB[:, sl], i=mr_hi[:, sl]:
                         E.matmul(o, ones_row, i, start=True, stop=False))
                    last = (tau == 3)
                    comp("tensor", lambda E, o=pB[:, sl], i=mr_lo[:, sl]:
                         E.matmul(o, ones_row, i, start=False, stop=True),
                         sem=(sPE if last else None))
                bcast_done = sPE.n

                stats_sc = sSC.n
                ap_marks = []
                for c in range(16):
                    d = c % 2
                    deps = [(sSC, stats_sc)] if c < 2 else [(sDV, ap_marks[c - 2])]
                    v = dma("sync", ar.f32(roll[d], 2048), src_dram[c * 128:(c + 1) * 128, :],
                            sR[d], deps=deps)
                    comp("vector", lambda E, o=ar.f32(sc["t1"], 2048), i=ar.f32(roll[d], 2048):
                         E.tensor_tensor(o, i, pA[:, :], OP.mult),
                         deps=[(sR[d], v), (sPE, bcast_done)])
                    comp("vector", lambda E, o=ar.bf16(A_LN_ + c * 1024, 2048), i=ar.f32(sc["t1"], 2048):
                         E.tensor_tensor(o, i, pB[:, :], OP.subtract), sem=sDV)
                    ap_marks.append(sDV.n)
                return sDV.n

            def emit_ffn(src_dram, dst_dram, first_call, ln_extra=(), mid=None, per_m=None):
                ar.cur = A0
                A_LN_ = ar.alloc(16384)
                sc = {"roll": ar.alloc(4096), "sqb": ar.alloc(2048), "xb": ar.alloc(2048),
                      "rows": ar.alloc(10240)}
                sc["t1"] = sc["rows"]
                ln_done = emit_layernorm(src_dram, A_LN_, sc, extra_deps=ln_extra)

                ar.cur = A0 + 16384
                G_GS = [ar.alloc(2048), ar.alloc(2048)]
                G_US = [ar.alloc(2048), ar.alloc(2048)]
                G_T1 = ar.alloc(2048)
                G_T2 = ar.alloc(2048)
                G_T3 = ar.alloc(2048)
                G_IS = [ar.alloc(1024), ar.alloc(1024)]
                G_WG = [ar.alloc(1024), ar.alloc(1024)]
                G_WU = [ar.alloc(1024), ar.alloc(1024)]

                def gup_src(m, which):
                    r_, loc = m // 16, m % 16
                    q, j = loc // 4, loc % 4
                    off = (r_ * 128 + ((0 if which == "g" else 4) + j) * 16) * TILE
                    a = ccA_out[q].ap().rearrange("a b -> (a b)")[off:off + 16 * TILE]
                    return a.rearrange("(t p c) -> p t c", p=128, c=128)

                def m_order(idx):
                    q, rest = idx // 32, idx % 32
                    r_, j = rest // 4, rest % 4
                    return r_ * 16 + 4 * q + j

                wld = []
                ev_marks = []
                ep_marks = []
                ep_v2 = []
                ep_v4 = []
                first_mm_dep = [(sDV, ln_done)]
                for mi in range(HC):
                    if per_m is not None:
                        per_m(mi)
                    m = m_order(mi)
                    d = mi % 2
                    deps = [(sCC, ccA_done[(m % 16) // 4])]
                    if mi < 2:
                        deps.append((sDV, ln_done))
                    else:
                        deps.append((sPE, ev_marks[mi - 2][2]))
                    v = dma("sync", ar.bf16(G_WG[d], 2048).rearrange("p (t c) -> p t c", c=128),
                            gup_src(m, "g"), sW[d], deps=deps)
                    v = dma("sync", ar.bf16(G_WU[d], 2048).rearrange("p (t c) -> p t c", c=128),
                            gup_src(m, "u"), sW[d])
                    wld.append((sW[d], v))

                    plan.wait("tensor", wld[mi][0], wld[mi][1])
                    if mi == 0:
                        for s_, vv in first_mm_dep:
                            plan.wait("tensor", s_, vv)
                    half_pe = []
                    for h2 in range(2):
                        if mi >= 1:
                            plan.wait("tensor", sSC, ev_marks[mi - 1][h2])
                        for k in range(16):
                            wg = ar.bf16(G_WG[d], 2048)[:, k * 128:(k + 1) * 128]
                            wu = ar.bf16(G_WU[d], 2048)[:, k * 128:(k + 1) * 128]
                            for tau in (2 * h2, 2 * h2 + 1):
                                ln_sl = ar.bf16(A_LN_ + k * 1024, 2048)[:, tau * 512:(tau + 1) * 512]
                                comp("tensor", lambda E, o=pA[:, tau * 512:(tau + 1) * 512], w=wg, i=ln_sl,
                                     st=(k == 0), sp=(k == 15):
                                     E.matmul(o, w, i, start=st, stop=sp))
                            for tau in (2 * h2, 2 * h2 + 1):
                                ln_sl = ar.bf16(A_LN_ + k * 1024, 2048)[:, tau * 512:(tau + 1) * 512]
                                last = (k == 15 and tau == 2 * h2 + 1)
                                comp("tensor", lambda E, o=pB[:, tau * 512:(tau + 1) * 512], w=wu, i=ln_sl,
                                     st=(k == 0), sp=(k == 15):
                                     E.matmul(o, w, i, start=st, stop=sp),
                                     sem=(sPE if last else None))
                        half_pe.append(sPE.n)
                    ev = []
                    for h2 in range(2):
                        sl = slice(h2 * 1024, (h2 + 1) * 1024)
                        evdeps = [(sPE, half_pe[h2])]
                        if h2 == 0 and mi >= 2:
                            evdeps.append((sDV, ep_v4[mi - 2]))
                        comp("scalar", lambda E, o=ar.f32(G_GS[d], 2048)[:, sl], i=pA[:, sl]:
                             E.activation(o, i, AF.Copy), deps=evdeps)
                        comp("scalar", lambda E, o=ar.f32(G_US[d], 2048)[:, sl], i=pB[:, sl]:
                             E.activation(o, i, AF.Copy), sem=sSC)
                        ev.append(sSC.n)
                    ev_marks.append((ev[0], ev[1], half_pe[1]))
                    if dbg_gu is not None and m == 5 and first_call:
                        dma("gpsimd", dbg_gu.ap()[0], ar.f32(G_GS[d], 2048), sGP,
                            deps=[(sSC, ev[1])])
                        dma("gpsimd", dbg_gu.ap()[1], ar.f32(G_US[d], 2048), sGP)

                    g_s = ar.f32(G_GS[d], 2048)
                    u_s = ar.f32(G_US[d], 2048)
                    sq_deps = [(sSC, ev[1])]
                    if mi >= 1:
                        sq_deps.append((sDV, ep_v2[mi - 1]))
                    comp("scalar", lambda E, o=ar.f32(G_T1, 2048), i=g_s:
                         E.activation(o, i, AF.Square), deps=sq_deps)
                    comp("scalar", lambda E, o=ar.f32(G_T1, 2048), i=ar.f32(G_T1, 2048):
                         E.activation(o, i, AF.Copy, bias=GC1, scale=GC3), sem=sSC)
                    v1 = sSC.n
                    if mi >= 2:
                        plan.wait("vector", ep_marks[mi - 2][0], ep_marks[mi - 2][1])
                    comp("vector", lambda E, o=ar.f32(G_T2, 2048), i=ar.f32(G_T1, 2048), g=g_s:
                         E.tensor_tensor(o, i, g, OP.mult), deps=[(sSC, v1)])
                    comp("vector", lambda E, o=ar.f32(G_T3, 2048), g=g_s, u=u_s:
                         E.tensor_tensor(o, g, u, OP.mult), sem=sDV)
                    v2 = sDV.n
                    ep_v2.append(v2)
                    comp("scalar", lambda E, o=ar.f32(G_T2, 2048), i=ar.f32(G_T2, 2048):
                         E.activation(o, i, AF.Tanh), deps=[(sDV, v2)])
                    comp("scalar", lambda E, o=ar.f32(G_T2, 2048), i=ar.f32(G_T2, 2048):
                         E.activation(o, i, AF.Copy, bias=0.5, scale=0.5), sem=sSC)
                    v3 = sSC.n
                    comp("vector", lambda E, o=ar.bf16(G_IS[d], 2048), a=ar.f32(G_T3, 2048),
                         b=ar.f32(G_T2, 2048): E.tensor_tensor(o, a, b, OP.mult),
                         sem=sDV, deps=[(sSC, v3)])
                    v4 = sDV.n
                    ep_v4.append(v4)
                    v = dma("gpsimd", inter_buf.ap()[m], ar.bf16(G_IS[d], 2048), sT[d],
                            deps=[(sDV, v4)])
                    ep_marks.append((sT[d], v))

                inter_all = [(sT0, sT0.n), (sT1, sT1.n)]
                if mid is not None:
                    mid()

                ar.cur = A0 + 16384
                DW_W = [ar.alloc(2048), ar.alloc(2048)]
                DW_I = [ar.alloc(4096), ar.alloc(4096)]
                DW_XB = [ar.alloc(4096), ar.alloc(4096)]

                dn_kg_pe = []
                pass_store = []
                ev_dv = 0
                for dg in range(4):
                    for th in range(2):
                        pidx = dg * 2 + th
                        DW_X = DW_XB[pidx % 2]
                        if pidx < 2:
                            xdeps = list(inter_all)
                        else:
                            xdeps = [pass_store[pidx - 2]]
                        for di in range(4):
                            dglob = dg * 4 + di
                            dma("sync", ar.f32(DW_X + di * 1024, 1024),
                                src_dram[dglob * 128:(dglob + 1) * 128, th * 1024:(th + 1) * 1024],
                                sDX, deps=xdeps if di == 0 else [])
                        x_ld = sDX.n
                        for kg in range(16):
                            d = kg % 2
                            deps = [(sCC, ccB_done)] + inter_all
                            if len(dn_kg_pe) >= 2:
                                deps.append((sPE, dn_kg_pe[-2]))
                            for si in range(2):
                                s_ = 2 * dg + si
                                off = (s_ * 256 + kg * 8 * 2) * TILE
                                a = ccB_out.ap().rearrange("a b -> (a b)")[off:off + 16 * TILE]
                                a = a.rearrange("(t p c) -> p t c", p=128, c=128)
                                dma("sync", ar.bf16(DW_W[d] + si * 1024, 2048)
                                    .rearrange("p (t c) -> p t c", c=128), a, sI[d],
                                    deps=deps if si == 0 else [])
                            iap = inter_buf.ap()[kg * 8:(kg + 1) * 8, :, th * 1024:(th + 1) * 1024]
                            v_ld = dma("sync", ar.bf16(DW_I[d], 8192).rearrange("p (t c) -> p t c", c=1024),
                                       iap.rearrange("t p c -> p t c"), sI[d])
                            plan.wait("tensor", sI[d], v_ld)
                            if kg == 0 and pidx > 0:
                                plan.wait("tensor", sDV, ev_dv)
                            for kc in range(8):
                                kcg = kg * 8 + kc
                                for di in range(4):
                                    si, dh = di // 2, di % 2
                                    wt = ar.bf16(DW_W[d] + si * 1024, 2048)[:, (kc * 2 + dh) * 128:(kc * 2 + dh + 1) * 128]
                                    pT = pA if di < 2 else pB
                                    bank0 = (di % 2) * 1024
                                    for j in range(2):
                                        it = ar.bf16(DW_I[d], 8192)[:, kc * 1024 + j * 512: kc * 1024 + (j + 1) * 512]
                                        last = (kc == 7 and di == 3 and j == 1)
                                        comp("tensor", lambda E, o=pT[:, bank0 + j * 512: bank0 + (j + 1) * 512],
                                             w=wt, i=it, st=(kcg == 0), sp=(kcg == 127):
                                             E.matmul(o, w, i, start=st, stop=sp),
                                             sem=(sPE if last else None))
                            dn_kg_pe.append(sPE.n)
                        dn_pe = sPE.n
                        for di in range(4):
                            dglob = dg * 4 + di
                            pT = pA if di < 2 else pB
                            bank0 = (di % 2) * 1024
                            comp("vector", lambda E, o=ar.f32(DW_X + di * 1024, 1024),
                                 p=pT[:, bank0:bank0 + 1024], x=ar.f32(DW_X + di * 1024, 1024):
                                 E.scalar_tensor_tensor(o, p, 0.5, x, OP.mult, OP.add),
                                 sem=sDV, deps=[(sPE, dn_pe), (sDX, x_ld)])
                            v_ev = sDV.n
                            dma("gpsimd", dst_dram[dglob * 128:(dglob + 1) * 128, th * 1024:(th + 1) * 1024],
                                ar.f32(DW_X + di * 1024, 1024), sDS, deps=[(sDV, v_ev)])
                        ev_dv = sDV.n
                        pass_store.append((sDS, sDS.n))
                return (sDS, sDS.n), pass_store

            cut(3)
            def p05_per_m(mi):
                want = (mi + 1) * len(cast_jobs) // HC
                while P05["ji"] < want:
                    if not emit_p05_job():
                        break

            ffn1_st, ffn1_passes = emit_ffn(d_x, x1_buf.ap(), True,
                                            ln_extra=[job_done[-1], job_done[-2]],
                                            mid=emit_p05_rest, per_m=p05_per_m)
            cut(4)

            ar.cur = A0
            A_LN2 = ar.alloc(16384)
            A_H = ar.alloc(20544)
            ROLL2 = ar.alloc(6144)
            W1B = [ar.alloc(1024), ar.alloc(1024)]
            sc2 = {"sqb": A_LN2, "xb": A_LN2 + 2048, "rows": A_LN2 + 4096,
                   "roll": ROLL2, "t1": ROLL2 + 4096}
            assert ar.cur <= 48000
            ln2_done = emit_layernorm(x1_buf.ap(), A_LN2, sc2,
                                      chunk_deps=lambda c: ffn1_passes[(c // 4) * 2 + 1])

            def h_view(c, col0, ncols, dtype_bf=True):
                off = A_H
                a = ar.bf16(off, 20544 * 2)
                return a[:, c * 2054 + col0: c * 2054 + col0 + ncols]

            def h_view3(col0, ncols):
                a = ar.bf16(A_H, 20544 * 2)[:, 0:20 * 2054]
                return a.rearrange("p (c t) -> p c t", t=2054)[:, :, col0:col0 + ncols]

            w1_pe = []
            w1_ev = []
            for r_ in range(20):
                d = r_ % 2
                deps = P05['done'] + [(sDV, ln2_done)]
                if r_ >= 2:
                    deps.append((sPE, w1_pe[r_ - 2]))
                v = dma("sync", ar.bf16(W1B[d], 2048).rearrange("p (t c) -> p t c", c=128),
                        attn_tile_ap(W1_BASE + r_ * 16, 16), sW[d], deps=deps)
                plan.wait("tensor", sW[d], v)
                if r_ == 0:
                    plan.wait("tensor", sDV, ln2_done)
                pT = pA if r_ % 2 == 0 else pB
                if r_ >= 2:
                    plan.wait("tensor", sSC, w1_ev[r_ - 2])
                for k in range(16):
                    w = ar.bf16(W1B[d], 2048)[:, k * 128:(k + 1) * 128]
                    for tau in range(4):
                        ln_sl = ar.bf16(A_LN2 + k * 1024, 2048)[:, tau * 512:(tau + 1) * 512]
                        last = (k == 15 and tau == 3)
                        comp("tensor", lambda E, o=pT[:, tau * 512:(tau + 1) * 512], w=w, i=ln_sl,
                             st=(k == 0), sp=(k == 15):
                             E.matmul(o, w, i, start=st, stop=sp),
                             sem=(sPE if last else None))
                w1_pe.append(sPE.n)
                for tau in range(4):
                    last = (tau == 3)
                    comp("scalar", lambda E, o=h_view(r_, 3 + tau * 512, 512),
                         i=pT[:, tau * 512:(tau + 1) * 512], b=vec_col(V_B1 + r_):
                         E.activation(o, i, AF.Silu, bias=b), sem=(sSC if last else None),
                         deps=[(sPE, w1_pe[r_])] if tau == 0 else [])
                w1_ev.append(sSC.n)
            w1_done = sSC.n

            E1S = ar.alloc(80)
            e1p = ar.bf16(E1S, 160)
            e1p3 = e1p.rearrange("p (c k) -> p c k", k=8)
            comp("vector", lambda E: E.tensor_copy(e1p3[:, :, 0:3], h_view3(3, 3)),
                 deps=[(sSC, w1_done)])
            comp("vector", lambda E: E.tensor_copy(e1p3[:, :, 3:6], h_view3(2048, 3)), sem=sDV)
            v = dma("gpsimd", e1_in.ap(), e1p, sE1S, deps=[(sDV, sDV.n)])
            plan.wait("gpsimd", sE1S, v)
            plan.ins("gpsimd", lambda E: E.collective_compute(
                "AllGather", OP.bypass, replica_groups=[list(range(8))],
                ins=[e1_in.ap().opt()], outs=[e1_out.ap().opt()]), inc=sCC)
            e1_done = sCC.n
            E1G = ar.alloc(8 * 80)
            v = None
            for j in range(8):
                v = dma("sync", ar.bf16(E1G + j * 80, 160), e1_out.ap()[j * 128:(j + 1) * 128, :],
                        sE1L, deps=[(sCC, e1_done)] if j == 0 else [])
            e1g_ld = sE1L.n
            ACCL = ar.alloc(120)
            accl = ar.f32(ACCL, 60).rearrange("p (c k) -> p c k", k=3)
            accr = ar.f32(ACCL + 60, 60).rearrange("p (c k) -> p c k", k=3)
            for j in range(8):
                g = ar.bf16(E1G + j * 80, 160).rearrange("p (c k) -> p c k", k=8)
                lsel = ar.f32(A_MSK + 16 + j, 1)
                rsel = ar.f32(A_MSK + 24 + j, 1)
                if j == 0:
                    comp("vector", lambda E, a=accl, b=g[:, :, 3:6], s=lsel:
                         E.tensor_scalar(a, b, s, None, OP.mult), deps=[(sE1L, e1g_ld)])
                    comp("vector", lambda E, a=accr, b=g[:, :, 0:3], s=rsel:
                         E.tensor_scalar(a, b, s, None, OP.mult))
                else:
                    comp("vector", lambda E, a=accl, b=g[:, :, 3:6], s=lsel:
                         E.scalar_tensor_tensor(a, b, s, a, OP.mult, OP.add))
                    comp("vector", lambda E, a=accr, b=g[:, :, 0:3], s=rsel:
                         E.scalar_tensor_tensor(a, b, s, a, OP.mult, OP.add))
            comp("vector", lambda E: E.tensor_copy(h_view3(0, 3), accl))
            comp("vector", lambda E: E.tensor_copy(h_view3(2051, 3), accr), sem=sDV)
            halo_done = sDV.n

            cut(5)
            TAIL = A_H + 20544
            assert TAIL + 10240 <= 48000, TAIL
            CVB = [A_LN2, TAIL]
            HCS = [A_LN2 + 10240, A_LN2 + 11264]
            cv_pe = []
            cv_st = []
            for o in range(20):
                d = o % 2
                deps = [(sDV, halo_done)]
                if o >= 2:
                    deps.append((sPE, cv_pe[o - 2]))
                v = dma("sync", ar.bf16(CVB[d], 10240).rearrange("p (t c) -> p t c", c=128),
                        attn_tile_ap(CV_BASE + o * 80, 80), sW[d], deps=deps)
                plan.wait("tensor", sW[d], v)
                plan.wait("tensor", sDV, halo_done)
                pT = pA if o % 2 == 0 else pB
                if o >= 2:
                    plan.wait("tensor", sDV, cv_st[o - 2])
                for tap in range(4):
                    for ic in range(20):
                        w = ar.bf16(CVB[d], 10240)[:, (tap * 20 + ic) * 128:(tap * 20 + ic + 1) * 128]
                        for tau in range(4):
                            i_ap = h_view(ic, tau * 512 + 2 * tap, 512)
                            last = (tap == 3 and ic == 19 and tau == 3)
                            comp("tensor", lambda E, oo=pT[:, tau * 512:(tau + 1) * 512], w=w, i=i_ap,
                                 st=(tap == 0 and ic == 0), sp=(tap == 3 and ic == 19):
                                 E.matmul(oo, w, i, start=st, stop=sp),
                                 sem=(sPE if last else None))
                cv_pe.append(sPE.n)
                comp("vector", lambda E, o_=ar.bf16(HCS[d], 2048), i=pT[:, :], b=vec_col(V_CB + o):
                     E.tensor_scalar(o_, i, b, None, OP.add), sem=sDV, deps=[(sPE, cv_pe[o])])
                v_hc = sDV.n
                dma("gpsimd", hc_buf.ap()[o], ar.bf16(HCS[d], 2048), sHS, deps=[(sDV, v_hc)])
                cv_st.append(v_hc)
            conv_st = sHS.n

            cut(6)
            ar.cur = A0
            P_HC = ar.alloc(5120)
            P_A = ar.alloc(10240)
            P_B = ar.alloc(10240)
            P_C = ar.alloc(10240)
            P_W = [ar.alloc(2560), ar.alloc(2560)]
            P_CAR = ar.alloc(48)
            gates_dv = 0

            hcv = ar.bf16(P_HC, 10240)
            Av = ar.f32(P_A, 10240)
            Bv = ar.f32(P_B, 10240)
            Cv = ar.f32(P_C, 10240)
            HLv = ar.bf16(P_C, 10240)
            ACv = ar.bf16(P_C + 5120, 10240)
            car_hl = ar.bf16(P_CAR, 40)
            car_ac = ar.bf16(P_CAR + 24, 40)

            ew_pe = []
            for tau in range(4):
                deps = [(sHS, conv_st)]
                if tau >= 1:
                    deps.append((sDV, gates_dv))
                    deps.append((sGP, gates_gp))
                for c in range(20):
                    dma("sync", hcv[:, c * 512:(c + 1) * 512],
                        hc_buf.ap()[c, :, tau * 512:(tau + 1) * 512], sHC,
                        deps=deps if c == 0 else [])
                hc_ld = sHC.n
                epe = []
                eev = []
                for e in range(20):
                    d = e % 2
                    depw = []
                    if e >= 2:
                        depw.append((sPE, epe[e - 2]))
                    v = dma("sync", ar.bf16(P_W[d], 2560).rearrange("p (t c) -> p t c", c=128),
                            attn_tile_ap(WA_BASE + e * 20, 20), sW[d], deps=depw)
                    v = dma("sync", ar.bf16(P_W[d] + 1280, 2560).rearrange("p (t c) -> p t c", c=128),
                            attn_tile_ap(WX_BASE + e * 20, 20), sW[d])
                    plan.wait("tensor", sW[d], v)
                    plan.wait("tensor", sHC, hc_ld)
                    if e >= 2:
                        plan.wait("tensor", sSC, eev[e - 2])
                    bnk_r = pA[:, (e % 2) * 512:(e % 2) * 512 + 512]
                    bnk_i = pA[:, 1024 + (e % 2) * 512:1024 + (e % 2) * 512 + 512]
                    for dc in range(20):
                        w = ar.bf16(P_W[d], 2560)[:, dc * 128:(dc + 1) * 128]
                        comp("tensor", lambda E, o=bnk_r, w_=w, i=hcv[:, dc * 512:(dc + 1) * 512],
                             st=(dc == 0), sp=(dc == 19):
                             E.matmul(o, w_, i, start=st, stop=sp))
                    for dc in range(20):
                        w = ar.bf16(P_W[d] + 1280, 2560)[:, dc * 128:(dc + 1) * 128]
                        last = (dc == 19)
                        comp("tensor", lambda E, o=bnk_i, w_=w, i=hcv[:, dc * 512:(dc + 1) * 512],
                             st=(dc == 0), sp=(dc == 19):
                             E.matmul(o, w_, i, start=st, stop=sp),
                             sem=(sPE if last else None))
                    epe.append(sPE.n)
                    comp("scalar", lambda E, o=Av[:, e * 512:(e + 1) * 512], i=bnk_r, b=vec_col(V_BA + e):
                         E.activation(o, i, AF.Sigmoid, bias=b), deps=[(sPE, epe[e])])
                    comp("scalar", lambda E, o=Bv[:, e * 512:(e + 1) * 512], i=bnk_i, b=vec_col(V_BX + e):
                         E.activation(o, i, AF.Sigmoid, bias=b), sem=sSC)
                    eev.append(sSC.n)
                gates_sc = sSC.n
                for c in range(20):
                    comp("vector", lambda E, o=Av[:, c * 512:(c + 1) * 512], s=ar.f32(A_SP + c, 1):
                         E.tensor_scalar(o, o, s, -8.0, OP.mult, OP.mult),
                         sem=(sDV if c == 19 else None),
                         deps=[(sSC, gates_sc), sp_ready] if c == 0 else [])
                v_loga = sDV.n
                comp("scalar", lambda E: E.activation(Av, Av, AF.Exp), deps=[(sDV, v_loga)])
                comp("scalar", lambda E: E.activation(Cv, Av, AF.Square), sem=sSC)
                v_a = sSC.n
                comp("vector", lambda E: E.tensor_scalar(Cv, Cv, -1.0, 1.0, OP.mult, OP.add),
                     sem=sDV, deps=[(sSC, v_a)])
                comp("scalar", lambda E: E.activation(Cv, Cv, AF.Sqrt), sem=sSC,
                     deps=[(sDV, sDV.n)])
                v_mult = sSC.n
                comp("vector", lambda E: E.tensor_tensor(Bv, Bv, hcv, OP.mult),
                     deps=[(sSC, v_mult)])
                comp("vector", lambda E: E.tensor_tensor(Bv, Bv, Cv, OP.mult), sem=sDV)
                v_b = sDV.n
                plan.wait("vector", sHLS, sHLS.n)
                for c in range(20):
                    if tau == 0:
                        comp("vector", lambda E, o=HLv[:, c * 512:(c + 1) * 512],
                             a=Av[:, c * 512:(c + 1) * 512], b=Bv[:, c * 512:(c + 1) * 512]:
                             E.tensor_tensor_scan(o, a, b, 0.0, OP.mult, OP.add))
                        comp("vector", lambda E, o=ACv[:, c * 512:(c + 1) * 512],
                             a=Av[:, c * 512:(c + 1) * 512]:
                             E.tensor_tensor_scan(o, a, zeros512, 1.0, OP.mult, OP.add))
                    else:
                        comp("vector", lambda E, o=HLv[:, c * 512:(c + 1) * 512],
                             a=Av[:, c * 512:(c + 1) * 512], b=Bv[:, c * 512:(c + 1) * 512],
                             i=car_hl[:, c:c + 1]:
                             E.tensor_tensor_scan(o, a, b, i, OP.mult, OP.add))
                        comp("vector", lambda E, o=ACv[:, c * 512:(c + 1) * 512],
                             a=Av[:, c * 512:(c + 1) * 512], i=car_ac[:, c:c + 1]:
                             E.tensor_tensor_scan(o, a, zeros512, i, OP.mult, OP.add))
                hl_last = HLv.rearrange("p (c t) -> p c t", t=512)[:, :, 511]
                ac_last = ACv.rearrange("p (c t) -> p c t", t=512)[:, :, 511]
                comp("vector", lambda E, o=car_hl[:, 0:20], i=hl_last: E.tensor_copy(o, i))
                comp("vector", lambda E, o=car_ac[:, 0:20], i=ac_last: E.tensor_copy(o, i),
                     sem=sDV)
                v_scan = sDV.n
                v_scan_gp = sGP.n
                for c in range(20):
                    dma("gpsimd", hl_buf.ap()[c, :, tau * 512:(tau + 1) * 512],
                        HLv[:, c * 512:(c + 1) * 512], sHLS,
                        deps=[(sDV, v_scan), (sGP, v_scan_gp)] if c == 0 else [])
                for c in range(20):
                    dma("gpsimd", ac_buf.ap()[c, :, tau * 512:(tau + 1) * 512],
                        ACv[:, c * 512:(c + 1) * 512], sHLS)
                gates_dv = v_scan
                gates_gp = v_scan_gp
            hlac_st = sHLS.n

            E2S = ar.alloc(20)
            e2p = ar.bf16(E2S, 40).rearrange("p (c k) -> p c k", k=2)
            comp("vector", lambda E: E.tensor_copy(e2p[:, :, 0], car_ac[:, 0:20]))
            comp("vector", lambda E: E.tensor_copy(e2p[:, :, 1], car_hl[:, 0:20]), sem=sDV)
            v = dma("gpsimd", e2_in.ap(), ar.bf16(E2S, 40), sE2S, deps=[(sDV, sDV.n)])
            plan.wait("gpsimd", sE2S, v)
            plan.ins("gpsimd", lambda E: E.collective_compute(
                "AllGather", OP.bypass, replica_groups=[list(range(8))],
                ins=[e2_in.ap().opt()], outs=[e2_out.ap().opt()]), inc=sCC)
            e2_done = sCC.n
            ar.cur = A0
            E2G = ar.alloc(8 * 20)
            for j in range(8):
                dma("sync", ar.bf16(E2G + j * 20, 40), e2_out.ap()[j * 128:(j + 1) * 128, :],
                    sE2L, deps=[(sCC, e2_done), (sHLS, hlac_st)] if j == 0 else [])
            e2g_ld = sE2L.n
            CFOLD = ar.alloc(9 * 20)
            HIN = ar.alloc(20)
            h_in = ar.f32(HIN, 20)
            comp("vector", lambda E: E.memset(ar.f32(CFOLD, 20), 0.0), deps=[(sE2L, e2g_ld)])
            comp("vector", lambda E: E.memset(h_in, 0.0))
            for j in range(7):
                g = ar.bf16(E2G + j * 20, 40).rearrange("p (c k) -> p c k", k=2)
                cj = ar.f32(CFOLD + j * 20, 20)
                cj1 = ar.f32(CFOLD + (j + 1) * 20, 20)
                comp("vector", lambda E, o=cj1, a=g[:, :, 0], c_=cj: E.tensor_tensor(o, a, c_, OP.mult))
                comp("vector", lambda E, o=cj1, b=g[:, :, 1]: E.tensor_tensor(o, o, b, OP.add))
                comp("vector", lambda E, o=cj1, s=ar.f32(A_MSK + j + 1, 1):
                     E.tensor_scalar(o, o, s, None, OP.mult))
            for j in range(1, 8):
                cj = ar.f32(CFOLD + j * 20, 20)
                comp("vector", lambda E, o=h_in, c_=cj, s=ar.f32(A_MSK + 8 + j, 1):
                     E.scalar_tensor_tensor(o, c_, s, o, OP.mult, OP.add),
                     sem=(sDV if j == 7 else None))
            hin_done = sDV.n

            cut(7)
            P3_HL = [ar.alloc(5120), ar.alloc(5120)]
            P3_AC = [ar.alloc(5120), ar.alloc(5120)]
            P3_HC = [ar.alloc(5120), ar.alloc(5120)]
            P3_W = [ar.alloc(1280), ar.alloc(1280)]
            P3_X = [ar.alloc(512), ar.alloc(512)]
            P3_O = [ar.alloc(512), ar.alloc(512)]
            dst_marks = []
            p3_dv = 0
            x2_st = 0
            p3_pe_marks = []
            for tau in range(4):
                tp = tau % 2
                hl_t = ar.bf16(P3_HL[tp], 10240)
                ac_t = ar.bf16(P3_AC[tp], 10240)
                hcor = ar.bf16(P3_HC[tp], 10240)
                deps = [(sHLS, hlac_st), (sDV, hin_done)]
                if tau >= 2:
                    deps.append((sPE, p3_pe_marks[tau - 2]))
                for c in range(20):
                    dma("sync", hl_t[:, c * 512:(c + 1) * 512],
                        hl_buf.ap()[c, :, tau * 512:(tau + 1) * 512], sHL,
                        deps=deps if c == 0 else [])
                for c in range(20):
                    dma("sync", ac_t[:, c * 512:(c + 1) * 512],
                        ac_buf.ap()[c, :, tau * 512:(tau + 1) * 512], sHL)
                v_ld = sHL.n
                for c in range(20):
                    comp("vector", lambda E, o=hcor[:, c * 512:(c + 1) * 512],
                         a=ac_t[:, c * 512:(c + 1) * 512], b=hl_t[:, c * 512:(c + 1) * 512],
                         s=ar.f32(HIN + c, 1):
                         E.scalar_tensor_tensor(o, a, s, b, OP.mult, OP.add),
                         sem=(sDV if c == 19 else None),
                         deps=[(sHL, v_ld), (sDV, hin_done)] if c == 0 else [])
                v_cor = sDV.n
                dpe = []
                dev = []
                for dd in range(16):
                    d = dd % 2
                    depw = []
                    if dd >= 2:
                        depw.append((sPE, dpe[dd - 2]))
                    if len(dst_marks) >= 2:
                        depw.append(dst_marks[-2])
                    v = dma("sync", ar.bf16(P3_W[d], 2560).rearrange("p (t c) -> p t c", c=128),
                            attn_tile_ap(W2_BASE + dd * 20, 20), sW[d], deps=depw)
                    vx = dma("sync", ar.f32(P3_X[d], 512),
                             x1_buf.ap()[dd * 128:(dd + 1) * 128, tau * 512:(tau + 1) * 512], sX[d])
                    plan.wait("tensor", sW[d], v)
                    plan.wait("tensor", sDV, v_cor)
                    if dd >= 4:
                        plan.wait("tensor", sDV, dev[dd - 4])
                    bank = [pA[:, 0:512], pA[:, 512:1024], pB[:, 0:512], pB[:, 512:1024]][dd % 4]
                    for rc_ in range(20):
                        w = ar.bf16(P3_W[d], 2560)[:, rc_ * 128:(rc_ + 1) * 128]
                        last = (rc_ == 19)
                        comp("tensor", lambda E, o=bank, w_=w, i=hcor[:, rc_ * 512:(rc_ + 1) * 512],
                             st=(rc_ == 0), sp=(rc_ == 19):
                             E.matmul(o, w_, i, start=st, stop=sp),
                             sem=(sPE if last else None))
                    dpe.append(sPE.n)
                    comp("vector", lambda E, o=ar.f32(P3_O[d], 512), p=bank,
                         b=vec_col(V_B2 + dd), x=ar.f32(P3_X[d], 512):
                         E.scalar_tensor_tensor(o, p, b, x, OP.add, OP.add),
                         sem=sDV, deps=[(sPE, dpe[dd]), (sX[d], vx)])
                    dev.append(sDV.n)
                    dma("gpsimd", x2_buf.ap()[dd * 128:(dd + 1) * 128, tau * 512:(tau + 1) * 512],
                        ar.f32(P3_O[d], 512), sO[d], deps=[(sDV, dev[dd])])
                    dst_marks.append((sO[d], sO[d].n))
                p3_dv = sDV.n
                p3_pe_marks.append(sPE.n)
            x2_st = [(sO0, sO0.n), (sO1, sO1.n)]

            cut(8)
            for s_, v_ in x2_st:
                plan.wait("sync", s_, v_)
            out_st, _ = emit_ffn(x2_buf.ap(), d_out, False)

        try:
            _emit_all()
        except _Cut:
            pass
        for nm_ in SEM_NAMES:
            plan.wait("sync", sems[nm_], sems[nm_].n)
        if os.environ.get("ATH_DBG"):
            dbgs = [("dbg_x1", x1_buf, [D, T], F32),
                    ("dbg_hc", hc_buf, [RC, 128, T], BF16),
                    ("dbg_hl", hl_buf, [RC, 128, T], BF16),
                    ("dbg_ac", ac_buf, [RC, 128, T], BF16),
                    ("dbg_x2", x2_buf, [D, T], F32),
                    ("dbg_ccB", ccB_out, [16384, 2048], BF16),
                    ("dbg_attnbf", attn_bf, [3040 * TILE], BF16),
                    ("dbg_inter", inter_buf, [H // 128, 128, T], BF16)]
            for nm_, buf_, shp_, dt_ in dbgs:
                o_ = nc.dram_tensor(nm_, shp_, dt_, kind="ExternalOutput").ap()
                sST.inc(16)
                plan.emit("sync", lambda E, a=o_, b=buf_.ap(): E.dma_start(out=a, in_=b).then_inc(sST.h, 16))
            o_ar = nc.dram_tensor("dbg_arena", [128, 48000], F32, kind="ExternalOutput").ap()
            sST.inc(16)
            plan.emit("sync", lambda E, a=o_ar: E.dma_start(out=a, in_=ar_t[:, :]).then_inc(sST.h, 16))
            plan.wait("sync", sST, sST.n)


        with nc.Block() as block:
            @block.sync
            def _(E):
                for fn in plan.q["sync"]:
                    fn(E)

            @block.gpsimd
            def _(E):
                for fn in plan.q["gpsimd"]:
                    fn(E)

            @block.vector
            def _(E):
                for fn in plan.q["vector"]:
                    fn(E)

            @block.scalar
            def _(E):
                for fn in plan.q["scalar"]:
                    fn(E)

            @block.tensor
            def _(E):
                for fn in plan.q["tensor"]:
                    fn(E)

    return nc


_CACHE = {}


def kernel(**inputs):
    x = np.asarray(inputs["L0_attn_x"], np.float32)
    w1 = np.asarray(inputs["L0_attn_w1"], np.float32)
    b1 = np.asarray(inputs["L0_attn_b1"], np.float32)
    conv_w = np.asarray(inputs["L0_attn_conv_w"], np.float32)
    conv_b = np.asarray(inputs["L0_attn_conv_b"], np.float32)
    wa = np.asarray(inputs["L0_attn_wa"], np.float32)
    ba = np.asarray(inputs["L0_attn_ba"], np.float32)
    wx = np.asarray(inputs["L0_attn_wx"], np.float32)
    bx = np.asarray(inputs["L0_attn_bx"], np.float32)
    lam = np.asarray(inputs["L0_attn_lam"], np.float32)
    w2 = np.asarray(inputs["L0_attn_w2"], np.float32)
    b2 = np.asarray(inputs["L0_attn_b2"], np.float32)
    gpk = np.ascontiguousarray(np.asarray(inputs["L0_ffn_gate_W_packed"], np.int32))
    gsc = np.ascontiguousarray(np.asarray(inputs["L0_ffn_gate_scales"], np.float32))
    upk = np.ascontiguousarray(np.asarray(inputs["L0_ffn_up_W_packed"], np.int32))
    usc = np.ascontiguousarray(np.asarray(inputs["L0_ffn_up_scales"], np.float32))
    dpk = np.ascontiguousarray(np.asarray(inputs["L0_ffn_down_W_packed"], np.int32))
    dsc = np.ascontiguousarray(np.asarray(inputs["L0_ffn_down_scales"], np.float32))

    if "nc" not in _CACHE:
        _CACHE["nc"] = build_program()
    nc = _CACHE["nc"]

    w1T = np.ascontiguousarray(w1.T)
    convT = np.ascontiguousarray(conv_w.transpose(2, 1, 0))
    waT = np.ascontiguousarray(wa.T)
    wxT = np.ascontiguousarray(wx.T)
    w2T = np.ascontiguousarray(w2.T)

    def chunkify(v, n):
        return np.ascontiguousarray(v.reshape(n, 128).T)

    vecs = np.zeros((128, 128), np.float32)
    vecs[:, 0:20] = chunkify(b1, 20)
    vecs[:, 20:40] = chunkify(conv_b, 20)
    vecs[:, 40:60] = chunkify(ba, 20)
    vecs[:, 60:80] = chunkify(bx, 20)
    vecs[:, 80:100] = chunkify(lam, 20)
    vecs[:, 100:116] = chunkify(b2, 16)
    consts = np.array([GC3, GC1, 0.5, 7.5, -4.0, -6.0, -1.0, 1.0], np.float32)
    vecs[:, 116:124] = np.tile(consts[None, :], (128, 1))

    ones = np.zeros((128, 130), BF)
    ones[:, 0] = BF(1.0)
    ones[0, 1:129] = BF(1.0)
    ident = np.eye(128, dtype=BF)

    in_maps = []
    for c in range(8):
        b = c // 4
        t0 = 2048 * (c % 4)
        msk = np.zeros((128, 32), np.float32)
        for j in range(8):
            msk[:, j] = 0.0 if j in (0, 4) else 1.0
        msk[:, 8 + c] = 1.0
        if c not in (0, 4):
            msk[:, 16 + (c - 1)] = 1.0
        if c not in (3, 7):
            msk[:, 24 + (c + 1)] = 1.0
        im = {
            "x_fm": np.ascontiguousarray(x[b, t0:t0 + T, :].T),
            "w1T": w1T, "convT": convT, "waT": waT, "wxT": wxT, "w2T": w2T,
            "vecs": vecs, "onesb": ones, "ident": ident, "msk": msk,
            "gpk": np.ascontiguousarray(gpk[c * GSL:(c + 1) * GSL]),
            "gsc": np.ascontiguousarray(gsc[c * GSL:(c + 1) * GSL]),
            "upk": np.ascontiguousarray(upk[c * GSL:(c + 1) * GSL]),
            "usc": np.ascontiguousarray(usc[c * GSL:(c + 1) * GSL]),
            "dpk": np.ascontiguousarray(dpk[c * DSL:(c + 1) * DSL]),
            "dsc": np.ascontiguousarray(dsc[c * DSL:(c + 1) * DSL]),
        }
        in_maps.append(im)

    res = run_bass_kernel_spmd(nc, in_maps, core_ids=list(range(8)))

    out = np.empty((2, 8192, 2048), np.float32)
    for c in range(8):
        b = c // 4
        t0 = 2048 * (c % 4)
        out[b, t0:t0 + T, :] = np.asarray(res.results[c]["out_fm"]).T
    return out


# version 12
# speedup vs baseline: 1.0990x; 1.0990x over previous
import contextlib
import os
import numpy as np
import ml_dtypes
import concourse.bass as bass
import concourse.mybir as mybir
from concourse.bass_utils import run_bass_kernel_spmd

F32 = mybir.dt.float32
BF16 = mybir.dt.bfloat16
I32 = mybir.dt.int32
AF = mybir.ActivationFunctionType
OP = mybir.AluOpType
BF = ml_dtypes.bfloat16

D = 2048
R = 2560
H = 16384
T = 2048
DC, RC, HC = D // 128, R // 128, H // 128
NTAU = 4
KB_G = D // 2
GSL = H // 8
DSL = D // 8
TILE = 128 * 128

GC1 = 0.7978845608028654
GC3 = 0.044715 * GC1
LN_EPS = 1e-5

ENGS = ("sync", "gpsimd", "vector", "scalar", "tensor")


class Sem:
    def __init__(self, name):
        self.name = name
        self.h = None
        self.n = 0

    def inc(self, k=1):
        self.n += k
        return self.n


class Plan:
    def __init__(self):
        self.q = {e: [] for e in ENGS}
        self.lastw = {e: {} for e in ENGS}

    def emit(self, eng, fn):
        self.q[eng].append(fn)

    def wait(self, eng, sem, val=None):
        v = sem.n if val is None else val
        if v <= 0:
            return
        if self.lastw[eng].get(sem.name, 0) >= v:
            return
        self.lastw[eng][sem.name] = v
        self.q[eng].append(lambda E, s=sem, vv=v: E.wait_ge(s.h, vv))

    def ins(self, eng, build, inc=None, inck=1):
        if inc is None:
            self.emit(eng, build)
        else:
            inc.inc(inck)
            self.emit(eng, lambda E, b=build, s=inc, k=inck: b(E).then_inc(s.h, k))


class Arena:

    def __init__(self, handle, base, cap):
        self.t = handle
        self.base = base
        self.cap = cap
        self.cur = base

    def reset(self):
        self.cur = self.base

    def alloc(self, n_f32):
        n_f32 = (n_f32 + 1) & ~1
        off = self.cur
        self.cur += n_f32
        assert self.cur <= self.cap, f"arena overflow: {self.cur} > {self.cap}"
        return off

    def f32(self, off, cols, parts=(0, 128)):
        return self.t[parts[0]:parts[1], off:off + cols]

    def bf16(self, off, cols, parts=(0, 128)):
        assert cols % 2 == 0
        return self.t[parts[0]:parts[1], off:off + cols // 2].bitcast(BF16)

    def i32(self, off, cols, parts=(0, 128)):
        return self.t[parts[0]:parts[1], off:off + cols].bitcast(I32)


def ceil_div(a, b):
    return (a + b - 1) // b



class _Cut(Exception):
    pass


def build_program():
    PH = int(os.environ.get("ATH_PH", "9"))

    def cut(n):
        if PH < n:
            raise _Cut()

    nc = bass.Bass("TRN2", target_bir_lowering=False, debug=False)

    def ein(name, shape, dt):
        return nc.dram_tensor(name, shape, dt, kind="ExternalInput").ap()

    d_x = ein("x_fm", [D, T], F32)
    d_w1T = ein("w1T", [D, R], F32)
    d_convT = ein("convT", [4, R, R], F32)
    d_waT = ein("waT", [R, R], F32)
    d_wxT = ein("wxT", [R, R], F32)
    d_w2T = ein("w2T", [R, D], F32)
    d_vecs = ein("vecs", [128, 128], F32)
    d_ones = ein("onesb", [128, 130], BF16)
    d_ident = ein("ident", [128, 128], BF16)
    d_gpk = ein("gpk", [GSL, KB_G], I32)
    d_gsc = ein("gsc", [GSL, D // 32], F32)
    d_upk = ein("upk", [GSL, KB_G], I32)
    d_usc = ein("usc", [GSL, D // 32], F32)
    d_dpk = ein("dpk", [DSL, H // 2], I32)
    d_dsc = ein("dsc", [DSL, H // 32], F32)

    d_out = nc.dram_tensor("out_fm", [D, T], F32, kind="ExternalOutput").ap()

    ccA_in = [nc.dram_tensor(f"ccA_in{q}", [1024, 2048], BF16) for q in range(4)]
    ccA_out = [nc.dram_tensor(f"ccA_out{q}", [8192, 2048], BF16) for q in range(4)]
    ccB_in = nc.dram_tensor("ccB_in", [2048, 2048], BF16)
    ccB_out = nc.dram_tensor("ccB_out", [16384, 2048], BF16)
    attn_bf = nc.dram_tensor("attn_bf", [3040 * TILE], BF16)
    x1_buf = nc.dram_tensor("x1_buf", [D, T], F32)
    x2_buf = nc.dram_tensor("x2_buf", [D, T], F32)
    inter_buf = nc.dram_tensor("inter_buf", [H // 128, 128, T], BF16)
    hc_buf = nc.dram_tensor("hc_buf", [RC, 128, T], BF16)
    hl_buf = nc.dram_tensor("hl_buf", [RC, 128, T], BF16)
    ac_buf = nc.dram_tensor("ac_buf", [RC, 128, T], BF16)
    e1_in = nc.dram_tensor("e1_in", [128, 160], BF16)
    e1_out = nc.dram_tensor("e1_out", [1024, 160], BF16)
    e2_in = nc.dram_tensor("e2_in", [128, 40], BF16)
    e2_out = nc.dram_tensor("e2_out", [1024, 40], BF16)
    dbg_gu = (nc.dram_tensor("dbg_gu", [2, 128, 2048], F32, kind="ExternalOutput")
              if os.environ.get("ATH_DBG") else None)

    W1_BASE = 0
    CV_BASE = W1_BASE + 20 * 16
    WA_BASE = CV_BASE + 20 * 80
    WX_BASE = WA_BASE + 400
    W2_BASE = WX_BASE + 400
    N_TILES = W2_BASE + 320
    assert N_TILES == 3040

    def attn_tile_ap(tile_idx, ntiles=1):
        a = attn_bf.ap()[tile_idx * TILE:(tile_idx + ntiles) * TILE]
        return a.rearrange("(t p c) -> p t c", p=128, c=128)

    es = contextlib.ExitStack()
    with es:
        ar_t = es.enter_context(nc.sbuf_tensor("arena", [128, 48000], F32))
        pA = es.enter_context(nc.psum_tensor("pA", [128, 2048], F32))
        pB = es.enter_context(nc.psum_tensor("pB", [128, 2048], F32))

        sems = {}
        SEM_NAMES = ("sLD", "sGP", "sPE", "sDV", "sSC", "sCC", "sST",
                     "sQ0", "sQ1", "sSa", "sSb", "sSc", "sSd", "sS5a", "sS5b",
                     "sR0", "sR1", "sW0", "sW1", "sI0", "sI1", "sDX", "sDS",
                     "sT0", "sT1", "sHC", "sHS", "sHLS", "sHL",
                     "sX0", "sX1", "sO0", "sO1", "sE1L", "sE1S", "sE2L", "sE2S")
        for nm in SEM_NAMES:
            sems[nm] = Sem(nm)
            sems[nm].h = es.enter_context(nc.semaphore(nm))
        (sLD, sGP, sPE, sDV, sSC, sCC, sST,
         sQ0, sQ1, sSa, sSb, sSc, sSd, sS5a, sS5b,
         sR0, sR1, sW0, sW1, sI0, sI1, sDX, sDS,
         sT0, sT1, sHC, sHS, sHLS, sHL,
         sX0, sX1, sO0, sO1, sE1L, sE1S, sE2L, sE2S) = (sems[n] for n in SEM_NAMES)
        sQ = [sQ0, sQ1]
        sR = [sR0, sR1]
        sW = [sW0, sW1]
        sI = [sI0, sI1]
        sT = [sT0, sT1]
        sX = [sX0, sX1]
        sO = [sO0, sO1]
        sS5 = [sS5a, sS5b]

        V_B1, V_CB, V_BA, V_BX, V_LAM, V_B2 = 0, 20, 40, 60, 80, 100

        d_msk = ein("msk", [128, 32], F32)

        CONST_F32 = 0
        ar = Arena(ar_t, 0, 48000)
        A_VEC = ar.alloc(128)
        A_MSK = ar.alloc(32)
        A_ONE = ar.alloc(66)
        A_ID = ar.alloc(64)
        A_ZER = ar.alloc(512)
        A_SP = ar.alloc(20)
        A0 = ar.cur

        def vec_col(c0, n=1):
            return ar.f32(A_VEC + c0, n)

        ones_col = ar.bf16(A_ONE, 2)[:, 0:1]
        ones_row = ar.bf16(A_ONE, 130)[0:1, 1:129]
        ident = ar.bf16(A_ID, 128)
        zeros512 = ar.f32(A_ZER, 512)

        CONSTS = [GC3, GC1, 0.5, 7.5, -4.0, -6.0, -1.0, 1.0]
        for i, v in enumerate(CONSTS):
            nc.const_aps.register(F32, v, ar_t[:, A_VEC + 116 + i:A_VEC + 117 + i])

        plan = Plan()

        def dma(qeng, out_ap, in_ap, sem, deps=()):
            for (s, v) in deps:
                plan.wait(qeng, s, v)
            plan.ins(qeng, lambda E, o=out_ap, i=in_ap: E.dma_start(out=o, in_=i),
                     inc=sem, inck=16)
            return sem.n

        def comp(eng, build, sem=None, deps=()):
            for (s, v) in deps:
                plan.wait(eng, s, v)
            plan.ins(eng, build, inc=sem)
            return None if sem is None else sem.n

        def _emit_all():
            n0 = dma("sync", ar.f32(A_VEC, 128), d_vecs, sLD)
            dma("sync", ar.f32(A_MSK, 32), d_msk, sLD)
            dma("sync", ar.bf16(A_ONE, 130), d_ones, sLD)
            dma("sync", ar.bf16(A_ID, 128), d_ident, sLD)
            ld_consts = sLD.n
            comp("vector", lambda E: E.memset(zeros512, 0.0), sem=sDV)

            SP_T1 = A0
            lam = ar.f32(A_VEC + V_LAM, 20)
            t1 = ar.f32(SP_T1, 20)
            t2 = ar.f32(SP_T1 + 20, 20)
            comp("scalar", lambda E: E.activation(t1, lam, AF.Abs), deps=[(sLD, ld_consts)])
            comp("scalar", lambda E: E.activation(t1, t1, AF.Exp, scale=-1.0))
            comp("scalar", lambda E: E.activation(t1, t1, AF.Ln, bias=1.0), sem=sSC)
            comp("vector", lambda E: E.tensor_scalar_max(t2, lam, 0.0),
                 deps=[(sLD, ld_consts)])
            comp("vector", lambda E: E.tensor_add(ar.f32(A_SP, 20), t2, t1),
                 sem=sDV, deps=[(sSC, sSC.n)])
            sp_ready = (sDV, sDV.n)

            cut(1)
            ar.cur = A0 + 64
            DQ = []
            for par in range(2):
                d = {}
                d["pk"] = ar.alloc(1024)
                d["lo"] = ar.alloc(1024)
                d["hi"] = ar.alloc(1024)
                d["c"] = ar.alloc(2048)
                d["s"] = ar.alloc(2048)
                d["sc"] = ar.alloc(64)
                d["t"] = ar.alloc(2048)
                d["r4"] = ar.alloc(2048)
                d["r6"] = ar.alloc(2048)
                d["m"] = ar.alloc(2048)
                d["dq"] = ar.alloc(1024)
                d["st"] = ar.alloc(1024)
                DQ.append(d)
            assert ar.cur < 48000

            jobs = []
            for q in range(4):
                for i in range(4 * q, 4 * q + 4):
                    jobs.append(("g", d_gpk, d_gsc, i * 128, 0))
                    jobs.append(("u", d_upk, d_usc, i * 128, 0))
            for rr in range(2):
                for j in range(8):
                    jobs.append(("d", d_dpk, d_dsc, rr * 128, j * 1024))

            job_ld = []
            job_done = []
            ccA_done = []

            for ji, (kind, dpk_, dsc_, row0, kb0) in enumerate(jobs):
                d = DQ[ji % 2]
                deps = []
                if ji >= 2:
                    deps = [job_done[ji - 2]]
                qsem = sQ[ji % 2]
                v = dma("sync", ar.i32(d["pk"], 1024), dpk_[row0:row0 + 128, kb0:kb0 + 1024],
                        qsem, deps=deps)
                v = dma("sync", ar.f32(d["sc"], 64),
                        dsc_[row0:row0 + 128, kb0 // 16:kb0 // 16 + 64], qsem)
                job_ld.append((qsem, v))

                pk = ar.i32(d["pk"], 1024)
                lo = ar.i32(d["lo"], 1024)
                hi = ar.i32(d["hi"], 1024)
                cc = ar.f32(d["c"], 2048)
                ss = ar.f32(d["s"], 2048)
                tt_ = ar.f32(d["t"], 2048)
                r4 = ar.f32(d["r4"], 2048)
                r6 = ar.f32(d["r6"], 2048)
                mm_ = ar.f32(d["m"], 2048)
                dq = ar.bf16(d["dq"], 2048)
                stg = ar.bf16(d["st"], 2048)

                ldv = job_ld[ji]
                comp("vector", lambda E, a=lo, b=pk: E.tensor_single_scalar(a, b, 15, OP.bitwise_and), deps=[ldv])
                comp("vector", lambda E, a=hi, b=pk: E.tensor_single_scalar(a, b, 4, OP.logical_shift_right), sem=sDV)
                c3 = cc.rearrange("p (k two) -> p k two", two=2)
                plan.wait("gpsimd", sDV, sDV.n)
                comp("gpsimd", lambda E, a=c3[:, :, 0], b=lo: E.tensor_copy(a, b))
                comp("gpsimd", lambda E, a=c3[:, :, 1], b=hi: E.tensor_copy(a, b), sem=sGP)
                v_c = sGP.n
                comp("scalar", lambda E, a=ss, b=cc: E.activation(a, b, AF.Sign, bias=7.5, scale=-1.0),
                     sem=sSC, deps=[(sGP, v_c)])
                v_sgn = sSC.n
                comp("vector", lambda E, a=tt_, b=ss, c_=cc: E.scalar_tensor_tensor(a, b, 4.0, c_, OP.mult, OP.add),
                     deps=[(sSC, v_sgn), (sGP, v_c)])
                comp("vector", lambda E, a=tt_: E.tensor_scalar(a, a, -4.0, None, OP.add), sem=sDV)
                v_t = sDV.n
                comp("scalar", lambda E, a=r4, b=tt_: E.activation(a, b, AF.Relu, bias=-4.0),
                     deps=[(sDV, v_t)])
                comp("scalar", lambda E, a=r6, b=tt_: E.activation(a, b, AF.Relu, bias=-6.0), sem=sSC)
                v_r = sSC.n
                comp("vector", lambda E, a=r4, b=tt_: E.tensor_add(a, a, b), deps=[(sSC, v_r)])
                comp("vector", lambda E, a=mm_, b=r4, c_=r6: E.scalar_tensor_tensor(a, b, 0.5, c_, OP.mult, OP.add))
                comp("vector", lambda E, a=mm_, b=ss: E.tensor_mul(a, a, b))
                scb = ar.f32(d["sc"], 64).unsqueeze(2).broadcast_to([128, 64, 32])
                m3 = mm_.rearrange("p (blk j) -> p blk j", j=32)
                d3 = dq.rearrange("p (blk j) -> p blk j", j=32)
                comp("vector", lambda E, a=d3, b=m3, c_=scb: E.tensor_tensor(a, b, c_, OP.mult), sem=sDV)
                v_dq = sDV.n

                for kc in range(16):
                    bank = kc % 4
                    pt = pA[:, bank * 512:bank * 512 + 64].bitcast(BF16)[:, 0:128]
                    depsT = [(sDV, v_dq)]
                    if kc >= 4:
                        depsT.append((sSC, sSC.n))
                    comp("tensor", lambda E, o=pt, i=dq[:, kc * 128:(kc + 1) * 128]:
                         E.transpose(o, i, ident), sem=sPE, deps=depsT)
                    v_tr = sPE.n
                    comp("scalar", lambda E, o=stg[:, kc * 128:(kc + 1) * 128], i=pt:
                         E.activation(o, i, AF.Copy), sem=sSC, deps=[(sPE, v_tr)])
                v_stg = sSC.n

                if kind in ("g", "u"):
                    i_ch = row0 // 128
                    q = i_ch // 4
                    j = i_ch % 4
                    off = ((0 if kind == "g" else 4) + j) * 16 * TILE
                    dst = ccA_in[q].ap().rearrange("a b -> (a b)")[off:off + 16 * TILE]
                    dst = dst.rearrange("(t p c) -> p t c", p=128, c=128)
                else:
                    kc0 = kb0 // 64
                    dh = row0 // 128
                    base = ccB_in.ap().rearrange("a b -> (a b)")
                    dst = base[(kc0 * 2 + dh) * TILE:(kc0 * 2 + dh) * TILE + 15 * 2 * TILE + TILE]
                    dst = dst.rearrange("(t x) -> t x", x=TILE)[::2]
                    dst = dst.rearrange("t (p c) -> p t c", p=128, c=128)
                src = stg.rearrange("p (t c) -> p t c", c=128)
                if ji < 32:
                    ssem = [sSa, sSb][ji % 2]
                else:
                    ssem = [sSc, sSd][ji % 2]
                v = dma("gpsimd", dst, src, ssem, deps=[(sSC, v_stg)])
                job_done.append((ssem, v))
                if ji < 32 and ji % 8 == 7:
                    qq = ji // 8
                    plan.wait("gpsimd", job_done[ji][0], job_done[ji][1])
                    plan.wait("gpsimd", job_done[ji - 1][0], job_done[ji - 1][1])
                    plan.ins("gpsimd", lambda E, q_=qq: E.collective_compute(
                        "AllGather", OP.bypass, replica_groups=[list(range(8))],
                        ins=[ccA_in[q_].ap().opt()], outs=[ccA_out[q_].ap().opt()]),
                        inc=sCC)
                    ccA_done.append(sCC.n)

            plan.wait("gpsimd", sSc, sSc.n)
            plan.wait("gpsimd", sSd, sSd.n)
            plan.ins("gpsimd", lambda E: E.collective_compute(
                "AllGather", OP.bypass, replica_groups=[list(range(8))],
                ins=[ccB_in.ap().opt()], outs=[ccB_out.ap().opt()]), inc=sCC)
            ccB_done = sCC.n

            cut(2)
            P05 = {"ji": 0}
            CW_BASE = 48000 - 2 * 3842
            CW = []
            for par in range(2):
                b = CW_BASE + par * 3842
                CW.append({"src": b, "dst": b + 2560})

            cast_jobs = []
            for kc in range(16):
                cast_jobs.append((d_w1T[kc * 128:(kc + 1) * 128, :], 20,
                                  [W1_BASE + r * 16 + kc for r in range(20)]))
            for tap in range(4):
                for ic in range(20):
                    cast_jobs.append((d_convT[tap, ic * 128:(ic + 1) * 128, :], 20,
                                      [CV_BASE + (o * 4 + tap) * 20 + ic for o in range(20)]))
            for dc in range(20):
                cast_jobs.append((d_waT[dc * 128:(dc + 1) * 128, :], 20,
                                  [WA_BASE + e * 20 + dc for e in range(20)]))
            for dc in range(20):
                cast_jobs.append((d_wxT[dc * 128:(dc + 1) * 128, :], 20,
                                  [WX_BASE + e * 20 + dc for e in range(20)]))
            for rc_ in range(20):
                cast_jobs.append((d_w2T[rc_ * 128:(rc_ + 1) * 128, :], 16,
                                  [W2_BASE + dd * 20 + rc_ for dd in range(16)]))
            cj_done = []

            def emit_p05_job():
                ji = P05["ji"]
                if ji >= len(cast_jobs):
                    return False
                P05["ji"] = ji + 1
                src_ap, ncol, tidx = cast_jobs[ji]
                d = CW[ji % 2]
                deps = [cj_done[ji - 2]] if ji >= 2 else []
                v = dma("sync", ar.f32(d["src"], ncol * 128), src_ap, sQ[ji % 2], deps=deps)
                comp("vector", lambda E, o=ar.bf16(d["dst"], ncol * 128),
                     i=ar.f32(d["src"], ncol * 128):
                     E.tensor_copy(o, i), sem=sDV, deps=[(sQ[ji % 2], v)])
                v_c = sDV.n
                stride = (tidx[1] - tidx[0]) * TILE if ncol > 1 else TILE
                base = attn_bf.ap()[tidx[0] * TILE: tidx[0] * TILE + (ncol - 1) * stride + TILE]
                dst = base.rearrange("(t x) -> t x", x=TILE)[::stride // TILE]
                dst = dst.rearrange("t (p c) -> p t c", p=128, c=128)
                srcv = ar.bf16(d["dst"], ncol * 128).rearrange("p (t c) -> p t c", c=128)
                v = dma("gpsimd", dst, srcv, sS5[ji % 2], deps=[(sDV, v_c)])
                cj_done.append((sS5[ji % 2], v))
                return True

            def emit_p05_rest():
                while emit_p05_job():
                    pass
                P05['done'] = [(sS5a, sS5a.n), (sS5b, sS5b.n)]


            def emit_layernorm(src_dram, A_LN_, sc, extra_deps=(), chunk_deps=None):
                plan.wait("tensor", sDV, sDV.n)
                plan.wait("tensor", sSC, sSC.n)
                roll = [sc["roll"] + i * 2048 for i in range(2)]
                sqb = [sc["sqb"] + i * 1024 for i in range(2)]
                xb = [sc["xb"] + i * 1024 for i in range(2)]
                r_mean = ar.f32(sc["rows"], 2048, (0, 1))
                r_var = ar.f32(sc["rows"] + 2048, 2048, (0, 1))
                r_t = ar.f32(sc["rows"] + 4096, 2048, (0, 1))
                hv = sc["rows"] + 6144
                rstd_hi = ar.bf16(hv, 2048, (0, 1))
                rstd_lo = ar.bf16(hv + 1024, 2048, (0, 1))
                mr_hi = ar.bf16(hv + 2048, 2048, (0, 1))
                mr_lo = ar.bf16(hv + 3072, 2048, (0, 1))

                pe_marks = []
                for c in range(16):
                    d = c % 2
                    deps = list(extra_deps) if c == 0 else []
                    if chunk_deps is not None:
                        deps.append(chunk_deps(c))
                    if c >= 2:
                        deps.append((sPE, pe_marks[c - 2]))
                    v = dma("sync", ar.f32(roll[d], 2048), src_dram[c * 128:(c + 1) * 128, :],
                            sR[d], deps=deps)
                    comp("scalar", lambda E, o=ar.bf16(sqb[d], 2048), i=ar.f32(roll[d], 2048):
                         E.activation(o, i, AF.Square), deps=[(sR[d], v)])
                    comp("scalar", lambda E, o=ar.bf16(xb[d], 2048), i=ar.f32(roll[d], 2048):
                         E.activation(o, i, AF.Copy), sem=sSC)
                    v_s = sSC.n
                    plan.wait("tensor", sSC, v_s)
                    for tau in range(4):
                        comp("tensor", lambda E, o=pA[0:1, tau * 512:(tau + 1) * 512],
                             i=ar.bf16(xb[d], 2048)[:, tau * 512:(tau + 1) * 512],
                             st=(c == 0), sp=(c == 15):
                             E.matmul(o, ones_col, i, start=st, stop=sp))
                    for tau in range(4):
                        last = (tau == 3)
                        comp("tensor", lambda E, o=pB[0:1, tau * 512:(tau + 1) * 512],
                             i=ar.bf16(sqb[d], 2048)[:, tau * 512:(tau + 1) * 512],
                             st=(c == 0), sp=(c == 15):
                             E.matmul(o, ones_col, i, start=st, stop=sp),
                             sem=(sPE if last else None))
                    pe_marks.append(sPE.n)
                stats_done = sPE.n

                comp("vector", lambda E: E.tensor_scalar(r_mean, pA[0:1, :], 1.0 / D, None, OP.mult),
                     deps=[(sPE, stats_done)])
                comp("vector", lambda E: E.tensor_scalar(r_var, pB[0:1, :], 1.0 / D, None, OP.mult))
                comp("vector", lambda E: E.scalar_tensor_tensor(r_t, r_mean, -1.0, r_mean, OP.mult, OP.mult))
                comp("vector", lambda E: E.tensor_add(r_var, r_var, r_t))
                comp("vector", lambda E: E.tensor_scalar(r_var, r_var, LN_EPS, None, OP.add), sem=sDV)
                v_rows = sDV.n
                comp("scalar", lambda E: E.activation(r_t, r_var, AF.Ln), sem=sSC, deps=[(sDV, v_rows)])
                comp("vector", lambda E: E.tensor_scalar(r_t, r_t, -0.5, None, OP.mult), sem=sDV,
                     deps=[(sSC, sSC.n)])
                comp("scalar", lambda E: E.activation(r_var, r_t, AF.Exp), deps=[(sDV, sDV.n)])
                comp("scalar", lambda E: E.activation(rstd_hi, r_var, AF.Copy), sem=sSC)
                v_hi1 = sSC.n
                comp("vector", lambda E: E.tensor_mul(r_t, r_mean, r_var), deps=[(sSC, v_hi1)])
                comp("vector", lambda E: E.tensor_tensor(r_mean, r_var, rstd_hi, OP.subtract))
                comp("vector", lambda E: E.tensor_copy(rstd_lo, r_mean), sem=sDV)
                comp("scalar", lambda E: E.activation(mr_hi, r_t, AF.Copy), sem=sSC,
                     deps=[(sDV, sDV.n)])
                comp("vector", lambda E: E.tensor_tensor(r_mean, r_t, mr_hi, OP.subtract),
                     deps=[(sSC, sSC.n)])
                comp("vector", lambda E: E.tensor_copy(mr_lo, r_mean), sem=sDV)
                v_lo = sDV.n

                plan.wait("tensor", sDV, v_lo)
                for tau in range(4):
                    sl = slice(tau * 512, (tau + 1) * 512)
                    comp("tensor", lambda E, o=pA[:, sl], i=rstd_hi[:, sl]:
                         E.matmul(o, ones_row, i, start=True, stop=False))
                    comp("tensor", lambda E, o=pA[:, sl], i=rstd_lo[:, sl]:
                         E.matmul(o, ones_row, i, start=False, stop=True))
                    comp("tensor", lambda E, o=pB[:, sl], i=mr_hi[:, sl]:
                         E.matmul(o, ones_row, i, start=True, stop=False))
                    last = (tau == 3)
                    comp("tensor", lambda E, o=pB[:, sl], i=mr_lo[:, sl]:
                         E.matmul(o, ones_row, i, start=False, stop=True),
                         sem=(sPE if last else None))
                bcast_done = sPE.n

                stats_sc = sSC.n
                ap_marks = []
                for c in range(16):
                    d = c % 2
                    deps = [(sSC, stats_sc)] if c < 2 else [(sDV, ap_marks[c - 2])]
                    v = dma("sync", ar.f32(roll[d], 2048), src_dram[c * 128:(c + 1) * 128, :],
                            sR[d], deps=deps)
                    comp("vector", lambda E, o=ar.f32(sc["t1"], 2048), i=ar.f32(roll[d], 2048):
                         E.tensor_tensor(o, i, pA[:, :], OP.mult),
                         deps=[(sR[d], v), (sPE, bcast_done)])
                    comp("vector", lambda E, o=ar.bf16(A_LN_ + c * 1024, 2048), i=ar.f32(sc["t1"], 2048):
                         E.tensor_tensor(o, i, pB[:, :], OP.subtract), sem=sDV)
                    ap_marks.append(sDV.n)
                return sDV.n

            def emit_ffn(src_dram, dst_dram, first_call, ln_extra=(), mid=None, per_m=None):
                ar.cur = A0
                A_LN_ = ar.alloc(16384)
                sc = {"roll": ar.alloc(4096), "sqb": ar.alloc(2048), "xb": ar.alloc(2048),
                      "rows": ar.alloc(10240)}
                sc["t1"] = sc["rows"]
                ln_done = emit_layernorm(src_dram, A_LN_, sc, extra_deps=ln_extra)

                ar.cur = A0 + 16384
                G_GS = [ar.alloc(2048), ar.alloc(2048)]
                G_US = [ar.alloc(2048), ar.alloc(2048)]
                G_T1 = ar.alloc(2048)
                G_T2 = ar.alloc(2048)
                G_T3 = ar.alloc(2048)
                G_IS = [ar.alloc(1024), ar.alloc(1024)]
                G_WG = [ar.alloc(1024), ar.alloc(1024)]
                G_WU = [ar.alloc(1024), ar.alloc(1024)]

                def gup_src(m, which):
                    r_, loc = m // 16, m % 16
                    q, j = loc // 4, loc % 4
                    off = (r_ * 128 + ((0 if which == "g" else 4) + j) * 16) * TILE
                    a = ccA_out[q].ap().rearrange("a b -> (a b)")[off:off + 16 * TILE]
                    return a.rearrange("(t p c) -> p t c", p=128, c=128)

                def m_order(idx):
                    q, rest = idx // 32, idx % 32
                    r_, j = rest // 4, rest % 4
                    return r_ * 16 + 4 * q + j

                wld = []
                ev_marks = []
                ep_marks = []
                ep_v2 = []
                ep_v4 = []
                first_mm_dep = [(sDV, ln_done)]
                for mi in range(HC):
                    if per_m is not None:
                        per_m(mi)
                    m = m_order(mi)
                    d = mi % 2
                    deps = [(sCC, ccA_done[(m % 16) // 4])]
                    if mi < 2:
                        deps.append((sDV, ln_done))
                    else:
                        deps.append((sPE, ev_marks[mi - 2][2]))
                    v = dma("sync", ar.bf16(G_WG[d], 2048).rearrange("p (t c) -> p t c", c=128),
                            gup_src(m, "g"), sW[d], deps=deps)
                    v = dma("sync", ar.bf16(G_WU[d], 2048).rearrange("p (t c) -> p t c", c=128),
                            gup_src(m, "u"), sW[d])
                    wld.append((sW[d], v))

                    plan.wait("tensor", wld[mi][0], wld[mi][1])
                    if mi == 0:
                        for s_, vv in first_mm_dep:
                            plan.wait("tensor", s_, vv)
                    half_pe = []
                    for h2 in range(2):
                        if mi >= 1:
                            plan.wait("tensor", sSC, ev_marks[mi - 1][h2])
                        for k in range(16):
                            wg = ar.bf16(G_WG[d], 2048)[:, k * 128:(k + 1) * 128]
                            wu = ar.bf16(G_WU[d], 2048)[:, k * 128:(k + 1) * 128]
                            for tau in (2 * h2, 2 * h2 + 1):
                                ln_sl = ar.bf16(A_LN_ + k * 1024, 2048)[:, tau * 512:(tau + 1) * 512]
                                comp("tensor", lambda E, o=pA[:, tau * 512:(tau + 1) * 512], w=wg, i=ln_sl,
                                     st=(k == 0), sp=(k == 15):
                                     E.matmul(o, w, i, start=st, stop=sp))
                            for tau in (2 * h2, 2 * h2 + 1):
                                ln_sl = ar.bf16(A_LN_ + k * 1024, 2048)[:, tau * 512:(tau + 1) * 512]
                                last = (k == 15 and tau == 2 * h2 + 1)
                                comp("tensor", lambda E, o=pB[:, tau * 512:(tau + 1) * 512], w=wu, i=ln_sl,
                                     st=(k == 0), sp=(k == 15):
                                     E.matmul(o, w, i, start=st, stop=sp),
                                     sem=(sPE if last else None))
                        half_pe.append(sPE.n)
                    ev = []
                    for h2 in range(2):
                        sl = slice(h2 * 1024, (h2 + 1) * 1024)
                        evdeps = [(sPE, half_pe[h2])]
                        if h2 == 0 and mi >= 2:
                            evdeps.append((sDV, ep_v4[mi - 2]))
                        comp("scalar", lambda E, o=ar.f32(G_GS[d], 2048)[:, sl], i=pA[:, sl]:
                             E.activation(o, i, AF.Copy), deps=evdeps)
                        comp("scalar", lambda E, o=ar.f32(G_US[d], 2048)[:, sl], i=pB[:, sl]:
                             E.activation(o, i, AF.Copy), sem=sSC)
                        ev.append(sSC.n)
                    ev_marks.append((ev[0], ev[1], half_pe[1]))
                    if dbg_gu is not None and m == 5 and first_call:
                        dma("gpsimd", dbg_gu.ap()[0], ar.f32(G_GS[d], 2048), sGP,
                            deps=[(sSC, ev[1])])
                        dma("gpsimd", dbg_gu.ap()[1], ar.f32(G_US[d], 2048), sGP)

                    g_s = ar.f32(G_GS[d], 2048)
                    u_s = ar.f32(G_US[d], 2048)
                    sq_deps = [(sSC, ev[1])]
                    if mi >= 1:
                        sq_deps.append((sDV, ep_v2[mi - 1]))
                    comp("scalar", lambda E, o=ar.f32(G_T1, 2048), i=g_s:
                         E.activation(o, i, AF.Square), deps=sq_deps)
                    comp("scalar", lambda E, o=ar.f32(G_T1, 2048), i=ar.f32(G_T1, 2048):
                         E.activation(o, i, AF.Copy, bias=GC1, scale=GC3), sem=sSC)
                    v1 = sSC.n
                    if mi >= 2:
                        plan.wait("vector", ep_marks[mi - 2][0], ep_marks[mi - 2][1])
                    comp("vector", lambda E, o=ar.f32(G_T2, 2048), i=ar.f32(G_T1, 2048), g=g_s:
                         E.tensor_tensor(o, i, g, OP.mult), deps=[(sSC, v1)])
                    comp("vector", lambda E, o=ar.f32(G_T3, 2048), g=g_s, u=u_s:
                         E.tensor_tensor(o, g, u, OP.mult), sem=sDV)
                    v2 = sDV.n
                    ep_v2.append(v2)
                    comp("scalar", lambda E, o=ar.f32(G_T2, 2048), i=ar.f32(G_T2, 2048):
                         E.activation(o, i, AF.Tanh), deps=[(sDV, v2)])
                    comp("scalar", lambda E, o=ar.f32(G_T2, 2048), i=ar.f32(G_T2, 2048):
                         E.activation(o, i, AF.Copy, bias=0.5, scale=0.5), sem=sSC)
                    v3 = sSC.n
                    comp("vector", lambda E, o=ar.bf16(G_IS[d], 2048), a=ar.f32(G_T3, 2048),
                         b=ar.f32(G_T2, 2048): E.tensor_tensor(o, a, b, OP.mult),
                         sem=sDV, deps=[(sSC, v3)])
                    v4 = sDV.n
                    ep_v4.append(v4)
                    v = dma("gpsimd", inter_buf.ap()[m], ar.bf16(G_IS[d], 2048), sT[d],
                            deps=[(sDV, v4)])
                    ep_marks.append((sT[d], v))

                inter_all = [(sT0, sT0.n), (sT1, sT1.n)]
                if mid is not None:
                    mid()

                ar.cur = A0 + 16384
                DW_W = [ar.alloc(2048), ar.alloc(2048)]
                DW_I = [ar.alloc(4096), ar.alloc(4096)]
                DW_XB = [ar.alloc(4096), ar.alloc(4096)]

                dn_kg_pe = []
                pass_store = []
                ev_dv = 0
                for dg in range(4):
                    for th in range(2):
                        pidx = dg * 2 + th
                        DW_X = DW_XB[pidx % 2]
                        if pidx < 2:
                            xdeps = list(inter_all)
                        else:
                            xdeps = [pass_store[pidx - 2]]
                        for di in range(4):
                            dglob = dg * 4 + di
                            dma("sync", ar.f32(DW_X + di * 1024, 1024),
                                src_dram[dglob * 128:(dglob + 1) * 128, th * 1024:(th + 1) * 1024],
                                sDX, deps=xdeps if di == 0 else [])
                        x_ld = sDX.n
                        for kg in range(16):
                            d = kg % 2
                            deps = [(sCC, ccB_done)] + inter_all
                            if len(dn_kg_pe) >= 2:
                                deps.append((sPE, dn_kg_pe[-2]))
                            for si in range(2):
                                s_ = 2 * dg + si
                                off = (s_ * 256 + kg * 8 * 2) * TILE
                                a = ccB_out.ap().rearrange("a b -> (a b)")[off:off + 16 * TILE]
                                a = a.rearrange("(t p c) -> p t c", p=128, c=128)
                                dma("sync", ar.bf16(DW_W[d] + si * 1024, 2048)
                                    .rearrange("p (t c) -> p t c", c=128), a, sI[d],
                                    deps=deps if si == 0 else [])
                            iap = inter_buf.ap()[kg * 8:(kg + 1) * 8, :, th * 1024:(th + 1) * 1024]
                            v_ld = dma("sync", ar.bf16(DW_I[d], 8192).rearrange("p (t c) -> p t c", c=1024),
                                       iap.rearrange("t p c -> p t c"), sI[d])
                            plan.wait("tensor", sI[d], v_ld)
                            if kg == 0 and pidx > 0:
                                plan.wait("tensor", sDV, ev_dv)
                            for kc in range(8):
                                kcg = kg * 8 + kc
                                for di in range(4):
                                    si, dh = di // 2, di % 2
                                    wt = ar.bf16(DW_W[d] + si * 1024, 2048)[:, (kc * 2 + dh) * 128:(kc * 2 + dh + 1) * 128]
                                    pT = pA if di < 2 else pB
                                    bank0 = (di % 2) * 1024
                                    for j in range(2):
                                        it = ar.bf16(DW_I[d], 8192)[:, kc * 1024 + j * 512: kc * 1024 + (j + 1) * 512]
                                        last = (kc == 7 and di == 3 and j == 1)
                                        comp("tensor", lambda E, o=pT[:, bank0 + j * 512: bank0 + (j + 1) * 512],
                                             w=wt, i=it, st=(kcg == 0), sp=(kcg == 127):
                                             E.matmul(o, w, i, start=st, stop=sp),
                                             sem=(sPE if last else None))
                            dn_kg_pe.append(sPE.n)
                        dn_pe = sPE.n
                        for di in range(4):
                            dglob = dg * 4 + di
                            pT = pA if di < 2 else pB
                            bank0 = (di % 2) * 1024
                            comp("vector", lambda E, o=ar.f32(DW_X + di * 1024, 1024),
                                 p=pT[:, bank0:bank0 + 1024], x=ar.f32(DW_X + di * 1024, 1024):
                                 E.scalar_tensor_tensor(o, p, 0.5, x, OP.mult, OP.add),
                                 sem=sDV, deps=[(sPE, dn_pe), (sDX, x_ld)])
                            v_ev = sDV.n
                            dma("gpsimd", dst_dram[dglob * 128:(dglob + 1) * 128, th * 1024:(th + 1) * 1024],
                                ar.f32(DW_X + di * 1024, 1024), sDS, deps=[(sDV, v_ev)])
                        ev_dv = sDV.n
                        pass_store.append((sDS, sDS.n))
                return (sDS, sDS.n), pass_store

            cut(3)
            def p05_per_m(mi):
                want = (mi + 1) * len(cast_jobs) // HC
                while P05["ji"] < want:
                    if not emit_p05_job():
                        break

            ffn1_st, ffn1_passes = emit_ffn(d_x, x1_buf.ap(), True,
                                            ln_extra=[job_done[-1], job_done[-2]],
                                            mid=emit_p05_rest, per_m=p05_per_m)
            cut(4)

            ar.cur = A0
            A_LN2 = ar.alloc(16384)
            A_H = ar.alloc(20544)
            W1B = [ar.alloc(1024), ar.alloc(1024)]
            sc2 = {"roll": A_H, "sqb": A_H + 4096, "xb": A_H + 6144,
                   "rows": A_H + 8192, "t1": A_H + 18432}
            plan.wait("sync", ffn1_st[0], ffn1_st[1])
            ln2_done = emit_layernorm(x1_buf.ap(), A_LN2, sc2)

            def h_view(c, col0, ncols, dtype_bf=True):
                off = A_H
                a = ar.bf16(off, 20544 * 2)
                return a[:, c * 2054 + col0: c * 2054 + col0 + ncols]

            def h_view3(col0, ncols):
                a = ar.bf16(A_H, 20544 * 2)[:, 0:20 * 2054]
                return a.rearrange("p (c t) -> p c t", t=2054)[:, :, col0:col0 + ncols]

            w1_pe = []
            w1_ev = []
            for r_ in range(20):
                d = r_ % 2
                deps = P05['done'] + [(sDV, ln2_done)]
                if r_ >= 2:
                    deps.append((sPE, w1_pe[r_ - 2]))
                v = dma("sync", ar.bf16(W1B[d], 2048).rearrange("p (t c) -> p t c", c=128),
                        attn_tile_ap(W1_BASE + r_ * 16, 16), sW[d], deps=deps)
                plan.wait("tensor", sW[d], v)
                if r_ == 0:
                    plan.wait("tensor", sDV, ln2_done)
                pT = pA if r_ % 2 == 0 else pB
                if r_ >= 2:
                    plan.wait("tensor", sSC, w1_ev[r_ - 2])
                for k in range(16):
                    w = ar.bf16(W1B[d], 2048)[:, k * 128:(k + 1) * 128]
                    for tau in range(4):
                        ln_sl = ar.bf16(A_LN2 + k * 1024, 2048)[:, tau * 512:(tau + 1) * 512]
                        last = (k == 15 and tau == 3)
                        comp("tensor", lambda E, o=pT[:, tau * 512:(tau + 1) * 512], w=w, i=ln_sl,
                             st=(k == 0), sp=(k == 15):
                             E.matmul(o, w, i, start=st, stop=sp),
                             sem=(sPE if last else None))
                w1_pe.append(sPE.n)
                for tau in range(4):
                    last = (tau == 3)
                    comp("scalar", lambda E, o=h_view(r_, 3 + tau * 512, 512),
                         i=pT[:, tau * 512:(tau + 1) * 512], b=vec_col(V_B1 + r_):
                         E.activation(o, i, AF.Silu, bias=b), sem=(sSC if last else None),
                         deps=[(sPE, w1_pe[r_])] if tau == 0 else [])
                w1_ev.append(sSC.n)
            w1_done = sSC.n

            E1S = ar.alloc(80)
            e1p = ar.bf16(E1S, 160)
            e1p3 = e1p.rearrange("p (c k) -> p c k", k=8)
            comp("vector", lambda E: E.tensor_copy(e1p3[:, :, 0:3], h_view3(3, 3)),
                 deps=[(sSC, w1_done)])
            comp("vector", lambda E: E.tensor_copy(e1p3[:, :, 3:6], h_view3(2048, 3)), sem=sDV)
            v = dma("gpsimd", e1_in.ap(), e1p, sE1S, deps=[(sDV, sDV.n)])
            plan.wait("gpsimd", sE1S, v)
            plan.ins("gpsimd", lambda E: E.collective_compute(
                "AllGather", OP.bypass, replica_groups=[list(range(8))],
                ins=[e1_in.ap().opt()], outs=[e1_out.ap().opt()]), inc=sCC)
            e1_done = sCC.n
            E1G = ar.alloc(8 * 80)
            v = None
            for j in range(8):
                v = dma("sync", ar.bf16(E1G + j * 80, 160), e1_out.ap()[j * 128:(j + 1) * 128, :],
                        sE1L, deps=[(sCC, e1_done)] if j == 0 else [])
            e1g_ld = sE1L.n
            ACCL = ar.alloc(120)
            accl = ar.f32(ACCL, 60).rearrange("p (c k) -> p c k", k=3)
            accr = ar.f32(ACCL + 60, 60).rearrange("p (c k) -> p c k", k=3)
            for j in range(8):
                g = ar.bf16(E1G + j * 80, 160).rearrange("p (c k) -> p c k", k=8)
                lsel = ar.f32(A_MSK + 16 + j, 1)
                rsel = ar.f32(A_MSK + 24 + j, 1)
                if j == 0:
                    comp("vector", lambda E, a=accl, b=g[:, :, 3:6], s=lsel:
                         E.tensor_scalar(a, b, s, None, OP.mult), deps=[(sE1L, e1g_ld)])
                    comp("vector", lambda E, a=accr, b=g[:, :, 0:3], s=rsel:
                         E.tensor_scalar(a, b, s, None, OP.mult))
                else:
                    comp("vector", lambda E, a=accl, b=g[:, :, 3:6], s=lsel:
                         E.scalar_tensor_tensor(a, b, s, a, OP.mult, OP.add))
                    comp("vector", lambda E, a=accr, b=g[:, :, 0:3], s=rsel:
                         E.scalar_tensor_tensor(a, b, s, a, OP.mult, OP.add))
            comp("vector", lambda E: E.tensor_copy(h_view3(0, 3), accl))
            comp("vector", lambda E: E.tensor_copy(h_view3(2051, 3), accr), sem=sDV)
            halo_done = sDV.n

            cut(5)
            TAIL = A_H + 20544
            assert TAIL + 10240 <= 48000, TAIL
            CVB = [A_LN2, TAIL]
            HCS = [A_LN2 + 10240, A_LN2 + 11264]
            cv_pe = []
            cv_st = []
            for o in range(20):
                d = o % 2
                deps = [(sDV, halo_done)]
                if o >= 2:
                    deps.append((sPE, cv_pe[o - 2]))
                v = dma("sync", ar.bf16(CVB[d], 10240).rearrange("p (t c) -> p t c", c=128),
                        attn_tile_ap(CV_BASE + o * 80, 80), sW[d], deps=deps)
                plan.wait("tensor", sW[d], v)
                plan.wait("tensor", sDV, halo_done)
                pT = pA if o % 2 == 0 else pB
                if o >= 2:
                    plan.wait("tensor", sDV, cv_st[o - 2])
                for tap in range(4):
                    for ic in range(20):
                        w = ar.bf16(CVB[d], 10240)[:, (tap * 20 + ic) * 128:(tap * 20 + ic + 1) * 128]
                        for tau in range(4):
                            i_ap = h_view(ic, tau * 512 + 2 * tap, 512)
                            last = (tap == 3 and ic == 19 and tau == 3)
                            comp("tensor", lambda E, oo=pT[:, tau * 512:(tau + 1) * 512], w=w, i=i_ap,
                                 st=(tap == 0 and ic == 0), sp=(tap == 3 and ic == 19):
                                 E.matmul(oo, w, i, start=st, stop=sp),
                                 sem=(sPE if last else None))
                cv_pe.append(sPE.n)
                comp("vector", lambda E, o_=ar.bf16(HCS[d], 2048), i=pT[:, :], b=vec_col(V_CB + o):
                     E.tensor_scalar(o_, i, b, None, OP.add), sem=sDV, deps=[(sPE, cv_pe[o])])
                v_hc = sDV.n
                dma("gpsimd", hc_buf.ap()[o], ar.bf16(HCS[d], 2048), sHS, deps=[(sDV, v_hc)])
                cv_st.append(v_hc)
            conv_st = sHS.n

            cut(6)
            ar.cur = A0
            P_HC = ar.alloc(5120)
            P_A = ar.alloc(10240)
            P_B = ar.alloc(10240)
            P_C = ar.alloc(10240)
            P_W = [ar.alloc(2560), ar.alloc(2560)]
            P_CAR = ar.alloc(48)
            gates_dv = 0

            hcv = ar.bf16(P_HC, 10240)
            Av = ar.f32(P_A, 10240)
            Bv = ar.f32(P_B, 10240)
            Cv = ar.f32(P_C, 10240)
            HLv = ar.bf16(P_C, 10240)
            ACv = ar.bf16(P_C + 5120, 10240)
            car_hl = ar.bf16(P_CAR, 40)
            car_ac = ar.bf16(P_CAR + 24, 40)

            ew_pe = []
            for tau in range(4):
                deps = [(sHS, conv_st)]
                if tau >= 1:
                    deps.append((sDV, gates_dv))
                    deps.append((sGP, gates_gp))
                for c in range(20):
                    dma("sync", hcv[:, c * 512:(c + 1) * 512],
                        hc_buf.ap()[c, :, tau * 512:(tau + 1) * 512], sHC,
                        deps=deps if c == 0 else [])
                hc_ld = sHC.n
                epe = []
                eev = []
                for e in range(20):
                    d = e % 2
                    depw = []
                    if e >= 2:
                        depw.append((sPE, epe[e - 2]))
                    v = dma("sync", ar.bf16(P_W[d], 2560).rearrange("p (t c) -> p t c", c=128),
                            attn_tile_ap(WA_BASE + e * 20, 20), sW[d], deps=depw)
                    v = dma("sync", ar.bf16(P_W[d] + 1280, 2560).rearrange("p (t c) -> p t c", c=128),
                            attn_tile_ap(WX_BASE + e * 20, 20), sW[d])
                    plan.wait("tensor", sW[d], v)
                    plan.wait("tensor", sHC, hc_ld)
                    if e >= 2:
                        plan.wait("tensor", sSC, eev[e - 2])
                    bnk_r = pA[:, (e % 2) * 512:(e % 2) * 512 + 512]
                    bnk_i = pA[:, 1024 + (e % 2) * 512:1024 + (e % 2) * 512 + 512]
                    for dc in range(20):
                        w = ar.bf16(P_W[d], 2560)[:, dc * 128:(dc + 1) * 128]
                        comp("tensor", lambda E, o=bnk_r, w_=w, i=hcv[:, dc * 512:(dc + 1) * 512],
                             st=(dc == 0), sp=(dc == 19):
                             E.matmul(o, w_, i, start=st, stop=sp))
                    for dc in range(20):
                        w = ar.bf16(P_W[d] + 1280, 2560)[:, dc * 128:(dc + 1) * 128]
                        last = (dc == 19)
                        comp("tensor", lambda E, o=bnk_i, w_=w, i=hcv[:, dc * 512:(dc + 1) * 512],
                             st=(dc == 0), sp=(dc == 19):
                             E.matmul(o, w_, i, start=st, stop=sp),
                             sem=(sPE if last else None))
                    epe.append(sPE.n)
                    comp("scalar", lambda E, o=Av[:, e * 512:(e + 1) * 512], i=bnk_r, b=vec_col(V_BA + e):
                         E.activation(o, i, AF.Sigmoid, bias=b), deps=[(sPE, epe[e])])
                    comp("scalar", lambda E, o=Bv[:, e * 512:(e + 1) * 512], i=bnk_i, b=vec_col(V_BX + e):
                         E.activation(o, i, AF.Sigmoid, bias=b), sem=sSC)
                    eev.append(sSC.n)
                gates_sc = sSC.n
                for c in range(20):
                    comp("vector", lambda E, o=Av[:, c * 512:(c + 1) * 512], s=ar.f32(A_SP + c, 1):
                         E.tensor_scalar(o, o, s, -8.0, OP.mult, OP.mult),
                         sem=(sDV if c == 19 else None),
                         deps=[(sSC, gates_sc), sp_ready] if c == 0 else [])
                v_loga = sDV.n
                comp("scalar", lambda E: E.activation(Av, Av, AF.Exp), deps=[(sDV, v_loga)])
                comp("scalar", lambda E: E.activation(Cv, Av, AF.Square), sem=sSC)
                v_a = sSC.n
                comp("vector", lambda E: E.tensor_scalar(Cv, Cv, -1.0, 1.0, OP.mult, OP.add),
                     sem=sDV, deps=[(sSC, v_a)])
                comp("scalar", lambda E: E.activation(Cv, Cv, AF.Sqrt), sem=sSC,
                     deps=[(sDV, sDV.n)])
                v_mult = sSC.n
                comp("vector", lambda E: E.tensor_tensor(Bv, Bv, hcv, OP.mult),
                     deps=[(sSC, v_mult)])
                comp("vector", lambda E: E.tensor_tensor(Bv, Bv, Cv, OP.mult), sem=sDV)
                v_b = sDV.n
                plan.wait("vector", sHLS, sHLS.n)
                for c in range(20):
                    if tau == 0:
                        comp("vector", lambda E, o=HLv[:, c * 512:(c + 1) * 512],
                             a=Av[:, c * 512:(c + 1) * 512], b=Bv[:, c * 512:(c + 1) * 512]:
                             E.tensor_tensor_scan(o, a, b, 0.0, OP.mult, OP.add))
                        comp("vector", lambda E, o=ACv[:, c * 512:(c + 1) * 512],
                             a=Av[:, c * 512:(c + 1) * 512]:
                             E.tensor_tensor_scan(o, a, zeros512, 1.0, OP.mult, OP.add))
                    else:
                        comp("vector", lambda E, o=HLv[:, c * 512:(c + 1) * 512],
                             a=Av[:, c * 512:(c + 1) * 512], b=Bv[:, c * 512:(c + 1) * 512],
                             i=car_hl[:, c:c + 1]:
                             E.tensor_tensor_scan(o, a, b, i, OP.mult, OP.add))
                        comp("vector", lambda E, o=ACv[:, c * 512:(c + 1) * 512],
                             a=Av[:, c * 512:(c + 1) * 512], i=car_ac[:, c:c + 1]:
                             E.tensor_tensor_scan(o, a, zeros512, i, OP.mult, OP.add))
                hl_last = HLv.rearrange("p (c t) -> p c t", t=512)[:, :, 511]
                ac_last = ACv.rearrange("p (c t) -> p c t", t=512)[:, :, 511]
                comp("vector", lambda E, o=car_hl[:, 0:20], i=hl_last: E.tensor_copy(o, i))
                comp("vector", lambda E, o=car_ac[:, 0:20], i=ac_last: E.tensor_copy(o, i),
                     sem=sDV)
                v_scan = sDV.n
                v_scan_gp = sGP.n
                for c in range(20):
                    dma("gpsimd", hl_buf.ap()[c, :, tau * 512:(tau + 1) * 512],
                        HLv[:, c * 512:(c + 1) * 512], sHLS,
                        deps=[(sDV, v_scan), (sGP, v_scan_gp)] if c == 0 else [])
                for c in range(20):
                    dma("gpsimd", ac_buf.ap()[c, :, tau * 512:(tau + 1) * 512],
                        ACv[:, c * 512:(c + 1) * 512], sHLS)
                gates_dv = v_scan
                gates_gp = v_scan_gp
            hlac_st = sHLS.n

            E2S = ar.alloc(20)
            e2p = ar.bf16(E2S, 40).rearrange("p (c k) -> p c k", k=2)
            comp("vector", lambda E: E.tensor_copy(e2p[:, :, 0], car_ac[:, 0:20]))
            comp("vector", lambda E: E.tensor_copy(e2p[:, :, 1], car_hl[:, 0:20]), sem=sDV)
            v = dma("gpsimd", e2_in.ap(), ar.bf16(E2S, 40), sE2S, deps=[(sDV, sDV.n)])
            plan.wait("gpsimd", sE2S, v)
            plan.ins("gpsimd", lambda E: E.collective_compute(
                "AllGather", OP.bypass, replica_groups=[list(range(8))],
                ins=[e2_in.ap().opt()], outs=[e2_out.ap().opt()]), inc=sCC)
            e2_done = sCC.n
            ar.cur = A0
            E2G = ar.alloc(8 * 20)
            for j in range(8):
                dma("sync", ar.bf16(E2G + j * 20, 40), e2_out.ap()[j * 128:(j + 1) * 128, :],
                    sE2L, deps=[(sCC, e2_done), (sHLS, hlac_st)] if j == 0 else [])
            e2g_ld = sE2L.n
            CFOLD = ar.alloc(9 * 20)
            HIN = ar.alloc(20)
            h_in = ar.f32(HIN, 20)
            comp("vector", lambda E: E.memset(ar.f32(CFOLD, 20), 0.0), deps=[(sE2L, e2g_ld)])
            comp("vector", lambda E: E.memset(h_in, 0.0))
            for j in range(7):
                g = ar.bf16(E2G + j * 20, 40).rearrange("p (c k) -> p c k", k=2)
                cj = ar.f32(CFOLD + j * 20, 20)
                cj1 = ar.f32(CFOLD + (j + 1) * 20, 20)
                comp("vector", lambda E, o=cj1, a=g[:, :, 0], c_=cj: E.tensor_tensor(o, a, c_, OP.mult))
                comp("vector", lambda E, o=cj1, b=g[:, :, 1]: E.tensor_tensor(o, o, b, OP.add))
                comp("vector", lambda E, o=cj1, s=ar.f32(A_MSK + j + 1, 1):
                     E.tensor_scalar(o, o, s, None, OP.mult))
            for j in range(1, 8):
                cj = ar.f32(CFOLD + j * 20, 20)
                comp("vector", lambda E, o=h_in, c_=cj, s=ar.f32(A_MSK + 8 + j, 1):
                     E.scalar_tensor_tensor(o, c_, s, o, OP.mult, OP.add),
                     sem=(sDV if j == 7 else None))
            hin_done = sDV.n

            cut(7)
            P3_HL = [ar.alloc(5120), ar.alloc(5120)]
            P3_AC = [ar.alloc(5120), ar.alloc(5120)]
            P3_HC = [ar.alloc(5120), ar.alloc(5120)]
            P3_W = [ar.alloc(1280), ar.alloc(1280)]
            P3_X = [ar.alloc(512), ar.alloc(512)]
            P3_O = [ar.alloc(512), ar.alloc(512)]
            dst_marks = []
            p3_dv = 0
            x2_st = 0
            p3_pe_marks = []
            for tau in range(4):
                tp = tau % 2
                hl_t = ar.bf16(P3_HL[tp], 10240)
                ac_t = ar.bf16(P3_AC[tp], 10240)
                hcor = ar.bf16(P3_HC[tp], 10240)
                deps = [(sHLS, hlac_st), (sDV, hin_done)]
                if tau >= 2:
                    deps.append((sPE, p3_pe_marks[tau - 2]))
                for c in range(20):
                    dma("sync", hl_t[:, c * 512:(c + 1) * 512],
                        hl_buf.ap()[c, :, tau * 512:(tau + 1) * 512], sHL,
                        deps=deps if c == 0 else [])
                for c in range(20):
                    dma("sync", ac_t[:, c * 512:(c + 1) * 512],
                        ac_buf.ap()[c, :, tau * 512:(tau + 1) * 512], sHL)
                v_ld = sHL.n
                for c in range(20):
                    comp("vector", lambda E, o=hcor[:, c * 512:(c + 1) * 512],
                         a=ac_t[:, c * 512:(c + 1) * 512], b=hl_t[:, c * 512:(c + 1) * 512],
                         s=ar.f32(HIN + c, 1):
                         E.scalar_tensor_tensor(o, a, s, b, OP.mult, OP.add),
                         sem=(sDV if c == 19 else None),
                         deps=[(sHL, v_ld), (sDV, hin_done)] if c == 0 else [])
                v_cor = sDV.n
                dpe = []
                dev = []
                for dd in range(16):
                    d = dd % 2
                    depw = []
                    if dd >= 2:
                        depw.append((sPE, dpe[dd - 2]))
                    if len(dst_marks) >= 2:
                        depw.append(dst_marks[-2])
                    v = dma("sync", ar.bf16(P3_W[d], 2560).rearrange("p (t c) -> p t c", c=128),
                            attn_tile_ap(W2_BASE + dd * 20, 20), sW[d], deps=depw)
                    vx = dma("sync", ar.f32(P3_X[d], 512),
                             x1_buf.ap()[dd * 128:(dd + 1) * 128, tau * 512:(tau + 1) * 512], sX[d])
                    plan.wait("tensor", sW[d], v)
                    plan.wait("tensor", sDV, v_cor)
                    if dd >= 4:
                        plan.wait("tensor", sDV, dev[dd - 4])
                    bank = [pA[:, 0:512], pA[:, 512:1024], pB[:, 0:512], pB[:, 512:1024]][dd % 4]
                    for rc_ in range(20):
                        w = ar.bf16(P3_W[d], 2560)[:, rc_ * 128:(rc_ + 1) * 128]
                        last = (rc_ == 19)
                        comp("tensor", lambda E, o=bank, w_=w, i=hcor[:, rc_ * 512:(rc_ + 1) * 512],
                             st=(rc_ == 0), sp=(rc_ == 19):
                             E.matmul(o, w_, i, start=st, stop=sp),
                             sem=(sPE if last else None))
                    dpe.append(sPE.n)
                    comp("vector", lambda E, o=ar.f32(P3_O[d], 512), p=bank,
                         b=vec_col(V_B2 + dd), x=ar.f32(P3_X[d], 512):
                         E.scalar_tensor_tensor(o, p, b, x, OP.add, OP.add),
                         sem=sDV, deps=[(sPE, dpe[dd]), (sX[d], vx)])
                    dev.append(sDV.n)
                    dma("gpsimd", x2_buf.ap()[dd * 128:(dd + 1) * 128, tau * 512:(tau + 1) * 512],
                        ar.f32(P3_O[d], 512), sO[d], deps=[(sDV, dev[dd])])
                    dst_marks.append((sO[d], sO[d].n))
                p3_dv = sDV.n
                p3_pe_marks.append(sPE.n)
            x2_st = [(sO0, sO0.n), (sO1, sO1.n)]

            cut(8)
            for s_, v_ in x2_st:
                plan.wait("sync", s_, v_)
            out_st, _ = emit_ffn(x2_buf.ap(), d_out, False)

        try:
            _emit_all()
        except _Cut:
            pass
        for nm_ in SEM_NAMES:
            plan.wait("sync", sems[nm_], sems[nm_].n)
        if os.environ.get("ATH_DBG"):
            dbgs = [("dbg_x1", x1_buf, [D, T], F32),
                    ("dbg_hc", hc_buf, [RC, 128, T], BF16),
                    ("dbg_hl", hl_buf, [RC, 128, T], BF16),
                    ("dbg_ac", ac_buf, [RC, 128, T], BF16),
                    ("dbg_x2", x2_buf, [D, T], F32),
                    ("dbg_ccB", ccB_out, [16384, 2048], BF16),
                    ("dbg_attnbf", attn_bf, [3040 * TILE], BF16),
                    ("dbg_inter", inter_buf, [H // 128, 128, T], BF16)]
            for nm_, buf_, shp_, dt_ in dbgs:
                o_ = nc.dram_tensor(nm_, shp_, dt_, kind="ExternalOutput").ap()
                sST.inc(16)
                plan.emit("sync", lambda E, a=o_, b=buf_.ap(): E.dma_start(out=a, in_=b).then_inc(sST.h, 16))
            o_ar = nc.dram_tensor("dbg_arena", [128, 48000], F32, kind="ExternalOutput").ap()
            sST.inc(16)
            plan.emit("sync", lambda E, a=o_ar: E.dma_start(out=a, in_=ar_t[:, :]).then_inc(sST.h, 16))
            plan.wait("sync", sST, sST.n)


        with nc.Block() as block:
            @block.sync
            def _(E):
                for fn in plan.q["sync"]:
                    fn(E)

            @block.gpsimd
            def _(E):
                for fn in plan.q["gpsimd"]:
                    fn(E)

            @block.vector
            def _(E):
                for fn in plan.q["vector"]:
                    fn(E)

            @block.scalar
            def _(E):
                for fn in plan.q["scalar"]:
                    fn(E)

            @block.tensor
            def _(E):
                for fn in plan.q["tensor"]:
                    fn(E)

    return nc


_CACHE = {}


def kernel(**inputs):
    x = np.asarray(inputs["L0_attn_x"], np.float32)
    w1 = np.asarray(inputs["L0_attn_w1"], np.float32)
    b1 = np.asarray(inputs["L0_attn_b1"], np.float32)
    conv_w = np.asarray(inputs["L0_attn_conv_w"], np.float32)
    conv_b = np.asarray(inputs["L0_attn_conv_b"], np.float32)
    wa = np.asarray(inputs["L0_attn_wa"], np.float32)
    ba = np.asarray(inputs["L0_attn_ba"], np.float32)
    wx = np.asarray(inputs["L0_attn_wx"], np.float32)
    bx = np.asarray(inputs["L0_attn_bx"], np.float32)
    lam = np.asarray(inputs["L0_attn_lam"], np.float32)
    w2 = np.asarray(inputs["L0_attn_w2"], np.float32)
    b2 = np.asarray(inputs["L0_attn_b2"], np.float32)
    gpk = np.ascontiguousarray(np.asarray(inputs["L0_ffn_gate_W_packed"], np.int32))
    gsc = np.ascontiguousarray(np.asarray(inputs["L0_ffn_gate_scales"], np.float32))
    upk = np.ascontiguousarray(np.asarray(inputs["L0_ffn_up_W_packed"], np.int32))
    usc = np.ascontiguousarray(np.asarray(inputs["L0_ffn_up_scales"], np.float32))
    dpk = np.ascontiguousarray(np.asarray(inputs["L0_ffn_down_W_packed"], np.int32))
    dsc = np.ascontiguousarray(np.asarray(inputs["L0_ffn_down_scales"], np.float32))

    if "nc" not in _CACHE:
        _CACHE["nc"] = build_program()
    nc = _CACHE["nc"]

    w1T = np.ascontiguousarray(w1.T)
    convT = np.ascontiguousarray(conv_w.transpose(2, 1, 0))
    waT = np.ascontiguousarray(wa.T)
    wxT = np.ascontiguousarray(wx.T)
    w2T = np.ascontiguousarray(w2.T)

    def chunkify(v, n):
        return np.ascontiguousarray(v.reshape(n, 128).T)

    vecs = np.zeros((128, 128), np.float32)
    vecs[:, 0:20] = chunkify(b1, 20)
    vecs[:, 20:40] = chunkify(conv_b, 20)
    vecs[:, 40:60] = chunkify(ba, 20)
    vecs[:, 60:80] = chunkify(bx, 20)
    vecs[:, 80:100] = chunkify(lam, 20)
    vecs[:, 100:116] = chunkify(b2, 16)
    consts = np.array([GC3, GC1, 0.5, 7.5, -4.0, -6.0, -1.0, 1.0], np.float32)
    vecs[:, 116:124] = np.tile(consts[None, :], (128, 1))

    ones = np.zeros((128, 130), BF)
    ones[:, 0] = BF(1.0)
    ones[0, 1:129] = BF(1.0)
    ident = np.eye(128, dtype=BF)

    in_maps = []
    for c in range(8):
        b = c // 4
        t0 = 2048 * (c % 4)
        msk = np.zeros((128, 32), np.float32)
        for j in range(8):
            msk[:, j] = 0.0 if j in (0, 4) else 1.0
        msk[:, 8 + c] = 1.0
        if c not in (0, 4):
            msk[:, 16 + (c - 1)] = 1.0
        if c not in (3, 7):
            msk[:, 24 + (c + 1)] = 1.0
        im = {
            "x_fm": np.ascontiguousarray(x[b, t0:t0 + T, :].T),
            "w1T": w1T, "convT": convT, "waT": waT, "wxT": wxT, "w2T": w2T,
            "vecs": vecs, "onesb": ones, "ident": ident, "msk": msk,
            "gpk": np.ascontiguousarray(gpk[c * GSL:(c + 1) * GSL]),
            "gsc": np.ascontiguousarray(gsc[c * GSL:(c + 1) * GSL]),
            "upk": np.ascontiguousarray(upk[c * GSL:(c + 1) * GSL]),
            "usc": np.ascontiguousarray(usc[c * GSL:(c + 1) * GSL]),
            "dpk": np.ascontiguousarray(dpk[c * DSL:(c + 1) * DSL]),
            "dsc": np.ascontiguousarray(dsc[c * DSL:(c + 1) * DSL]),
        }
        in_maps.append(im)

    res = run_bass_kernel_spmd(nc, in_maps, core_ids=list(range(8)))

    out = np.empty((2, 8192, 2048), np.float32)
    for c in range(8):
        b = c // 4
        t0 = 2048 * (c % 4)
        out[b, t0:t0 + T, :] = np.asarray(res.results[c]["out_fm"]).T
    return out


# version 13
# speedup vs baseline: 1.1126x; 1.0124x over previous
import contextlib
import os
import numpy as np
import ml_dtypes
import concourse.bass as bass
import concourse.mybir as mybir
from concourse.bass_utils import run_bass_kernel_spmd

F32 = mybir.dt.float32
BF16 = mybir.dt.bfloat16
I32 = mybir.dt.int32
AF = mybir.ActivationFunctionType
OP = mybir.AluOpType
BF = ml_dtypes.bfloat16

D = 2048
R = 2560
H = 16384
T = 2048
DC, RC, HC = D // 128, R // 128, H // 128
NTAU = 4
KB_G = D // 2
GSL = H // 8
DSL = D // 8
TILE = 128 * 128

GC1 = 0.7978845608028654
GC3 = 0.044715 * GC1
LN_EPS = 1e-5

ENGS = ("sync", "gpsimd", "vector", "scalar", "tensor")


class Sem:
    def __init__(self, name):
        self.name = name
        self.h = None
        self.n = 0

    def inc(self, k=1):
        self.n += k
        return self.n


class Plan:
    def __init__(self):
        self.q = {e: [] for e in ENGS}
        self.lastw = {e: {} for e in ENGS}

    def emit(self, eng, fn):
        self.q[eng].append(fn)

    def wait(self, eng, sem, val=None):
        v = sem.n if val is None else val
        if v <= 0:
            return
        if self.lastw[eng].get(sem.name, 0) >= v:
            return
        self.lastw[eng][sem.name] = v
        self.q[eng].append(lambda E, s=sem, vv=v: E.wait_ge(s.h, vv))

    def ins(self, eng, build, inc=None, inck=1):
        if inc is None:
            self.emit(eng, build)
        else:
            inc.inc(inck)
            self.emit(eng, lambda E, b=build, s=inc, k=inck: b(E).then_inc(s.h, k))


class Arena:

    def __init__(self, handle, base, cap):
        self.t = handle
        self.base = base
        self.cap = cap
        self.cur = base

    def reset(self):
        self.cur = self.base

    def alloc(self, n_f32):
        n_f32 = (n_f32 + 1) & ~1
        off = self.cur
        self.cur += n_f32
        assert self.cur <= self.cap, f"arena overflow: {self.cur} > {self.cap}"
        return off

    def f32(self, off, cols, parts=(0, 128)):
        return self.t[parts[0]:parts[1], off:off + cols]

    def bf16(self, off, cols, parts=(0, 128)):
        assert cols % 2 == 0
        return self.t[parts[0]:parts[1], off:off + cols // 2].bitcast(BF16)

    def i32(self, off, cols, parts=(0, 128)):
        return self.t[parts[0]:parts[1], off:off + cols].bitcast(I32)


def ceil_div(a, b):
    return (a + b - 1) // b



class _Cut(Exception):
    pass


def build_program():
    PH = int(os.environ.get("ATH_PH", "9"))

    def cut(n):
        if PH < n:
            raise _Cut()

    nc = bass.Bass("TRN2", target_bir_lowering=False, debug=False)

    def ein(name, shape, dt):
        return nc.dram_tensor(name, shape, dt, kind="ExternalInput").ap()

    d_x = ein("x_fm", [D, T], F32)
    d_w1T = ein("w1T", [D, R], F32)
    d_convT = ein("convT", [4, R, R], F32)
    d_waT = ein("waT", [R, R], F32)
    d_wxT = ein("wxT", [R, R], F32)
    d_w2T = ein("w2T", [R, D], F32)
    d_vecs = ein("vecs", [128, 128], F32)
    d_ones = ein("onesb", [128, 130], BF16)
    d_ident = ein("ident", [128, 128], BF16)
    d_gpk = ein("gpk", [GSL, KB_G], I32)
    d_gsc = ein("gsc", [GSL, D // 32], F32)
    d_upk = ein("upk", [GSL, KB_G], I32)
    d_usc = ein("usc", [GSL, D // 32], F32)
    d_dpk = ein("dpk", [DSL, H // 2], I32)
    d_dsc = ein("dsc", [DSL, H // 32], F32)

    d_out = nc.dram_tensor("out_fm", [D, T], F32, kind="ExternalOutput").ap()

    ccA_in = [nc.dram_tensor(f"ccA_in{q}", [1024, 2048], BF16) for q in range(4)]
    ccA_out = [nc.dram_tensor(f"ccA_out{q}", [8192, 2048], BF16) for q in range(4)]
    ccB_in = nc.dram_tensor("ccB_in", [2048, 2048], BF16)
    ccB_out = nc.dram_tensor("ccB_out", [16384, 2048], BF16)
    attn_bf = nc.dram_tensor("attn_bf", [3040 * TILE], BF16)
    x1_buf = nc.dram_tensor("x1_buf", [D, T], F32)
    x2_buf = nc.dram_tensor("x2_buf", [D, T], F32)
    inter_buf = nc.dram_tensor("inter_buf", [H // 128, 128, T], BF16)
    hc_buf = nc.dram_tensor("hc_buf", [RC, 128, T], BF16)
    hl_buf = nc.dram_tensor("hl_buf", [RC, 128, T], BF16)
    ac_buf = nc.dram_tensor("ac_buf", [RC, 128, T], BF16)
    e1_in = nc.dram_tensor("e1_in", [128, 160], BF16)
    e1_out = nc.dram_tensor("e1_out", [1024, 160], BF16)
    e2_in = nc.dram_tensor("e2_in", [128, 40], BF16)
    e2_out = nc.dram_tensor("e2_out", [1024, 40], BF16)
    dbg_gu = (nc.dram_tensor("dbg_gu", [2, 128, 2048], F32, kind="ExternalOutput")
              if os.environ.get("ATH_DBG") else None)

    W1_BASE = 0
    CV_BASE = W1_BASE + 20 * 16
    WA_BASE = CV_BASE + 20 * 80
    WX_BASE = WA_BASE + 400
    W2_BASE = WX_BASE + 400
    N_TILES = W2_BASE + 320
    assert N_TILES == 3040

    def attn_tile_ap(tile_idx, ntiles=1):
        a = attn_bf.ap()[tile_idx * TILE:(tile_idx + ntiles) * TILE]
        return a.rearrange("(t p c) -> p t c", p=128, c=128)

    es = contextlib.ExitStack()
    with es:
        ar_t = es.enter_context(nc.sbuf_tensor("arena", [128, 48000], F32))
        pA = es.enter_context(nc.psum_tensor("pA", [128, 2048], F32))
        pB = es.enter_context(nc.psum_tensor("pB", [128, 2048], F32))

        sems = {}
        SEM_NAMES = ("sLD", "sGP", "sPE", "sDV", "sSC", "sCC", "sST",
                     "sQ0", "sQ1", "sSa", "sSb", "sSc", "sSd", "sS5a", "sS5b",
                     "sR0", "sR1", "sW0", "sW1", "sI0", "sI1", "sDX", "sDS",
                     "sT0", "sT1", "sHC", "sHS", "sHLS", "sHL",
                     "sX0", "sX1", "sO0", "sO1", "sE1L", "sE1S", "sE2L", "sE2S")
        for nm in SEM_NAMES:
            sems[nm] = Sem(nm)
            sems[nm].h = es.enter_context(nc.semaphore(nm))
        (sLD, sGP, sPE, sDV, sSC, sCC, sST,
         sQ0, sQ1, sSa, sSb, sSc, sSd, sS5a, sS5b,
         sR0, sR1, sW0, sW1, sI0, sI1, sDX, sDS,
         sT0, sT1, sHC, sHS, sHLS, sHL,
         sX0, sX1, sO0, sO1, sE1L, sE1S, sE2L, sE2S) = (sems[n] for n in SEM_NAMES)
        sQ = [sQ0, sQ1]
        sR = [sR0, sR1]
        sW = [sW0, sW1]
        sI = [sI0, sI1]
        sT = [sT0, sT1]
        sX = [sX0, sX1]
        sO = [sO0, sO1]
        sS5 = [sS5a, sS5b]

        V_B1, V_CB, V_BA, V_BX, V_LAM, V_B2 = 0, 20, 40, 60, 80, 100

        d_msk = ein("msk", [128, 32], F32)

        CONST_F32 = 0
        ar = Arena(ar_t, 0, 48000)
        A_VEC = ar.alloc(128)
        A_MSK = ar.alloc(32)
        A_ONE = ar.alloc(66)
        A_ID = ar.alloc(64)
        A_ZER = ar.alloc(512)
        A_SP = ar.alloc(20)
        A0 = ar.cur

        def vec_col(c0, n=1):
            return ar.f32(A_VEC + c0, n)

        ones_col = ar.bf16(A_ONE, 2)[:, 0:1]
        ones_row = ar.bf16(A_ONE, 130)[0:1, 1:129]
        ident = ar.bf16(A_ID, 128)
        zeros512 = ar.f32(A_ZER, 512)

        CONSTS = [GC3, GC1, 0.5, 7.5, -4.0, -6.0, -1.0, 1.0]
        for i, v in enumerate(CONSTS):
            nc.const_aps.register(F32, v, ar_t[:, A_VEC + 116 + i:A_VEC + 117 + i])

        plan = Plan()

        def dma(qeng, out_ap, in_ap, sem, deps=()):
            for (s, v) in deps:
                plan.wait(qeng, s, v)
            plan.ins(qeng, lambda E, o=out_ap, i=in_ap: E.dma_start(out=o, in_=i),
                     inc=sem, inck=16)
            return sem.n

        def comp(eng, build, sem=None, deps=()):
            for (s, v) in deps:
                plan.wait(eng, s, v)
            plan.ins(eng, build, inc=sem)
            return None if sem is None else sem.n

        def _emit_all():
            n0 = dma("sync", ar.f32(A_VEC, 128), d_vecs, sLD)
            dma("sync", ar.f32(A_MSK, 32), d_msk, sLD)
            dma("sync", ar.bf16(A_ONE, 130), d_ones, sLD)
            dma("sync", ar.bf16(A_ID, 128), d_ident, sLD)
            ld_consts = sLD.n
            comp("vector", lambda E: E.memset(zeros512, 0.0), sem=sDV)

            SP_T1 = A0
            lam = ar.f32(A_VEC + V_LAM, 20)
            t1 = ar.f32(SP_T1, 20)
            t2 = ar.f32(SP_T1 + 20, 20)
            comp("scalar", lambda E: E.activation(t1, lam, AF.Abs), deps=[(sLD, ld_consts)])
            comp("scalar", lambda E: E.activation(t1, t1, AF.Exp, scale=-1.0))
            comp("scalar", lambda E: E.activation(t1, t1, AF.Ln, bias=1.0), sem=sSC)
            comp("vector", lambda E: E.tensor_scalar_max(t2, lam, 0.0),
                 deps=[(sLD, ld_consts)])
            comp("vector", lambda E: E.tensor_add(ar.f32(A_SP, 20), t2, t1),
                 sem=sDV, deps=[(sSC, sSC.n)])
            sp_ready = (sDV, sDV.n)

            cut(1)
            ar.cur = A0 + 64
            DQ = []
            for par in range(2):
                d = {}
                d["pk"] = ar.alloc(1024)
                d["lo"] = ar.alloc(1024)
                d["hi"] = ar.alloc(1024)
                d["c"] = ar.alloc(2048)
                d["s"] = ar.alloc(2048)
                d["sc"] = ar.alloc(64)
                d["t"] = ar.alloc(2048)
                d["r4"] = ar.alloc(2048)
                d["r6"] = ar.alloc(2048)
                d["m"] = ar.alloc(2048)
                d["dq"] = ar.alloc(1024)
                d["st"] = ar.alloc(1024)
                DQ.append(d)
            assert ar.cur < 48000

            jobs = []
            for q in range(4):
                for i in range(4 * q, 4 * q + 4):
                    jobs.append(("g", d_gpk, d_gsc, i * 128, 0))
                    jobs.append(("u", d_upk, d_usc, i * 128, 0))
            for rr in range(2):
                for j in range(8):
                    jobs.append(("d", d_dpk, d_dsc, rr * 128, j * 1024))

            job_ld = []
            job_done = []
            ccA_done = []

            for ji, (kind, dpk_, dsc_, row0, kb0) in enumerate(jobs):
                d = DQ[ji % 2]
                deps = []
                if ji >= 2:
                    deps = [job_done[ji - 2]]
                qsem = sQ[ji % 2]
                v = dma("sync", ar.i32(d["pk"], 1024), dpk_[row0:row0 + 128, kb0:kb0 + 1024],
                        qsem, deps=deps)
                v = dma("sync", ar.f32(d["sc"], 64),
                        dsc_[row0:row0 + 128, kb0 // 16:kb0 // 16 + 64], qsem)
                job_ld.append((qsem, v))

                pk = ar.i32(d["pk"], 1024)
                lo = ar.i32(d["lo"], 1024)
                hi = ar.i32(d["hi"], 1024)
                cc = ar.f32(d["c"], 2048)
                ss = ar.f32(d["s"], 2048)
                tt_ = ar.f32(d["t"], 2048)
                r4 = ar.f32(d["r4"], 2048)
                r6 = ar.f32(d["r6"], 2048)
                mm_ = ar.f32(d["m"], 2048)
                dq = ar.bf16(d["dq"], 2048)
                stg = ar.bf16(d["st"], 2048)

                ldv = job_ld[ji]
                comp("vector", lambda E, a=lo, b=pk: E.tensor_single_scalar(a, b, 15, OP.bitwise_and), deps=[ldv])
                comp("vector", lambda E, a=hi, b=pk: E.tensor_single_scalar(a, b, 4, OP.logical_shift_right), sem=sDV)
                c3 = cc.rearrange("p (k two) -> p k two", two=2)
                comp("vector", lambda E, a=c3[:, :, 0], b=lo: E.tensor_copy(a, b))
                comp("vector", lambda E, a=c3[:, :, 1], b=hi: E.tensor_copy(a, b), sem=sDV)
                v_c = sDV.n
                comp("scalar", lambda E, a=ss, b=cc: E.activation(a, b, AF.Sign, bias=7.5, scale=-1.0),
                     sem=sSC, deps=[(sDV, v_c)])
                v_sgn = sSC.n
                comp("vector", lambda E, a=tt_, b=ss, c_=cc: E.scalar_tensor_tensor(a, b, 4.0, c_, OP.mult, OP.add),
                     deps=[(sSC, v_sgn)])
                comp("vector", lambda E, a=tt_: E.tensor_scalar(a, a, -4.0, None, OP.add), sem=sDV)
                v_t = sDV.n
                comp("scalar", lambda E, a=r4, b=tt_: E.activation(a, b, AF.Relu, bias=-4.0),
                     deps=[(sDV, v_t)])
                comp("scalar", lambda E, a=r6, b=tt_: E.activation(a, b, AF.Relu, bias=-6.0), sem=sSC)
                v_r = sSC.n
                comp("vector", lambda E, a=r4, b=tt_: E.tensor_add(a, a, b), deps=[(sSC, v_r)])
                comp("vector", lambda E, a=mm_, b=r4, c_=r6: E.scalar_tensor_tensor(a, b, 0.5, c_, OP.mult, OP.add))
                comp("vector", lambda E, a=mm_, b=ss: E.tensor_mul(a, a, b))
                scb = ar.f32(d["sc"], 64).unsqueeze(2).broadcast_to([128, 64, 32])
                m3 = mm_.rearrange("p (blk j) -> p blk j", j=32)
                d3 = dq.rearrange("p (blk j) -> p blk j", j=32)
                comp("vector", lambda E, a=d3, b=m3, c_=scb: E.tensor_tensor(a, b, c_, OP.mult), sem=sDV)
                v_dq = sDV.n

                for kc in range(16):
                    bank = kc % 4
                    pt = pA[:, bank * 512:bank * 512 + 64].bitcast(BF16)[:, 0:128]
                    depsT = [(sDV, v_dq)]
                    if kc >= 4:
                        depsT.append((sDV, sDV.n))
                    comp("tensor", lambda E, o=pt, i=dq[:, kc * 128:(kc + 1) * 128]:
                         E.transpose(o, i, ident), sem=sPE, deps=depsT)
                    v_tr = sPE.n
                    comp("vector", lambda E, o=stg[:, kc * 128:(kc + 1) * 128], i=pt:
                         E.tensor_copy(o, i), sem=sDV, deps=[(sPE, v_tr)])
                v_stg = sDV.n

                if kind in ("g", "u"):
                    i_ch = row0 // 128
                    q = i_ch // 4
                    j = i_ch % 4
                    off = ((0 if kind == "g" else 4) + j) * 16 * TILE
                    dst = ccA_in[q].ap().rearrange("a b -> (a b)")[off:off + 16 * TILE]
                    dst = dst.rearrange("(t p c) -> p t c", p=128, c=128)
                else:
                    kc0 = kb0 // 64
                    dh = row0 // 128
                    base = ccB_in.ap().rearrange("a b -> (a b)")
                    dst = base[(kc0 * 2 + dh) * TILE:(kc0 * 2 + dh) * TILE + 15 * 2 * TILE + TILE]
                    dst = dst.rearrange("(t x) -> t x", x=TILE)[::2]
                    dst = dst.rearrange("t (p c) -> p t c", p=128, c=128)
                src = stg.rearrange("p (t c) -> p t c", c=128)
                if ji < 32:
                    ssem = [sSa, sSb][ji % 2]
                else:
                    ssem = [sSc, sSd][ji % 2]
                v = dma("gpsimd", dst, src, ssem, deps=[(sDV, v_stg)])
                job_done.append((ssem, v))
                if ji < 32 and ji % 8 == 7:
                    qq = ji // 8
                    plan.wait("gpsimd", job_done[ji][0], job_done[ji][1])
                    plan.wait("gpsimd", job_done[ji - 1][0], job_done[ji - 1][1])
                    plan.ins("gpsimd", lambda E, q_=qq: E.collective_compute(
                        "AllGather", OP.bypass, replica_groups=[list(range(8))],
                        ins=[ccA_in[q_].ap().opt()], outs=[ccA_out[q_].ap().opt()]),
                        inc=sCC)
                    ccA_done.append(sCC.n)

            plan.wait("gpsimd", sSc, sSc.n)
            plan.wait("gpsimd", sSd, sSd.n)
            plan.ins("gpsimd", lambda E: E.collective_compute(
                "AllGather", OP.bypass, replica_groups=[list(range(8))],
                ins=[ccB_in.ap().opt()], outs=[ccB_out.ap().opt()]), inc=sCC)
            ccB_done = sCC.n

            cut(2)
            P05 = {"ji": 0}
            CW_BASE = 48000 - 2 * 3842
            CW = []
            for par in range(2):
                b = CW_BASE + par * 3842
                CW.append({"src": b, "dst": b + 2560})

            cast_jobs = []
            for kc in range(16):
                cast_jobs.append((d_w1T[kc * 128:(kc + 1) * 128, :], 20,
                                  [W1_BASE + r * 16 + kc for r in range(20)]))
            for tap in range(4):
                for ic in range(20):
                    cast_jobs.append((d_convT[tap, ic * 128:(ic + 1) * 128, :], 20,
                                      [CV_BASE + (o * 4 + tap) * 20 + ic for o in range(20)]))
            for dc in range(20):
                cast_jobs.append((d_waT[dc * 128:(dc + 1) * 128, :], 20,
                                  [WA_BASE + e * 20 + dc for e in range(20)]))
            for dc in range(20):
                cast_jobs.append((d_wxT[dc * 128:(dc + 1) * 128, :], 20,
                                  [WX_BASE + e * 20 + dc for e in range(20)]))
            for rc_ in range(20):
                cast_jobs.append((d_w2T[rc_ * 128:(rc_ + 1) * 128, :], 16,
                                  [W2_BASE + dd * 20 + rc_ for dd in range(16)]))
            cj_done = []

            def emit_p05_job():
                ji = P05["ji"]
                if ji >= len(cast_jobs):
                    return False
                P05["ji"] = ji + 1
                src_ap, ncol, tidx = cast_jobs[ji]
                d = CW[ji % 2]
                deps = [cj_done[ji - 2]] if ji >= 2 else []
                v = dma("sync", ar.f32(d["src"], ncol * 128), src_ap, sQ[ji % 2], deps=deps)
                comp("vector", lambda E, o=ar.bf16(d["dst"], ncol * 128),
                     i=ar.f32(d["src"], ncol * 128):
                     E.tensor_copy(o, i), sem=sDV, deps=[(sQ[ji % 2], v)])
                v_c = sDV.n
                stride = (tidx[1] - tidx[0]) * TILE if ncol > 1 else TILE
                base = attn_bf.ap()[tidx[0] * TILE: tidx[0] * TILE + (ncol - 1) * stride + TILE]
                dst = base.rearrange("(t x) -> t x", x=TILE)[::stride // TILE]
                dst = dst.rearrange("t (p c) -> p t c", p=128, c=128)
                srcv = ar.bf16(d["dst"], ncol * 128).rearrange("p (t c) -> p t c", c=128)
                v = dma("gpsimd", dst, srcv, sS5[ji % 2], deps=[(sDV, v_c)])
                cj_done.append((sS5[ji % 2], v))
                return True

            def emit_p05_rest():
                while emit_p05_job():
                    pass
                P05['done'] = [(sS5a, sS5a.n), (sS5b, sS5b.n)]


            def emit_layernorm(src_dram, A_LN_, sc, extra_deps=(), chunk_deps=None):
                plan.wait("tensor", sDV, sDV.n)
                plan.wait("tensor", sSC, sSC.n)
                roll = [sc["roll"] + i * 2048 for i in range(2)]
                sqb = [sc["sqb"] + i * 1024 for i in range(2)]
                xb = [sc["xb"] + i * 1024 for i in range(2)]
                r_mean = ar.f32(sc["rows"], 2048, (0, 1))
                r_var = ar.f32(sc["rows"] + 2048, 2048, (0, 1))
                r_t = ar.f32(sc["rows"] + 4096, 2048, (0, 1))
                hv = sc["rows"] + 6144
                rstd_hi = ar.bf16(hv, 2048, (0, 1))
                rstd_lo = ar.bf16(hv + 1024, 2048, (0, 1))
                mr_hi = ar.bf16(hv + 2048, 2048, (0, 1))
                mr_lo = ar.bf16(hv + 3072, 2048, (0, 1))

                pe_marks = []
                for c in range(16):
                    d = c % 2
                    deps = list(extra_deps) if c == 0 else []
                    if chunk_deps is not None:
                        deps.append(chunk_deps(c))
                    if c >= 2:
                        deps.append((sPE, pe_marks[c - 2]))
                    v = dma("sync", ar.f32(roll[d], 2048), src_dram[c * 128:(c + 1) * 128, :],
                            sR[d], deps=deps)
                    comp("scalar", lambda E, o=ar.bf16(sqb[d], 2048), i=ar.f32(roll[d], 2048):
                         E.activation(o, i, AF.Square), deps=[(sR[d], v)])
                    comp("scalar", lambda E, o=ar.bf16(xb[d], 2048), i=ar.f32(roll[d], 2048):
                         E.activation(o, i, AF.Copy), sem=sSC)
                    v_s = sSC.n
                    plan.wait("tensor", sSC, v_s)
                    for tau in range(4):
                        comp("tensor", lambda E, o=pA[0:1, tau * 512:(tau + 1) * 512],
                             i=ar.bf16(xb[d], 2048)[:, tau * 512:(tau + 1) * 512],
                             st=(c == 0), sp=(c == 15):
                             E.matmul(o, ones_col, i, start=st, stop=sp))
                    for tau in range(4):
                        last = (tau == 3)
                        comp("tensor", lambda E, o=pB[0:1, tau * 512:(tau + 1) * 512],
                             i=ar.bf16(sqb[d], 2048)[:, tau * 512:(tau + 1) * 512],
                             st=(c == 0), sp=(c == 15):
                             E.matmul(o, ones_col, i, start=st, stop=sp),
                             sem=(sPE if last else None))
                    pe_marks.append(sPE.n)
                stats_done = sPE.n

                comp("vector", lambda E: E.tensor_scalar(r_mean, pA[0:1, :], 1.0 / D, None, OP.mult),
                     deps=[(sPE, stats_done)])
                comp("vector", lambda E: E.tensor_scalar(r_var, pB[0:1, :], 1.0 / D, None, OP.mult))
                comp("vector", lambda E: E.scalar_tensor_tensor(r_t, r_mean, -1.0, r_mean, OP.mult, OP.mult))
                comp("vector", lambda E: E.tensor_add(r_var, r_var, r_t))
                comp("vector", lambda E: E.tensor_scalar(r_var, r_var, LN_EPS, None, OP.add), sem=sDV)
                v_rows = sDV.n
                comp("scalar", lambda E: E.activation(r_t, r_var, AF.Ln), sem=sSC, deps=[(sDV, v_rows)])
                comp("vector", lambda E: E.tensor_scalar(r_t, r_t, -0.5, None, OP.mult), sem=sDV,
                     deps=[(sSC, sSC.n)])
                comp("scalar", lambda E: E.activation(r_var, r_t, AF.Exp), deps=[(sDV, sDV.n)])
                comp("scalar", lambda E: E.activation(rstd_hi, r_var, AF.Copy), sem=sSC)
                v_hi1 = sSC.n
                comp("vector", lambda E: E.tensor_mul(r_t, r_mean, r_var), deps=[(sSC, v_hi1)])
                comp("vector", lambda E: E.tensor_tensor(r_mean, r_var, rstd_hi, OP.subtract))
                comp("vector", lambda E: E.tensor_copy(rstd_lo, r_mean), sem=sDV)
                comp("scalar", lambda E: E.activation(mr_hi, r_t, AF.Copy), sem=sSC,
                     deps=[(sDV, sDV.n)])
                comp("vector", lambda E: E.tensor_tensor(r_mean, r_t, mr_hi, OP.subtract),
                     deps=[(sSC, sSC.n)])
                comp("vector", lambda E: E.tensor_copy(mr_lo, r_mean), sem=sDV)
                v_lo = sDV.n

                plan.wait("tensor", sDV, v_lo)
                for tau in range(4):
                    sl = slice(tau * 512, (tau + 1) * 512)
                    comp("tensor", lambda E, o=pA[:, sl], i=rstd_hi[:, sl]:
                         E.matmul(o, ones_row, i, start=True, stop=False))
                    comp("tensor", lambda E, o=pA[:, sl], i=rstd_lo[:, sl]:
                         E.matmul(o, ones_row, i, start=False, stop=True))
                    comp("tensor", lambda E, o=pB[:, sl], i=mr_hi[:, sl]:
                         E.matmul(o, ones_row, i, start=True, stop=False))
                    last = (tau == 3)
                    comp("tensor", lambda E, o=pB[:, sl], i=mr_lo[:, sl]:
                         E.matmul(o, ones_row, i, start=False, stop=True),
                         sem=(sPE if last else None))
                bcast_done = sPE.n

                stats_sc = sSC.n
                ap_marks = []
                for c in range(16):
                    d = c % 2
                    deps = [(sSC, stats_sc)] if c < 2 else [(sDV, ap_marks[c - 2])]
                    v = dma("sync", ar.f32(roll[d], 2048), src_dram[c * 128:(c + 1) * 128, :],
                            sR[d], deps=deps)
                    comp("vector", lambda E, o=ar.f32(sc["t1"], 2048), i=ar.f32(roll[d], 2048):
                         E.tensor_tensor(o, i, pA[:, :], OP.mult),
                         deps=[(sR[d], v), (sPE, bcast_done)])
                    comp("vector", lambda E, o=ar.bf16(A_LN_ + c * 1024, 2048), i=ar.f32(sc["t1"], 2048):
                         E.tensor_tensor(o, i, pB[:, :], OP.subtract), sem=sDV)
                    ap_marks.append(sDV.n)
                return sDV.n

            def emit_ffn(src_dram, dst_dram, first_call, ln_extra=(), mid=None, per_m=None):
                ar.cur = A0
                A_LN_ = ar.alloc(16384)
                sc = {"roll": ar.alloc(4096), "sqb": ar.alloc(2048), "xb": ar.alloc(2048),
                      "rows": ar.alloc(10240)}
                sc["t1"] = sc["rows"]
                ln_done = emit_layernorm(src_dram, A_LN_, sc, extra_deps=ln_extra)

                ar.cur = A0 + 16384
                G_GS = [ar.alloc(2048), ar.alloc(2048)]
                G_US = [ar.alloc(2048), ar.alloc(2048)]
                G_T1 = ar.alloc(2048)
                G_T2 = ar.alloc(2048)
                G_T3 = ar.alloc(2048)
                G_IS = [ar.alloc(1024), ar.alloc(1024)]
                G_WG = [ar.alloc(1024), ar.alloc(1024)]
                G_WU = [ar.alloc(1024), ar.alloc(1024)]

                def gup_src(m, which):
                    r_, loc = m // 16, m % 16
                    q, j = loc // 4, loc % 4
                    off = (r_ * 128 + ((0 if which == "g" else 4) + j) * 16) * TILE
                    a = ccA_out[q].ap().rearrange("a b -> (a b)")[off:off + 16 * TILE]
                    return a.rearrange("(t p c) -> p t c", p=128, c=128)

                def m_order(idx):
                    q, rest = idx // 32, idx % 32
                    r_, j = rest // 4, rest % 4
                    return r_ * 16 + 4 * q + j

                wld = []
                ev_marks = []
                ep_marks = []
                ep_v2 = []
                ep_v4 = []
                first_mm_dep = [(sDV, ln_done)]
                for mi in range(HC):
                    if per_m is not None:
                        per_m(mi)
                    m = m_order(mi)
                    d = mi % 2
                    deps = [(sCC, ccA_done[(m % 16) // 4])]
                    if mi < 2:
                        deps.append((sDV, ln_done))
                    else:
                        deps.append((sPE, ev_marks[mi - 2][2]))
                    v = dma("sync", ar.bf16(G_WG[d], 2048).rearrange("p (t c) -> p t c", c=128),
                            gup_src(m, "g"), sW[d], deps=deps)
                    v = dma("sync", ar.bf16(G_WU[d], 2048).rearrange("p (t c) -> p t c", c=128),
                            gup_src(m, "u"), sW[d])
                    wld.append((sW[d], v))

                    plan.wait("tensor", wld[mi][0], wld[mi][1])
                    if mi == 0:
                        for s_, vv in first_mm_dep:
                            plan.wait("tensor", s_, vv)
                    half_pe = []
                    for h2 in range(2):
                        if mi >= 1:
                            plan.wait("tensor", sSC, ev_marks[mi - 1][h2])
                        for k in range(16):
                            wg = ar.bf16(G_WG[d], 2048)[:, k * 128:(k + 1) * 128]
                            wu = ar.bf16(G_WU[d], 2048)[:, k * 128:(k + 1) * 128]
                            for tau in (2 * h2, 2 * h2 + 1):
                                ln_sl = ar.bf16(A_LN_ + k * 1024, 2048)[:, tau * 512:(tau + 1) * 512]
                                comp("tensor", lambda E, o=pA[:, tau * 512:(tau + 1) * 512], w=wg, i=ln_sl,
                                     st=(k == 0), sp=(k == 15):
                                     E.matmul(o, w, i, start=st, stop=sp))
                            for tau in (2 * h2, 2 * h2 + 1):
                                ln_sl = ar.bf16(A_LN_ + k * 1024, 2048)[:, tau * 512:(tau + 1) * 512]
                                last = (k == 15 and tau == 2 * h2 + 1)
                                comp("tensor", lambda E, o=pB[:, tau * 512:(tau + 1) * 512], w=wu, i=ln_sl,
                                     st=(k == 0), sp=(k == 15):
                                     E.matmul(o, w, i, start=st, stop=sp),
                                     sem=(sPE if last else None))
                        half_pe.append(sPE.n)
                    ev = []
                    for h2 in range(2):
                        sl = slice(h2 * 1024, (h2 + 1) * 1024)
                        evdeps = [(sPE, half_pe[h2])]
                        if h2 == 0 and mi >= 2:
                            evdeps.append((sDV, ep_v4[mi - 2]))
                        comp("scalar", lambda E, o=ar.f32(G_GS[d], 2048)[:, sl], i=pA[:, sl]:
                             E.activation(o, i, AF.Copy), deps=evdeps)
                        comp("scalar", lambda E, o=ar.f32(G_US[d], 2048)[:, sl], i=pB[:, sl]:
                             E.activation(o, i, AF.Copy), sem=sSC)
                        ev.append(sSC.n)
                    ev_marks.append((ev[0], ev[1], half_pe[1]))
                    if dbg_gu is not None and m == 5 and first_call:
                        dma("gpsimd", dbg_gu.ap()[0], ar.f32(G_GS[d], 2048), sGP,
                            deps=[(sSC, ev[1])])
                        dma("gpsimd", dbg_gu.ap()[1], ar.f32(G_US[d], 2048), sGP)

                    g_s = ar.f32(G_GS[d], 2048)
                    u_s = ar.f32(G_US[d], 2048)
                    sq_deps = [(sSC, ev[1])]
                    if mi >= 1:
                        sq_deps.append((sDV, ep_v2[mi - 1]))
                    comp("scalar", lambda E, o=ar.f32(G_T1, 2048), i=g_s:
                         E.activation(o, i, AF.Square), deps=sq_deps)
                    comp("scalar", lambda E, o=ar.f32(G_T1, 2048), i=ar.f32(G_T1, 2048):
                         E.activation(o, i, AF.Copy, bias=GC1, scale=GC3), sem=sSC)
                    v1 = sSC.n
                    if mi >= 2:
                        plan.wait("vector", ep_marks[mi - 2][0], ep_marks[mi - 2][1])
                    comp("vector", lambda E, o=ar.f32(G_T2, 2048), i=ar.f32(G_T1, 2048), g=g_s:
                         E.tensor_tensor(o, i, g, OP.mult), deps=[(sSC, v1)])
                    comp("vector", lambda E, o=ar.f32(G_T3, 2048), g=g_s, u=u_s:
                         E.tensor_tensor(o, g, u, OP.mult), sem=sDV)
                    v2 = sDV.n
                    ep_v2.append(v2)
                    comp("scalar", lambda E, o=ar.f32(G_T2, 2048), i=ar.f32(G_T2, 2048):
                         E.activation(o, i, AF.Tanh), deps=[(sDV, v2)])
                    comp("scalar", lambda E, o=ar.f32(G_T2, 2048), i=ar.f32(G_T2, 2048):
                         E.activation(o, i, AF.Copy, bias=0.5, scale=0.5), sem=sSC)
                    v3 = sSC.n
                    comp("vector", lambda E, o=ar.bf16(G_IS[d], 2048), a=ar.f32(G_T3, 2048),
                         b=ar.f32(G_T2, 2048): E.tensor_tensor(o, a, b, OP.mult),
                         sem=sDV, deps=[(sSC, v3)])
                    v4 = sDV.n
                    ep_v4.append(v4)
                    v = dma("gpsimd", inter_buf.ap()[m], ar.bf16(G_IS[d], 2048), sT[d],
                            deps=[(sDV, v4)])
                    ep_marks.append((sT[d], v))

                inter_all = [(sT0, sT0.n), (sT1, sT1.n)]
                if mid is not None:
                    mid()

                ar.cur = A0 + 16384
                DW_W = [ar.alloc(2048), ar.alloc(2048)]
                DW_I = [ar.alloc(4096), ar.alloc(4096)]
                DW_XB = [ar.alloc(4096), ar.alloc(4096)]

                dn_kg_pe = []
                pass_store = []
                ev_dv = 0
                for dg in range(4):
                    for th in range(2):
                        pidx = dg * 2 + th
                        DW_X = DW_XB[pidx % 2]
                        if pidx < 2:
                            xdeps = list(inter_all)
                        else:
                            xdeps = [pass_store[pidx - 2]]
                        for di in range(4):
                            dglob = dg * 4 + di
                            dma("sync", ar.f32(DW_X + di * 1024, 1024),
                                src_dram[dglob * 128:(dglob + 1) * 128, th * 1024:(th + 1) * 1024],
                                sDX, deps=xdeps if di == 0 else [])
                        x_ld = sDX.n
                        for kg in range(16):
                            d = kg % 2
                            deps = [(sCC, ccB_done)] + inter_all
                            if len(dn_kg_pe) >= 2:
                                deps.append((sPE, dn_kg_pe[-2]))
                            for si in range(2):
                                s_ = 2 * dg + si
                                off = (s_ * 256 + kg * 8 * 2) * TILE
                                a = ccB_out.ap().rearrange("a b -> (a b)")[off:off + 16 * TILE]
                                a = a.rearrange("(t p c) -> p t c", p=128, c=128)
                                dma("sync", ar.bf16(DW_W[d] + si * 1024, 2048)
                                    .rearrange("p (t c) -> p t c", c=128), a, sI[d],
                                    deps=deps if si == 0 else [])
                            iap = inter_buf.ap()[kg * 8:(kg + 1) * 8, :, th * 1024:(th + 1) * 1024]
                            v_ld = dma("sync", ar.bf16(DW_I[d], 8192).rearrange("p (t c) -> p t c", c=1024),
                                       iap.rearrange("t p c -> p t c"), sI[d])
                            plan.wait("tensor", sI[d], v_ld)
                            if kg == 0 and pidx > 0:
                                plan.wait("tensor", sDV, ev_dv)
                            for kc in range(8):
                                kcg = kg * 8 + kc
                                for di in range(4):
                                    si, dh = di // 2, di % 2
                                    wt = ar.bf16(DW_W[d] + si * 1024, 2048)[:, (kc * 2 + dh) * 128:(kc * 2 + dh + 1) * 128]
                                    pT = pA if di < 2 else pB
                                    bank0 = (di % 2) * 1024
                                    for j in range(2):
                                        it = ar.bf16(DW_I[d], 8192)[:, kc * 1024 + j * 512: kc * 1024 + (j + 1) * 512]
                                        last = (kc == 7 and di == 3 and j == 1)
                                        comp("tensor", lambda E, o=pT[:, bank0 + j * 512: bank0 + (j + 1) * 512],
                                             w=wt, i=it, st=(kcg == 0), sp=(kcg == 127):
                                             E.matmul(o, w, i, start=st, stop=sp),
                                             sem=(sPE if last else None))
                            dn_kg_pe.append(sPE.n)
                        dn_pe = sPE.n
                        for di in range(4):
                            dglob = dg * 4 + di
                            pT = pA if di < 2 else pB
                            bank0 = (di % 2) * 1024
                            comp("vector", lambda E, o=ar.f32(DW_X + di * 1024, 1024),
                                 p=pT[:, bank0:bank0 + 1024], x=ar.f32(DW_X + di * 1024, 1024):
                                 E.scalar_tensor_tensor(o, p, 0.5, x, OP.mult, OP.add),
                                 sem=sDV, deps=[(sPE, dn_pe), (sDX, x_ld)])
                            v_ev = sDV.n
                            dma("gpsimd", dst_dram[dglob * 128:(dglob + 1) * 128, th * 1024:(th + 1) * 1024],
                                ar.f32(DW_X + di * 1024, 1024), sDS, deps=[(sDV, v_ev)])
                        ev_dv = sDV.n
                        pass_store.append((sDS, sDS.n))
                return (sDS, sDS.n), pass_store

            cut(3)
            def p05_per_m(mi):
                want = (mi + 1) * len(cast_jobs) // HC
                while P05["ji"] < want:
                    if not emit_p05_job():
                        break

            ffn1_st, ffn1_passes = emit_ffn(d_x, x1_buf.ap(), True,
                                            ln_extra=[job_done[-1], job_done[-2]],
                                            mid=emit_p05_rest, per_m=p05_per_m)
            cut(4)

            ar.cur = A0
            A_LN2 = ar.alloc(16384)
            A_H = ar.alloc(20544)
            W1B = [ar.alloc(1024), ar.alloc(1024)]
            sc2 = {"roll": A_H, "sqb": A_H + 4096, "xb": A_H + 6144,
                   "rows": A_H + 8192, "t1": A_H + 18432}
            plan.wait("sync", ffn1_st[0], ffn1_st[1])
            ln2_done = emit_layernorm(x1_buf.ap(), A_LN2, sc2)

            def h_view(c, col0, ncols, dtype_bf=True):
                off = A_H
                a = ar.bf16(off, 20544 * 2)
                return a[:, c * 2054 + col0: c * 2054 + col0 + ncols]

            def h_view3(col0, ncols):
                a = ar.bf16(A_H, 20544 * 2)[:, 0:20 * 2054]
                return a.rearrange("p (c t) -> p c t", t=2054)[:, :, col0:col0 + ncols]

            w1_pe = []
            w1_ev = []
            for r_ in range(20):
                d = r_ % 2
                deps = P05['done'] + [(sDV, ln2_done)]
                if r_ >= 2:
                    deps.append((sPE, w1_pe[r_ - 2]))
                v = dma("sync", ar.bf16(W1B[d], 2048).rearrange("p (t c) -> p t c", c=128),
                        attn_tile_ap(W1_BASE + r_ * 16, 16), sW[d], deps=deps)
                plan.wait("tensor", sW[d], v)
                if r_ == 0:
                    plan.wait("tensor", sDV, ln2_done)
                pT = pA if r_ % 2 == 0 else pB
                if r_ >= 2:
                    plan.wait("tensor", sSC, w1_ev[r_ - 2])
                for k in range(16):
                    w = ar.bf16(W1B[d], 2048)[:, k * 128:(k + 1) * 128]
                    for tau in range(4):
                        ln_sl = ar.bf16(A_LN2 + k * 1024, 2048)[:, tau * 512:(tau + 1) * 512]
                        last = (k == 15 and tau == 3)
                        comp("tensor", lambda E, o=pT[:, tau * 512:(tau + 1) * 512], w=w, i=ln_sl,
                             st=(k == 0), sp=(k == 15):
                             E.matmul(o, w, i, start=st, stop=sp),
                             sem=(sPE if last else None))
                w1_pe.append(sPE.n)
                for tau in range(4):
                    last = (tau == 3)
                    comp("scalar", lambda E, o=h_view(r_, 3 + tau * 512, 512),
                         i=pT[:, tau * 512:(tau + 1) * 512], b=vec_col(V_B1 + r_):
                         E.activation(o, i, AF.Silu, bias=b), sem=(sSC if last else None),
                         deps=[(sPE, w1_pe[r_])] if tau == 0 else [])
                w1_ev.append(sSC.n)
            w1_done = sSC.n

            E1S = ar.alloc(80)
            e1p = ar.bf16(E1S, 160)
            e1p3 = e1p.rearrange("p (c k) -> p c k", k=8)
            comp("vector", lambda E: E.tensor_copy(e1p3[:, :, 0:3], h_view3(3, 3)),
                 deps=[(sSC, w1_done)])
            comp("vector", lambda E: E.tensor_copy(e1p3[:, :, 3:6], h_view3(2048, 3)), sem=sDV)
            v = dma("gpsimd", e1_in.ap(), e1p, sE1S, deps=[(sDV, sDV.n)])
            plan.wait("gpsimd", sE1S, v)
            plan.ins("gpsimd", lambda E: E.collective_compute(
                "AllGather", OP.bypass, replica_groups=[list(range(8))],
                ins=[e1_in.ap().opt()], outs=[e1_out.ap().opt()]), inc=sCC)
            e1_done = sCC.n
            E1G = ar.alloc(8 * 80)
            v = None
            for j in range(8):
                v = dma("sync", ar.bf16(E1G + j * 80, 160), e1_out.ap()[j * 128:(j + 1) * 128, :],
                        sE1L, deps=[(sCC, e1_done)] if j == 0 else [])
            e1g_ld = sE1L.n
            ACCL = ar.alloc(120)
            accl = ar.f32(ACCL, 60).rearrange("p (c k) -> p c k", k=3)
            accr = ar.f32(ACCL + 60, 60).rearrange("p (c k) -> p c k", k=3)
            for j in range(8):
                g = ar.bf16(E1G + j * 80, 160).rearrange("p (c k) -> p c k", k=8)
                lsel = ar.f32(A_MSK + 16 + j, 1)
                rsel = ar.f32(A_MSK + 24 + j, 1)
                if j == 0:
                    comp("vector", lambda E, a=accl, b=g[:, :, 3:6], s=lsel:
                         E.tensor_scalar(a, b, s, None, OP.mult), deps=[(sE1L, e1g_ld)])
                    comp("vector", lambda E, a=accr, b=g[:, :, 0:3], s=rsel:
                         E.tensor_scalar(a, b, s, None, OP.mult))
                else:
                    comp("vector", lambda E, a=accl, b=g[:, :, 3:6], s=lsel:
                         E.scalar_tensor_tensor(a, b, s, a, OP.mult, OP.add))
                    comp("vector", lambda E, a=accr, b=g[:, :, 0:3], s=rsel:
                         E.scalar_tensor_tensor(a, b, s, a, OP.mult, OP.add))
            comp("vector", lambda E: E.tensor_copy(h_view3(0, 3), accl))
            comp("vector", lambda E: E.tensor_copy(h_view3(2051, 3), accr), sem=sDV)
            halo_done = sDV.n

            cut(5)
            TAIL = A_H + 20544
            assert TAIL + 10240 <= 48000, TAIL
            CVB = [A_LN2, TAIL]
            HCS = [A_LN2 + 10240, A_LN2 + 11264]
            cv_pe = []
            cv_st = []
            for o in range(20):
                d = o % 2
                deps = [(sDV, halo_done)]
                if o >= 2:
                    deps.append((sPE, cv_pe[o - 2]))
                v = dma("sync", ar.bf16(CVB[d], 10240).rearrange("p (t c) -> p t c", c=128),
                        attn_tile_ap(CV_BASE + o * 80, 80), sW[d], deps=deps)
                plan.wait("tensor", sW[d], v)
                plan.wait("tensor", sDV, halo_done)
                pT = pA if o % 2 == 0 else pB
                if o >= 2:
                    plan.wait("tensor", sDV, cv_st[o - 2])
                for tap in range(4):
                    for ic in range(20):
                        w = ar.bf16(CVB[d], 10240)[:, (tap * 20 + ic) * 128:(tap * 20 + ic + 1) * 128]
                        for tau in range(4):
                            i_ap = h_view(ic, tau * 512 + 2 * tap, 512)
                            last = (tap == 3 and ic == 19 and tau == 3)
                            comp("tensor", lambda E, oo=pT[:, tau * 512:(tau + 1) * 512], w=w, i=i_ap,
                                 st=(tap == 0 and ic == 0), sp=(tap == 3 and ic == 19):
                                 E.matmul(oo, w, i, start=st, stop=sp),
                                 sem=(sPE if last else None))
                cv_pe.append(sPE.n)
                comp("vector", lambda E, o_=ar.bf16(HCS[d], 2048), i=pT[:, :], b=vec_col(V_CB + o):
                     E.tensor_scalar(o_, i, b, None, OP.add), sem=sDV, deps=[(sPE, cv_pe[o])])
                v_hc = sDV.n
                dma("gpsimd", hc_buf.ap()[o], ar.bf16(HCS[d], 2048), sHS, deps=[(sDV, v_hc)])
                cv_st.append(v_hc)
            conv_st = sHS.n

            cut(6)
            ar.cur = A0
            P_HC = ar.alloc(5120)
            P_A = ar.alloc(10240)
            P_B = ar.alloc(10240)
            P_C = ar.alloc(10240)
            P_W = [ar.alloc(2560), ar.alloc(2560)]
            P_CAR = ar.alloc(48)
            gates_dv = 0

            hcv = ar.bf16(P_HC, 10240)
            Av = ar.f32(P_A, 10240)
            Bv = ar.f32(P_B, 10240)
            Cv = ar.f32(P_C, 10240)
            HLv = ar.bf16(P_C, 10240)
            ACv = ar.bf16(P_C + 5120, 10240)
            car_hl = ar.bf16(P_CAR, 40)
            car_ac = ar.bf16(P_CAR + 24, 40)

            ew_pe = []
            for tau in range(4):
                deps = [(sHS, conv_st)]
                if tau >= 1:
                    deps.append((sDV, gates_dv))
                    deps.append((sGP, gates_gp))
                for c in range(20):
                    dma("sync", hcv[:, c * 512:(c + 1) * 512],
                        hc_buf.ap()[c, :, tau * 512:(tau + 1) * 512], sHC,
                        deps=deps if c == 0 else [])
                hc_ld = sHC.n
                epe = []
                eev = []
                for e in range(20):
                    d = e % 2
                    depw = []
                    if e >= 2:
                        depw.append((sPE, epe[e - 2]))
                    v = dma("sync", ar.bf16(P_W[d], 2560).rearrange("p (t c) -> p t c", c=128),
                            attn_tile_ap(WA_BASE + e * 20, 20), sW[d], deps=depw)
                    v = dma("sync", ar.bf16(P_W[d] + 1280, 2560).rearrange("p (t c) -> p t c", c=128),
                            attn_tile_ap(WX_BASE + e * 20, 20), sW[d])
                    plan.wait("tensor", sW[d], v)
                    plan.wait("tensor", sHC, hc_ld)
                    if e >= 2:
                        plan.wait("tensor", sSC, eev[e - 2])
                    bnk_r = pA[:, (e % 2) * 512:(e % 2) * 512 + 512]
                    bnk_i = pA[:, 1024 + (e % 2) * 512:1024 + (e % 2) * 512 + 512]
                    for dc in range(20):
                        w = ar.bf16(P_W[d], 2560)[:, dc * 128:(dc + 1) * 128]
                        comp("tensor", lambda E, o=bnk_r, w_=w, i=hcv[:, dc * 512:(dc + 1) * 512],
                             st=(dc == 0), sp=(dc == 19):
                             E.matmul(o, w_, i, start=st, stop=sp))
                    for dc in range(20):
                        w = ar.bf16(P_W[d] + 1280, 2560)[:, dc * 128:(dc + 1) * 128]
                        last = (dc == 19)
                        comp("tensor", lambda E, o=bnk_i, w_=w, i=hcv[:, dc * 512:(dc + 1) * 512],
                             st=(dc == 0), sp=(dc == 19):
                             E.matmul(o, w_, i, start=st, stop=sp),
                             sem=(sPE if last else None))
                    epe.append(sPE.n)
                    comp("scalar", lambda E, o=Av[:, e * 512:(e + 1) * 512], i=bnk_r, b=vec_col(V_BA + e):
                         E.activation(o, i, AF.Sigmoid, bias=b), deps=[(sPE, epe[e])])
                    comp("scalar", lambda E, o=Bv[:, e * 512:(e + 1) * 512], i=bnk_i, b=vec_col(V_BX + e):
                         E.activation(o, i, AF.Sigmoid, bias=b), sem=sSC)
                    eev.append(sSC.n)
                gates_sc = sSC.n
                for c in range(20):
                    comp("vector", lambda E, o=Av[:, c * 512:(c + 1) * 512], s=ar.f32(A_SP + c, 1):
                         E.tensor_scalar(o, o, s, -8.0, OP.mult, OP.mult),
                         sem=(sDV if c == 19 else None),
                         deps=[(sSC, gates_sc), sp_ready] if c == 0 else [])
                v_loga = sDV.n
                comp("scalar", lambda E: E.activation(Av, Av, AF.Exp), deps=[(sDV, v_loga)])
                comp("scalar", lambda E: E.activation(Cv, Av, AF.Square), sem=sSC)
                v_a = sSC.n
                comp("vector", lambda E: E.tensor_scalar(Cv, Cv, -1.0, 1.0, OP.mult, OP.add),
                     sem=sDV, deps=[(sSC, v_a)])
                comp("scalar", lambda E: E.activation(Cv, Cv, AF.Sqrt), sem=sSC,
                     deps=[(sDV, sDV.n)])
                v_mult = sSC.n
                comp("vector", lambda E: E.tensor_tensor(Bv, Bv, hcv, OP.mult),
                     deps=[(sSC, v_mult)])
                comp("vector", lambda E: E.tensor_tensor(Bv, Bv, Cv, OP.mult), sem=sDV)
                v_b = sDV.n
                plan.wait("vector", sHLS, sHLS.n)
                for c in range(20):
                    if tau == 0:
                        comp("vector", lambda E, o=HLv[:, c * 512:(c + 1) * 512],
                             a=Av[:, c * 512:(c + 1) * 512], b=Bv[:, c * 512:(c + 1) * 512]:
                             E.tensor_tensor_scan(o, a, b, 0.0, OP.mult, OP.add))
                        comp("vector", lambda E, o=ACv[:, c * 512:(c + 1) * 512],
                             a=Av[:, c * 512:(c + 1) * 512]:
                             E.tensor_tensor_scan(o, a, zeros512, 1.0, OP.mult, OP.add))
                    else:
                        comp("vector", lambda E, o=HLv[:, c * 512:(c + 1) * 512],
                             a=Av[:, c * 512:(c + 1) * 512], b=Bv[:, c * 512:(c + 1) * 512],
                             i=car_hl[:, c:c + 1]:
                             E.tensor_tensor_scan(o, a, b, i, OP.mult, OP.add))
                        comp("vector", lambda E, o=ACv[:, c * 512:(c + 1) * 512],
                             a=Av[:, c * 512:(c + 1) * 512], i=car_ac[:, c:c + 1]:
                             E.tensor_tensor_scan(o, a, zeros512, i, OP.mult, OP.add))
                hl_last = HLv.rearrange("p (c t) -> p c t", t=512)[:, :, 511]
                ac_last = ACv.rearrange("p (c t) -> p c t", t=512)[:, :, 511]
                comp("vector", lambda E, o=car_hl[:, 0:20], i=hl_last: E.tensor_copy(o, i))
                comp("vector", lambda E, o=car_ac[:, 0:20], i=ac_last: E.tensor_copy(o, i),
                     sem=sDV)
                v_scan = sDV.n
                v_scan_gp = sGP.n
                for c in range(20):
                    dma("gpsimd", hl_buf.ap()[c, :, tau * 512:(tau + 1) * 512],
                        HLv[:, c * 512:(c + 1) * 512], sHLS,
                        deps=[(sDV, v_scan), (sGP, v_scan_gp)] if c == 0 else [])
                for c in range(20):
                    dma("gpsimd", ac_buf.ap()[c, :, tau * 512:(tau + 1) * 512],
                        ACv[:, c * 512:(c + 1) * 512], sHLS)
                gates_dv = v_scan
                gates_gp = v_scan_gp
            hlac_st = sHLS.n

            E2S = ar.alloc(20)
            e2p = ar.bf16(E2S, 40).rearrange("p (c k) -> p c k", k=2)
            comp("vector", lambda E: E.tensor_copy(e2p[:, :, 0], car_ac[:, 0:20]))
            comp("vector", lambda E: E.tensor_copy(e2p[:, :, 1], car_hl[:, 0:20]), sem=sDV)
            v = dma("gpsimd", e2_in.ap(), ar.bf16(E2S, 40), sE2S, deps=[(sDV, sDV.n)])
            plan.wait("gpsimd", sE2S, v)
            plan.ins("gpsimd", lambda E: E.collective_compute(
                "AllGather", OP.bypass, replica_groups=[list(range(8))],
                ins=[e2_in.ap().opt()], outs=[e2_out.ap().opt()]), inc=sCC)
            e2_done = sCC.n
            ar.cur = A0
            E2G = ar.alloc(8 * 20)
            for j in range(8):
                dma("sync", ar.bf16(E2G + j * 20, 40), e2_out.ap()[j * 128:(j + 1) * 128, :],
                    sE2L, deps=[(sCC, e2_done), (sHLS, hlac_st)] if j == 0 else [])
            e2g_ld = sE2L.n
            CFOLD = ar.alloc(9 * 20)
            HIN = ar.alloc(20)
            h_in = ar.f32(HIN, 20)
            comp("vector", lambda E: E.memset(ar.f32(CFOLD, 20), 0.0), deps=[(sE2L, e2g_ld)])
            comp("vector", lambda E: E.memset(h_in, 0.0))
            for j in range(7):
                g = ar.bf16(E2G + j * 20, 40).rearrange("p (c k) -> p c k", k=2)
                cj = ar.f32(CFOLD + j * 20, 20)
                cj1 = ar.f32(CFOLD + (j + 1) * 20, 20)
                comp("vector", lambda E, o=cj1, a=g[:, :, 0], c_=cj: E.tensor_tensor(o, a, c_, OP.mult))
                comp("vector", lambda E, o=cj1, b=g[:, :, 1]: E.tensor_tensor(o, o, b, OP.add))
                comp("vector", lambda E, o=cj1, s=ar.f32(A_MSK + j + 1, 1):
                     E.tensor_scalar(o, o, s, None, OP.mult))
            for j in range(1, 8):
                cj = ar.f32(CFOLD + j * 20, 20)
                comp("vector", lambda E, o=h_in, c_=cj, s=ar.f32(A_MSK + 8 + j, 1):
                     E.scalar_tensor_tensor(o, c_, s, o, OP.mult, OP.add),
                     sem=(sDV if j == 7 else None))
            hin_done = sDV.n

            cut(7)
            P3_HL = [ar.alloc(5120), ar.alloc(5120)]
            P3_AC = [ar.alloc(5120), ar.alloc(5120)]
            P3_HC = [ar.alloc(5120), ar.alloc(5120)]
            P3_W = [ar.alloc(1280), ar.alloc(1280)]
            P3_X = [ar.alloc(512), ar.alloc(512)]
            P3_O = [ar.alloc(512), ar.alloc(512)]
            dst_marks = []
            p3_dv = 0
            x2_st = 0
            p3_pe_marks = []
            for tau in range(4):
                tp = tau % 2
                hl_t = ar.bf16(P3_HL[tp], 10240)
                ac_t = ar.bf16(P3_AC[tp], 10240)
                hcor = ar.bf16(P3_HC[tp], 10240)
                deps = [(sHLS, hlac_st), (sDV, hin_done)]
                if tau >= 2:
                    deps.append((sPE, p3_pe_marks[tau - 2]))
                for c in range(20):
                    dma("sync", hl_t[:, c * 512:(c + 1) * 512],
                        hl_buf.ap()[c, :, tau * 512:(tau + 1) * 512], sHL,
                        deps=deps if c == 0 else [])
                for c in range(20):
                    dma("sync", ac_t[:, c * 512:(c + 1) * 512],
                        ac_buf.ap()[c, :, tau * 512:(tau + 1) * 512], sHL)
                v_ld = sHL.n
                for c in range(20):
                    comp("vector", lambda E, o=hcor[:, c * 512:(c + 1) * 512],
                         a=ac_t[:, c * 512:(c + 1) * 512], b=hl_t[:, c * 512:(c + 1) * 512],
                         s=ar.f32(HIN + c, 1):
                         E.scalar_tensor_tensor(o, a, s, b, OP.mult, OP.add),
                         sem=(sDV if c == 19 else None),
                         deps=[(sHL, v_ld), (sDV, hin_done)] if c == 0 else [])
                v_cor = sDV.n
                dpe = []
                dev = []
                for dd in range(16):
                    d = dd % 2
                    depw = []
                    if dd >= 2:
                        depw.append((sPE, dpe[dd - 2]))
                    if len(dst_marks) >= 2:
                        depw.append(dst_marks[-2])
                    v = dma("sync", ar.bf16(P3_W[d], 2560).rearrange("p (t c) -> p t c", c=128),
                            attn_tile_ap(W2_BASE + dd * 20, 20), sW[d], deps=depw)
                    vx = dma("sync", ar.f32(P3_X[d], 512),
                             x1_buf.ap()[dd * 128:(dd + 1) * 128, tau * 512:(tau + 1) * 512], sX[d])
                    plan.wait("tensor", sW[d], v)
                    plan.wait("tensor", sDV, v_cor)
                    if dd >= 4:
                        plan.wait("tensor", sDV, dev[dd - 4])
                    bank = [pA[:, 0:512], pA[:, 512:1024], pB[:, 0:512], pB[:, 512:1024]][dd % 4]
                    for rc_ in range(20):
                        w = ar.bf16(P3_W[d], 2560)[:, rc_ * 128:(rc_ + 1) * 128]
                        last = (rc_ == 19)
                        comp("tensor", lambda E, o=bank, w_=w, i=hcor[:, rc_ * 512:(rc_ + 1) * 512],
                             st=(rc_ == 0), sp=(rc_ == 19):
                             E.matmul(o, w_, i, start=st, stop=sp),
                             sem=(sPE if last else None))
                    dpe.append(sPE.n)
                    comp("vector", lambda E, o=ar.f32(P3_O[d], 512), p=bank,
                         b=vec_col(V_B2 + dd), x=ar.f32(P3_X[d], 512):
                         E.scalar_tensor_tensor(o, p, b, x, OP.add, OP.add),
                         sem=sDV, deps=[(sPE, dpe[dd]), (sX[d], vx)])
                    dev.append(sDV.n)
                    dma("gpsimd", x2_buf.ap()[dd * 128:(dd + 1) * 128, tau * 512:(tau + 1) * 512],
                        ar.f32(P3_O[d], 512), sO[d], deps=[(sDV, dev[dd])])
                    dst_marks.append((sO[d], sO[d].n))
                p3_dv = sDV.n
                p3_pe_marks.append(sPE.n)
            x2_st = [(sO0, sO0.n), (sO1, sO1.n)]

            cut(8)
            for s_, v_ in x2_st:
                plan.wait("sync", s_, v_)
            out_st, _ = emit_ffn(x2_buf.ap(), d_out, False)

        try:
            _emit_all()
        except _Cut:
            pass
        for nm_ in SEM_NAMES:
            plan.wait("sync", sems[nm_], sems[nm_].n)
        if os.environ.get("ATH_DBG"):
            dbgs = [("dbg_x1", x1_buf, [D, T], F32),
                    ("dbg_hc", hc_buf, [RC, 128, T], BF16),
                    ("dbg_hl", hl_buf, [RC, 128, T], BF16),
                    ("dbg_ac", ac_buf, [RC, 128, T], BF16),
                    ("dbg_x2", x2_buf, [D, T], F32),
                    ("dbg_ccB", ccB_out, [16384, 2048], BF16),
                    ("dbg_attnbf", attn_bf, [3040 * TILE], BF16),
                    ("dbg_inter", inter_buf, [H // 128, 128, T], BF16)]
            for nm_, buf_, shp_, dt_ in dbgs:
                o_ = nc.dram_tensor(nm_, shp_, dt_, kind="ExternalOutput").ap()
                sST.inc(16)
                plan.emit("sync", lambda E, a=o_, b=buf_.ap(): E.dma_start(out=a, in_=b).then_inc(sST.h, 16))
            o_ar = nc.dram_tensor("dbg_arena", [128, 48000], F32, kind="ExternalOutput").ap()
            sST.inc(16)
            plan.emit("sync", lambda E, a=o_ar: E.dma_start(out=a, in_=ar_t[:, :]).then_inc(sST.h, 16))
            plan.wait("sync", sST, sST.n)


        with nc.Block() as block:
            @block.sync
            def _(E):
                for fn in plan.q["sync"]:
                    fn(E)

            @block.gpsimd
            def _(E):
                for fn in plan.q["gpsimd"]:
                    fn(E)

            @block.vector
            def _(E):
                for fn in plan.q["vector"]:
                    fn(E)

            @block.scalar
            def _(E):
                for fn in plan.q["scalar"]:
                    fn(E)

            @block.tensor
            def _(E):
                for fn in plan.q["tensor"]:
                    fn(E)

    return nc


_CACHE = {}


def kernel(**inputs):
    x = np.asarray(inputs["L0_attn_x"], np.float32)
    w1 = np.asarray(inputs["L0_attn_w1"], np.float32)
    b1 = np.asarray(inputs["L0_attn_b1"], np.float32)
    conv_w = np.asarray(inputs["L0_attn_conv_w"], np.float32)
    conv_b = np.asarray(inputs["L0_attn_conv_b"], np.float32)
    wa = np.asarray(inputs["L0_attn_wa"], np.float32)
    ba = np.asarray(inputs["L0_attn_ba"], np.float32)
    wx = np.asarray(inputs["L0_attn_wx"], np.float32)
    bx = np.asarray(inputs["L0_attn_bx"], np.float32)
    lam = np.asarray(inputs["L0_attn_lam"], np.float32)
    w2 = np.asarray(inputs["L0_attn_w2"], np.float32)
    b2 = np.asarray(inputs["L0_attn_b2"], np.float32)
    gpk = np.ascontiguousarray(np.asarray(inputs["L0_ffn_gate_W_packed"], np.int32))
    gsc = np.ascontiguousarray(np.asarray(inputs["L0_ffn_gate_scales"], np.float32))
    upk = np.ascontiguousarray(np.asarray(inputs["L0_ffn_up_W_packed"], np.int32))
    usc = np.ascontiguousarray(np.asarray(inputs["L0_ffn_up_scales"], np.float32))
    dpk = np.ascontiguousarray(np.asarray(inputs["L0_ffn_down_W_packed"], np.int32))
    dsc = np.ascontiguousarray(np.asarray(inputs["L0_ffn_down_scales"], np.float32))

    if "nc" not in _CACHE:
        _CACHE["nc"] = build_program()
    nc = _CACHE["nc"]

    w1T = np.ascontiguousarray(w1.T)
    convT = np.ascontiguousarray(conv_w.transpose(2, 1, 0))
    waT = np.ascontiguousarray(wa.T)
    wxT = np.ascontiguousarray(wx.T)
    w2T = np.ascontiguousarray(w2.T)

    def chunkify(v, n):
        return np.ascontiguousarray(v.reshape(n, 128).T)

    vecs = np.zeros((128, 128), np.float32)
    vecs[:, 0:20] = chunkify(b1, 20)
    vecs[:, 20:40] = chunkify(conv_b, 20)
    vecs[:, 40:60] = chunkify(ba, 20)
    vecs[:, 60:80] = chunkify(bx, 20)
    vecs[:, 80:100] = chunkify(lam, 20)
    vecs[:, 100:116] = chunkify(b2, 16)
    consts = np.array([GC3, GC1, 0.5, 7.5, -4.0, -6.0, -1.0, 1.0], np.float32)
    vecs[:, 116:124] = np.tile(consts[None, :], (128, 1))

    ones = np.zeros((128, 130), BF)
    ones[:, 0] = BF(1.0)
    ones[0, 1:129] = BF(1.0)
    ident = np.eye(128, dtype=BF)

    in_maps = []
    for c in range(8):
        b = c // 4
        t0 = 2048 * (c % 4)
        msk = np.zeros((128, 32), np.float32)
        for j in range(8):
            msk[:, j] = 0.0 if j in (0, 4) else 1.0
        msk[:, 8 + c] = 1.0
        if c not in (0, 4):
            msk[:, 16 + (c - 1)] = 1.0
        if c not in (3, 7):
            msk[:, 24 + (c + 1)] = 1.0
        im = {
            "x_fm": np.ascontiguousarray(x[b, t0:t0 + T, :].T),
            "w1T": w1T, "convT": convT, "waT": waT, "wxT": wxT, "w2T": w2T,
            "vecs": vecs, "onesb": ones, "ident": ident, "msk": msk,
            "gpk": np.ascontiguousarray(gpk[c * GSL:(c + 1) * GSL]),
            "gsc": np.ascontiguousarray(gsc[c * GSL:(c + 1) * GSL]),
            "upk": np.ascontiguousarray(upk[c * GSL:(c + 1) * GSL]),
            "usc": np.ascontiguousarray(usc[c * GSL:(c + 1) * GSL]),
            "dpk": np.ascontiguousarray(dpk[c * DSL:(c + 1) * DSL]),
            "dsc": np.ascontiguousarray(dsc[c * DSL:(c + 1) * DSL]),
        }
        in_maps.append(im)

    res = run_bass_kernel_spmd(nc, in_maps, core_ids=list(range(8)))

    out = np.empty((2, 8192, 2048), np.float32)
    for c in range(8):
        b = c // 4
        t0 = 2048 * (c % 4)
        out[b, t0:t0 + T, :] = np.asarray(res.results[c]["out_fm"]).T
    return out


# version 14 (final)
# speedup vs baseline: 1.1243x; 1.0106x over previous
import contextlib
import os
import numpy as np
import ml_dtypes
import concourse.bass as bass
import concourse.mybir as mybir
from concourse.bass_utils import run_bass_kernel_spmd

F32 = mybir.dt.float32
BF16 = mybir.dt.bfloat16
I32 = mybir.dt.int32
AF = mybir.ActivationFunctionType
OP = mybir.AluOpType
BF = ml_dtypes.bfloat16

D = 2048
R = 2560
H = 16384
T = 2048
DC, RC, HC = D // 128, R // 128, H // 128
NTAU = 4
KB_G = D // 2
GSL = H // 8
DSL = D // 8
TILE = 128 * 128

GC1 = 0.7978845608028654
GC3 = 0.044715 * GC1
LN_EPS = 1e-5

ENGS = ("sync", "gpsimd", "vector", "scalar", "tensor")


class Sem:
    def __init__(self, name):
        self.name = name
        self.h = None
        self.n = 0

    def inc(self, k=1):
        self.n += k
        return self.n


class Plan:
    def __init__(self):
        self.q = {e: [] for e in ENGS}
        self.lastw = {e: {} for e in ENGS}

    def emit(self, eng, fn):
        self.q[eng].append(fn)

    def wait(self, eng, sem, val=None):
        v = sem.n if val is None else val
        if v <= 0:
            return
        if self.lastw[eng].get(sem.name, 0) >= v:
            return
        self.lastw[eng][sem.name] = v
        self.q[eng].append(lambda E, s=sem, vv=v: E.wait_ge(s.h, vv))

    def ins(self, eng, build, inc=None, inck=1):
        if inc is None:
            self.emit(eng, build)
        else:
            inc.inc(inck)
            self.emit(eng, lambda E, b=build, s=inc, k=inck: b(E).then_inc(s.h, k))


class Arena:

    def __init__(self, handle, base, cap):
        self.t = handle
        self.base = base
        self.cap = cap
        self.cur = base

    def reset(self):
        self.cur = self.base

    def alloc(self, n_f32):
        n_f32 = (n_f32 + 1) & ~1
        off = self.cur
        self.cur += n_f32
        assert self.cur <= self.cap, f"arena overflow: {self.cur} > {self.cap}"
        return off

    def f32(self, off, cols, parts=(0, 128)):
        return self.t[parts[0]:parts[1], off:off + cols]

    def bf16(self, off, cols, parts=(0, 128)):
        assert cols % 2 == 0
        return self.t[parts[0]:parts[1], off:off + cols // 2].bitcast(BF16)

    def i32(self, off, cols, parts=(0, 128)):
        return self.t[parts[0]:parts[1], off:off + cols].bitcast(I32)


def ceil_div(a, b):
    return (a + b - 1) // b



class _Cut(Exception):
    pass


def build_program():
    PH = int(os.environ.get("ATH_PH", "9"))

    def cut(n):
        if PH < n:
            raise _Cut()

    nc = bass.Bass("TRN2", target_bir_lowering=False, debug=False)

    def ein(name, shape, dt):
        return nc.dram_tensor(name, shape, dt, kind="ExternalInput").ap()

    d_x = ein("x_fm", [D, T], F32)
    d_w1T = ein("w1T", [D, R], F32)
    d_convT = ein("convT", [4, R, R], F32)
    d_waT = ein("waT", [R, R], F32)
    d_wxT = ein("wxT", [R, R], F32)
    d_w2T = ein("w2T", [R, D], F32)
    d_vecs = ein("vecs", [128, 128], F32)
    d_ones = ein("onesb", [128, 130], BF16)
    d_ident = ein("ident", [128, 128], BF16)
    d_gpk = ein("gpk", [GSL, KB_G], I32)
    d_gsc = ein("gsc", [GSL, D // 32], F32)
    d_upk = ein("upk", [GSL, KB_G], I32)
    d_usc = ein("usc", [GSL, D // 32], F32)
    d_dpk = ein("dpk", [DSL, H // 2], I32)
    d_dsc = ein("dsc", [DSL, H // 32], F32)

    d_out = nc.dram_tensor("out_fm", [D, T], F32, kind="ExternalOutput").ap()

    ccA_in = [nc.dram_tensor(f"ccA_in{q}", [1024, 2048], BF16) for q in range(4)]
    ccA_out = [nc.dram_tensor(f"ccA_out{q}", [8192, 2048], BF16) for q in range(4)]
    ccB_in = nc.dram_tensor("ccB_in", [2048, 2048], BF16)
    ccB_out = nc.dram_tensor("ccB_out", [16384, 2048], BF16)
    attn_bf = nc.dram_tensor("attn_bf", [3040 * TILE], BF16)
    x1_buf = nc.dram_tensor("x1_buf", [D, T], F32)
    x2_buf = nc.dram_tensor("x2_buf", [D, T], F32)
    inter_buf = nc.dram_tensor("inter_buf", [H // 128, 128, T], BF16)
    hc_buf = nc.dram_tensor("hc_buf", [RC, 128, T], BF16)
    hl_buf = nc.dram_tensor("hl_buf", [RC, 128, T], BF16)
    ac_buf = nc.dram_tensor("ac_buf", [RC, 128, T], BF16)
    e1_in = nc.dram_tensor("e1_in", [128, 160], BF16)
    e1_out = nc.dram_tensor("e1_out", [1024, 160], BF16)
    e2_in = nc.dram_tensor("e2_in", [128, 40], BF16)
    e2_out = nc.dram_tensor("e2_out", [1024, 40], BF16)
    dbg_gu = (nc.dram_tensor("dbg_gu", [2, 128, 2048], F32, kind="ExternalOutput")
              if os.environ.get("ATH_DBG") else None)

    W1_BASE = 0
    CV_BASE = W1_BASE + 20 * 16
    WA_BASE = CV_BASE + 20 * 80
    WX_BASE = WA_BASE + 400
    W2_BASE = WX_BASE + 400
    N_TILES = W2_BASE + 320
    assert N_TILES == 3040

    def attn_tile_ap(tile_idx, ntiles=1):
        a = attn_bf.ap()[tile_idx * TILE:(tile_idx + ntiles) * TILE]
        return a.rearrange("(t p c) -> p t c", p=128, c=128)

    es = contextlib.ExitStack()
    with es:
        ar_t = es.enter_context(nc.sbuf_tensor("arena", [128, 48000], F32))
        pA = es.enter_context(nc.psum_tensor("pA", [128, 2048], F32))
        pB = es.enter_context(nc.psum_tensor("pB", [128, 2048], F32))

        sems = {}
        SEM_NAMES = ("sLD", "sGP", "sPE", "sDV", "sSC", "sCC", "sST",
                     "sQ0", "sQ1", "sSa", "sSb", "sSc", "sSd", "sS5a", "sS5b",
                     "sR0", "sR1", "sW0", "sW1", "sI0", "sI1", "sDX", "sDS",
                     "sT0", "sT1", "sHC", "sHS", "sHLS", "sHL",
                     "sX0", "sX1", "sO0", "sO1", "sE1L", "sE1S", "sE2L", "sE2S")
        for nm in SEM_NAMES:
            sems[nm] = Sem(nm)
            sems[nm].h = es.enter_context(nc.semaphore(nm))
        (sLD, sGP, sPE, sDV, sSC, sCC, sST,
         sQ0, sQ1, sSa, sSb, sSc, sSd, sS5a, sS5b,
         sR0, sR1, sW0, sW1, sI0, sI1, sDX, sDS,
         sT0, sT1, sHC, sHS, sHLS, sHL,
         sX0, sX1, sO0, sO1, sE1L, sE1S, sE2L, sE2S) = (sems[n] for n in SEM_NAMES)
        sQ = [sQ0, sQ1]
        sR = [sR0, sR1]
        sW = [sW0, sW1]
        sI = [sI0, sI1]
        sT = [sT0, sT1]
        sX = [sX0, sX1]
        sO = [sO0, sO1]
        sS5 = [sS5a, sS5b]

        V_B1, V_CB, V_BA, V_BX, V_LAM, V_B2 = 0, 20, 40, 60, 80, 100

        d_msk = ein("msk", [128, 32], F32)

        CONST_F32 = 0
        ar = Arena(ar_t, 0, 48000)
        A_VEC = ar.alloc(128)
        A_MSK = ar.alloc(32)
        A_ONE = ar.alloc(66)
        A_ID = ar.alloc(64)
        A_ZER = ar.alloc(512)
        A_SP = ar.alloc(20)
        A0 = ar.cur

        def vec_col(c0, n=1):
            return ar.f32(A_VEC + c0, n)

        ones_col = ar.bf16(A_ONE, 2)[:, 0:1]
        ones_row = ar.bf16(A_ONE, 130)[0:1, 1:129]
        ident = ar.bf16(A_ID, 128)
        zeros512 = ar.f32(A_ZER, 512)

        CONSTS = [GC3, GC1, 0.5, 7.5, -4.0, -6.0, -1.0, 1.0]
        for i, v in enumerate(CONSTS):
            nc.const_aps.register(F32, v, ar_t[:, A_VEC + 116 + i:A_VEC + 117 + i])

        plan = Plan()

        def dma(qeng, out_ap, in_ap, sem, deps=()):
            for (s, v) in deps:
                plan.wait(qeng, s, v)
            plan.ins(qeng, lambda E, o=out_ap, i=in_ap: E.dma_start(out=o, in_=i),
                     inc=sem, inck=16)
            return sem.n

        def comp(eng, build, sem=None, deps=()):
            for (s, v) in deps:
                plan.wait(eng, s, v)
            plan.ins(eng, build, inc=sem)
            return None if sem is None else sem.n

        def _emit_all():
            n0 = dma("sync", ar.f32(A_VEC, 128), d_vecs, sLD)
            dma("sync", ar.f32(A_MSK, 32), d_msk, sLD)
            dma("sync", ar.bf16(A_ONE, 130), d_ones, sLD)
            dma("sync", ar.bf16(A_ID, 128), d_ident, sLD)
            ld_consts = sLD.n
            comp("vector", lambda E: E.memset(zeros512, 0.0), sem=sDV)

            SP_T1 = A0
            lam = ar.f32(A_VEC + V_LAM, 20)
            t1 = ar.f32(SP_T1, 20)
            t2 = ar.f32(SP_T1 + 20, 20)
            comp("scalar", lambda E: E.activation(t1, lam, AF.Abs), deps=[(sLD, ld_consts)])
            comp("scalar", lambda E: E.activation(t1, t1, AF.Exp, scale=-1.0))
            comp("scalar", lambda E: E.activation(t1, t1, AF.Ln, bias=1.0), sem=sSC)
            comp("vector", lambda E: E.tensor_scalar_max(t2, lam, 0.0),
                 deps=[(sLD, ld_consts)])
            comp("vector", lambda E: E.tensor_add(ar.f32(A_SP, 20), t2, t1),
                 sem=sDV, deps=[(sSC, sSC.n)])
            sp_ready = (sDV, sDV.n)

            cut(1)
            ar.cur = A0 + 64
            DQ = []
            for par in range(2):
                d = {}
                d["pk"] = ar.alloc(1024)
                d["lo"] = ar.alloc(1024)
                d["hi"] = ar.alloc(1024)
                d["c"] = ar.alloc(2048)
                d["s"] = ar.alloc(2048)
                d["sc"] = ar.alloc(64)
                d["t"] = ar.alloc(2048)
                d["r4"] = ar.alloc(2048)
                d["r6"] = ar.alloc(2048)
                d["m"] = ar.alloc(2048)
                d["dq"] = ar.alloc(1024)
                d["st"] = ar.alloc(1024)
                DQ.append(d)
            assert ar.cur < 48000

            jobs = []
            for q in range(4):
                for i in range(4 * q, 4 * q + 4):
                    jobs.append(("g", d_gpk, d_gsc, i * 128, 0))
                    jobs.append(("u", d_upk, d_usc, i * 128, 0))
            for rr in range(2):
                for j in range(8):
                    jobs.append(("d", d_dpk, d_dsc, rr * 128, j * 1024))

            job_ld = []
            job_done = []
            ccA_done = []

            for ji, (kind, dpk_, dsc_, row0, kb0) in enumerate(jobs):
                d = DQ[ji % 2]
                deps = []
                if ji >= 2:
                    deps = [job_done[ji - 2]]
                qsem = sQ[ji % 2]
                v = dma("sync", ar.i32(d["pk"], 1024), dpk_[row0:row0 + 128, kb0:kb0 + 1024],
                        qsem, deps=deps)
                v = dma("sync", ar.f32(d["sc"], 64),
                        dsc_[row0:row0 + 128, kb0 // 16:kb0 // 16 + 64], qsem)
                job_ld.append((qsem, v))

                pk = ar.i32(d["pk"], 1024)
                lo = ar.i32(d["lo"], 1024)
                hi = ar.i32(d["hi"], 1024)
                cc = ar.f32(d["c"], 2048)
                ss = ar.f32(d["s"], 2048)
                tt_ = ar.f32(d["t"], 2048)
                r4 = ar.f32(d["r4"], 2048)
                r6 = ar.f32(d["r6"], 2048)
                mm_ = ar.f32(d["m"], 2048)
                dq = ar.bf16(d["dq"], 2048)
                stg = ar.bf16(d["st"], 2048)

                ldv = job_ld[ji]
                comp("vector", lambda E, a=lo, b=pk: E.tensor_single_scalar(a, b, 15, OP.bitwise_and), deps=[ldv])
                comp("vector", lambda E, a=hi, b=pk: E.tensor_single_scalar(a, b, 4, OP.logical_shift_right), sem=sDV)
                c3 = cc.rearrange("p (k two) -> p k two", two=2)
                comp("vector", lambda E, a=c3[:, :, 0], b=lo: E.tensor_copy(a, b))
                comp("vector", lambda E, a=c3[:, :, 1], b=hi: E.tensor_copy(a, b), sem=sDV)
                v_c = sDV.n
                comp("scalar", lambda E, a=ss, b=cc: E.activation(a, b, AF.Sign, bias=7.5, scale=-1.0),
                     sem=sSC, deps=[(sDV, v_c)])
                v_sgn = sSC.n
                comp("vector", lambda E, a=tt_, b=ss, c_=cc: E.scalar_tensor_tensor(a, b, 4.0, c_, OP.mult, OP.add),
                     deps=[(sSC, v_sgn)])
                comp("vector", lambda E, a=tt_: E.tensor_scalar(a, a, -4.0, None, OP.add), sem=sDV)
                v_t = sDV.n
                comp("scalar", lambda E, a=r4, b=tt_: E.activation(a, b, AF.Relu, bias=-4.0),
                     deps=[(sDV, v_t)])
                comp("scalar", lambda E, a=r6, b=tt_: E.activation(a, b, AF.Relu, bias=-6.0), sem=sSC)
                v_r = sSC.n
                comp("vector", lambda E, a=r4, b=tt_: E.tensor_add(a, a, b), deps=[(sSC, v_r)])
                comp("vector", lambda E, a=mm_, b=r4, c_=r6: E.scalar_tensor_tensor(a, b, 0.5, c_, OP.mult, OP.add))
                comp("vector", lambda E, a=mm_, b=ss: E.tensor_mul(a, a, b))
                scb = ar.f32(d["sc"], 64).unsqueeze(2).broadcast_to([128, 64, 32])
                m3 = mm_.rearrange("p (blk j) -> p blk j", j=32)
                d3 = dq.rearrange("p (blk j) -> p blk j", j=32)
                comp("vector", lambda E, a=d3, b=m3, c_=scb: E.tensor_tensor(a, b, c_, OP.mult), sem=sDV)
                v_dq = sDV.n

                for kc in range(16):
                    bank = kc % 4
                    pt = pA[:, bank * 512:bank * 512 + 64].bitcast(BF16)[:, 0:128]
                    depsT = [(sDV, v_dq)]
                    if kc >= 4:
                        depsT.append((sDV, sDV.n))
                    comp("tensor", lambda E, o=pt, i=dq[:, kc * 128:(kc + 1) * 128]:
                         E.transpose(o, i, ident), sem=sPE, deps=depsT)
                    v_tr = sPE.n
                    comp("vector", lambda E, o=stg[:, kc * 128:(kc + 1) * 128], i=pt:
                         E.tensor_copy(o, i), sem=sDV, deps=[(sPE, v_tr)])
                v_stg = sDV.n

                if kind in ("g", "u"):
                    i_ch = row0 // 128
                    q = i_ch // 4
                    j = i_ch % 4
                    off = ((0 if kind == "g" else 4) + j) * 16 * TILE
                    dst = ccA_in[q].ap().rearrange("a b -> (a b)")[off:off + 16 * TILE]
                    dst = dst.rearrange("(t p c) -> p t c", p=128, c=128)
                else:
                    kc0 = kb0 // 64
                    dh = row0 // 128
                    base = ccB_in.ap().rearrange("a b -> (a b)")
                    dst = base[(kc0 * 2 + dh) * TILE:(kc0 * 2 + dh) * TILE + 15 * 2 * TILE + TILE]
                    dst = dst.rearrange("(t x) -> t x", x=TILE)[::2]
                    dst = dst.rearrange("t (p c) -> p t c", p=128, c=128)
                src = stg.rearrange("p (t c) -> p t c", c=128)
                if ji < 32:
                    ssem = [sSa, sSb][ji % 2]
                else:
                    ssem = [sSc, sSd][ji % 2]
                v = dma("gpsimd", dst, src, ssem, deps=[(sDV, v_stg)])
                job_done.append((ssem, v))
                if ji < 32 and ji % 8 == 7:
                    qq = ji // 8
                    plan.wait("gpsimd", job_done[ji][0], job_done[ji][1])
                    plan.wait("gpsimd", job_done[ji - 1][0], job_done[ji - 1][1])
                    plan.ins("gpsimd", lambda E, q_=qq: E.collective_compute(
                        "AllGather", OP.bypass, replica_groups=[list(range(8))],
                        ins=[ccA_in[q_].ap().opt()], outs=[ccA_out[q_].ap().opt()]),
                        inc=sCC)
                    ccA_done.append(sCC.n)

            plan.wait("gpsimd", sSc, sSc.n)
            plan.wait("gpsimd", sSd, sSd.n)
            plan.ins("gpsimd", lambda E: E.collective_compute(
                "AllGather", OP.bypass, replica_groups=[list(range(8))],
                ins=[ccB_in.ap().opt()], outs=[ccB_out.ap().opt()]), inc=sCC)
            ccB_done = sCC.n

            cut(2)
            P05 = {"ji": 0}
            CW_BASE = 48000 - 2 * 3842
            CW = []
            for par in range(2):
                b = CW_BASE + par * 3842
                CW.append({"src": b, "dst": b + 2560})

            cast_jobs = []
            for kc in range(16):
                cast_jobs.append((d_w1T[kc * 128:(kc + 1) * 128, :], 20,
                                  [W1_BASE + r * 16 + kc for r in range(20)]))
            for tap in range(4):
                for ic in range(20):
                    cast_jobs.append((d_convT[tap, ic * 128:(ic + 1) * 128, :], 20,
                                      [CV_BASE + (o * 4 + tap) * 20 + ic for o in range(20)]))
            for dc in range(20):
                cast_jobs.append((d_waT[dc * 128:(dc + 1) * 128, :], 20,
                                  [WA_BASE + e * 20 + dc for e in range(20)]))
            for dc in range(20):
                cast_jobs.append((d_wxT[dc * 128:(dc + 1) * 128, :], 20,
                                  [WX_BASE + e * 20 + dc for e in range(20)]))
            for rc_ in range(20):
                cast_jobs.append((d_w2T[rc_ * 128:(rc_ + 1) * 128, :], 16,
                                  [W2_BASE + dd * 20 + rc_ for dd in range(16)]))
            cj_done = []

            def emit_p05_job():
                ji = P05["ji"]
                if ji >= len(cast_jobs):
                    return False
                P05["ji"] = ji + 1
                src_ap, ncol, tidx = cast_jobs[ji]
                d = CW[ji % 2]
                deps = [cj_done[ji - 2]] if ji >= 2 else []
                v = dma("sync", ar.f32(d["src"], ncol * 128), src_ap, sQ[ji % 2], deps=deps)
                comp("vector", lambda E, o=ar.bf16(d["dst"], ncol * 128),
                     i=ar.f32(d["src"], ncol * 128):
                     E.tensor_copy(o, i), sem=sDV, deps=[(sQ[ji % 2], v)])
                v_c = sDV.n
                stride = (tidx[1] - tidx[0]) * TILE if ncol > 1 else TILE
                base = attn_bf.ap()[tidx[0] * TILE: tidx[0] * TILE + (ncol - 1) * stride + TILE]
                dst = base.rearrange("(t x) -> t x", x=TILE)[::stride // TILE]
                dst = dst.rearrange("t (p c) -> p t c", p=128, c=128)
                srcv = ar.bf16(d["dst"], ncol * 128).rearrange("p (t c) -> p t c", c=128)
                v = dma("gpsimd", dst, srcv, sS5[ji % 2], deps=[(sDV, v_c)])
                cj_done.append((sS5[ji % 2], v))
                return True

            def emit_p05_rest():
                while emit_p05_job():
                    pass
                P05['done'] = [(sS5a, sS5a.n), (sS5b, sS5b.n)]


            def emit_layernorm(src_dram, A_LN_, sc, extra_deps=(), chunk_deps=None):
                plan.wait("tensor", sDV, sDV.n)
                plan.wait("tensor", sSC, sSC.n)
                roll = [sc["roll"] + i * 2048 for i in range(2)]
                sqb = [sc["sqb"] + i * 1024 for i in range(2)]
                xb = [sc["xb"] + i * 1024 for i in range(2)]
                r_mean = ar.f32(sc["rows"], 2048, (0, 1))
                r_var = ar.f32(sc["rows"] + 2048, 2048, (0, 1))
                r_t = ar.f32(sc["rows"] + 4096, 2048, (0, 1))
                hv = sc["rows"] + 6144
                rstd_hi = ar.bf16(hv, 2048, (0, 1))
                rstd_lo = ar.bf16(hv + 1024, 2048, (0, 1))
                mr_hi = ar.bf16(hv + 2048, 2048, (0, 1))
                mr_lo = ar.bf16(hv + 3072, 2048, (0, 1))

                pe_marks = []
                for c in range(16):
                    d = c % 2
                    deps = list(extra_deps) if c == 0 else []
                    if chunk_deps is not None:
                        deps.append(chunk_deps(c))
                    if c >= 2:
                        deps.append((sPE, pe_marks[c - 2]))
                    v = dma("sync", ar.f32(roll[d], 2048), src_dram[c * 128:(c + 1) * 128, :],
                            sR[d], deps=deps)
                    comp("scalar", lambda E, o=ar.bf16(sqb[d], 2048), i=ar.f32(roll[d], 2048):
                         E.activation(o, i, AF.Square), deps=[(sR[d], v)])
                    comp("scalar", lambda E, o=ar.bf16(xb[d], 2048), i=ar.f32(roll[d], 2048):
                         E.activation(o, i, AF.Copy), sem=sSC)
                    v_s = sSC.n
                    plan.wait("tensor", sSC, v_s)
                    for tau in range(4):
                        comp("tensor", lambda E, o=pA[0:1, tau * 512:(tau + 1) * 512],
                             i=ar.bf16(xb[d], 2048)[:, tau * 512:(tau + 1) * 512],
                             st=(c == 0), sp=(c == 15):
                             E.matmul(o, ones_col, i, start=st, stop=sp))
                    for tau in range(4):
                        last = (tau == 3)
                        comp("tensor", lambda E, o=pB[0:1, tau * 512:(tau + 1) * 512],
                             i=ar.bf16(sqb[d], 2048)[:, tau * 512:(tau + 1) * 512],
                             st=(c == 0), sp=(c == 15):
                             E.matmul(o, ones_col, i, start=st, stop=sp),
                             sem=(sPE if last else None))
                    pe_marks.append(sPE.n)
                stats_done = sPE.n

                comp("vector", lambda E: E.tensor_scalar(r_mean, pA[0:1, :], 1.0 / D, None, OP.mult),
                     deps=[(sPE, stats_done)])
                comp("vector", lambda E: E.tensor_scalar(r_var, pB[0:1, :], 1.0 / D, None, OP.mult))
                comp("vector", lambda E: E.scalar_tensor_tensor(r_t, r_mean, -1.0, r_mean, OP.mult, OP.mult))
                comp("vector", lambda E: E.tensor_add(r_var, r_var, r_t))
                comp("vector", lambda E: E.tensor_scalar(r_var, r_var, LN_EPS, None, OP.add), sem=sDV)
                v_rows = sDV.n
                comp("scalar", lambda E: E.activation(r_t, r_var, AF.Ln), sem=sSC, deps=[(sDV, v_rows)])
                comp("vector", lambda E: E.tensor_scalar(r_t, r_t, -0.5, None, OP.mult), sem=sDV,
                     deps=[(sSC, sSC.n)])
                comp("scalar", lambda E: E.activation(r_var, r_t, AF.Exp), deps=[(sDV, sDV.n)])
                comp("scalar", lambda E: E.activation(rstd_hi, r_var, AF.Copy), sem=sSC)
                v_hi1 = sSC.n
                comp("vector", lambda E: E.tensor_mul(r_t, r_mean, r_var), deps=[(sSC, v_hi1)])
                comp("vector", lambda E: E.tensor_tensor(r_mean, r_var, rstd_hi, OP.subtract))
                comp("vector", lambda E: E.tensor_copy(rstd_lo, r_mean), sem=sDV)
                comp("scalar", lambda E: E.activation(mr_hi, r_t, AF.Copy), sem=sSC,
                     deps=[(sDV, sDV.n)])
                comp("vector", lambda E: E.tensor_tensor(r_mean, r_t, mr_hi, OP.subtract),
                     deps=[(sSC, sSC.n)])
                comp("vector", lambda E: E.tensor_copy(mr_lo, r_mean), sem=sDV)
                v_lo = sDV.n

                plan.wait("tensor", sDV, v_lo)
                for tau in range(4):
                    sl = slice(tau * 512, (tau + 1) * 512)
                    comp("tensor", lambda E, o=pA[:, sl], i=rstd_hi[:, sl]:
                         E.matmul(o, ones_row, i, start=True, stop=False))
                    comp("tensor", lambda E, o=pA[:, sl], i=rstd_lo[:, sl]:
                         E.matmul(o, ones_row, i, start=False, stop=True))
                    comp("tensor", lambda E, o=pB[:, sl], i=mr_hi[:, sl]:
                         E.matmul(o, ones_row, i, start=True, stop=False))
                    last = (tau == 3)
                    comp("tensor", lambda E, o=pB[:, sl], i=mr_lo[:, sl]:
                         E.matmul(o, ones_row, i, start=False, stop=True),
                         sem=(sPE if last else None))
                bcast_done = sPE.n

                stats_sc = sSC.n
                ap_marks = []
                for c in range(16):
                    d = c % 2
                    deps = [(sSC, stats_sc)] if c < 2 else [(sDV, ap_marks[c - 2])]
                    v = dma("sync", ar.f32(roll[d], 2048), src_dram[c * 128:(c + 1) * 128, :],
                            sR[d], deps=deps)
                    comp("vector", lambda E, o=ar.f32(sc["t1"], 2048), i=ar.f32(roll[d], 2048):
                         E.tensor_tensor(o, i, pA[:, :], OP.mult),
                         deps=[(sR[d], v), (sPE, bcast_done)])
                    comp("vector", lambda E, o=ar.bf16(A_LN_ + c * 1024, 2048), i=ar.f32(sc["t1"], 2048):
                         E.tensor_tensor(o, i, pB[:, :], OP.subtract), sem=sDV)
                    ap_marks.append(sDV.n)
                return sDV.n

            def emit_ffn(src_dram, dst_dram, first_call, ln_extra=(), mid=None, per_m=None):
                ar.cur = A0
                A_LN_ = ar.alloc(16384)
                sc = {"roll": ar.alloc(4096), "sqb": ar.alloc(2048), "xb": ar.alloc(2048),
                      "rows": ar.alloc(10240)}
                sc["t1"] = sc["rows"]
                ln_done = emit_layernorm(src_dram, A_LN_, sc, extra_deps=ln_extra)

                ar.cur = A0 + 16384
                G_GS = [ar.alloc(2048), ar.alloc(2048)]
                G_US = [ar.alloc(2048), ar.alloc(2048)]
                G_T1 = ar.alloc(2048)
                G_T2 = ar.alloc(2048)
                G_T3 = ar.alloc(2048)
                G_IS = [ar.alloc(1024), ar.alloc(1024)]
                G_WG = [ar.alloc(1024), ar.alloc(1024)]
                G_WU = [ar.alloc(1024), ar.alloc(1024)]

                def gup_src(m, which):
                    r_, loc = m // 16, m % 16
                    q, j = loc // 4, loc % 4
                    off = (r_ * 128 + ((0 if which == "g" else 4) + j) * 16) * TILE
                    a = ccA_out[q].ap().rearrange("a b -> (a b)")[off:off + 16 * TILE]
                    return a.rearrange("(t p c) -> p t c", p=128, c=128)

                def m_order(idx):
                    q, rest = idx // 32, idx % 32
                    r_, j = rest // 4, rest % 4
                    return r_ * 16 + 4 * q + j

                wld = []
                ev_marks = []
                ep_marks = []
                ep_v2 = []
                ep_v4 = []
                first_mm_dep = [(sDV, ln_done)]
                for mi in range(HC):
                    if per_m is not None:
                        per_m(mi)
                    m = m_order(mi)
                    d = mi % 2
                    deps = [(sCC, ccA_done[(m % 16) // 4])]
                    if mi < 2:
                        deps.append((sDV, ln_done))
                    else:
                        deps.append((sPE, ev_marks[mi - 2][2]))
                    v = dma("sync", ar.bf16(G_WG[d], 2048).rearrange("p (t c) -> p t c", c=128),
                            gup_src(m, "g"), sW[d], deps=deps)
                    v = dma("sync", ar.bf16(G_WU[d], 2048).rearrange("p (t c) -> p t c", c=128),
                            gup_src(m, "u"), sW[d])
                    wld.append((sW[d], v))

                    plan.wait("tensor", wld[mi][0], wld[mi][1])
                    if mi == 0:
                        for s_, vv in first_mm_dep:
                            plan.wait("tensor", s_, vv)
                    half_pe = []
                    for h2 in range(2):
                        if mi >= 1:
                            plan.wait("tensor", sSC, ev_marks[mi - 1][h2])
                        for k in range(16):
                            wg = ar.bf16(G_WG[d], 2048)[:, k * 128:(k + 1) * 128]
                            wu = ar.bf16(G_WU[d], 2048)[:, k * 128:(k + 1) * 128]
                            for tau in (2 * h2, 2 * h2 + 1):
                                ln_sl = ar.bf16(A_LN_ + k * 1024, 2048)[:, tau * 512:(tau + 1) * 512]
                                comp("tensor", lambda E, o=pA[:, tau * 512:(tau + 1) * 512], w=wg, i=ln_sl,
                                     st=(k == 0), sp=(k == 15):
                                     E.matmul(o, w, i, start=st, stop=sp))
                            for tau in (2 * h2, 2 * h2 + 1):
                                ln_sl = ar.bf16(A_LN_ + k * 1024, 2048)[:, tau * 512:(tau + 1) * 512]
                                last = (k == 15 and tau == 2 * h2 + 1)
                                comp("tensor", lambda E, o=pB[:, tau * 512:(tau + 1) * 512], w=wu, i=ln_sl,
                                     st=(k == 0), sp=(k == 15):
                                     E.matmul(o, w, i, start=st, stop=sp),
                                     sem=(sPE if last else None))
                        half_pe.append(sPE.n)
                    ev = []
                    for h2 in range(2):
                        sl = slice(h2 * 1024, (h2 + 1) * 1024)
                        evdeps = [(sPE, half_pe[h2])]
                        if h2 == 0 and mi >= 2:
                            evdeps.append((sDV, ep_v4[mi - 2]))
                        comp("scalar", lambda E, o=ar.f32(G_GS[d], 2048)[:, sl], i=pA[:, sl]:
                             E.activation(o, i, AF.Copy), deps=evdeps)
                        comp("scalar", lambda E, o=ar.f32(G_US[d], 2048)[:, sl], i=pB[:, sl]:
                             E.activation(o, i, AF.Copy), sem=sSC)
                        ev.append(sSC.n)
                    ev_marks.append((ev[0], ev[1], half_pe[1]))
                    if dbg_gu is not None and m == 5 and first_call:
                        dma("gpsimd", dbg_gu.ap()[0], ar.f32(G_GS[d], 2048), sGP,
                            deps=[(sSC, ev[1])])
                        dma("gpsimd", dbg_gu.ap()[1], ar.f32(G_US[d], 2048), sGP)

                    g_s = ar.f32(G_GS[d], 2048)
                    u_s = ar.f32(G_US[d], 2048)
                    sq_deps = [(sSC, ev[1])]
                    if mi >= 1:
                        sq_deps.append((sDV, ep_v2[mi - 1]))
                    comp("scalar", lambda E, o=ar.f32(G_T1, 2048), i=g_s:
                         E.activation(o, i, AF.Square), deps=sq_deps)
                    comp("scalar", lambda E, o=ar.f32(G_T1, 2048), i=ar.f32(G_T1, 2048):
                         E.activation(o, i, AF.Copy, bias=GC1, scale=GC3), sem=sSC)
                    v1 = sSC.n
                    if mi >= 2:
                        plan.wait("vector", ep_marks[mi - 2][0], ep_marks[mi - 2][1])
                    comp("vector", lambda E, o=ar.f32(G_T2, 2048), i=ar.f32(G_T1, 2048), g=g_s:
                         E.tensor_tensor(o, i, g, OP.mult), deps=[(sSC, v1)])
                    comp("vector", lambda E, o=ar.f32(G_T3, 2048), g=g_s, u=u_s:
                         E.tensor_tensor(o, g, u, OP.mult), sem=sDV)
                    v2 = sDV.n
                    ep_v2.append(v2)
                    comp("scalar", lambda E, o=ar.f32(G_T2, 2048), i=ar.f32(G_T2, 2048):
                         E.activation(o, i, AF.Tanh), deps=[(sDV, v2)])
                    comp("scalar", lambda E, o=ar.f32(G_T2, 2048), i=ar.f32(G_T2, 2048):
                         E.activation(o, i, AF.Copy, bias=0.5, scale=0.5), sem=sSC)
                    v3 = sSC.n
                    comp("vector", lambda E, o=ar.bf16(G_IS[d], 2048), a=ar.f32(G_T3, 2048),
                         b=ar.f32(G_T2, 2048): E.tensor_tensor(o, a, b, OP.mult),
                         sem=sDV, deps=[(sSC, v3)])
                    v4 = sDV.n
                    ep_v4.append(v4)
                    v = dma("gpsimd", inter_buf.ap()[m], ar.bf16(G_IS[d], 2048), sT[d],
                            deps=[(sDV, v4)])
                    ep_marks.append((sT[d], v))

                inter_all = [(sT0, sT0.n), (sT1, sT1.n)]
                if mid is not None:
                    mid()

                ar.cur = A0 + 16384
                DW_W = [ar.alloc(2048), ar.alloc(2048)]
                DW_I = [ar.alloc(4096), ar.alloc(4096)]
                DW_XB = [ar.alloc(4096), ar.alloc(4096)]

                dn_kg_pe = []
                pass_store = []
                ev_dv = 0
                for dg in range(4):
                    for th in range(2):
                        pidx = dg * 2 + th
                        DW_X = DW_XB[pidx % 2]
                        if pidx < 2:
                            xdeps = list(inter_all)
                        else:
                            xdeps = [pass_store[pidx - 2]]
                        for di in range(4):
                            dglob = dg * 4 + di
                            dma("sync", ar.f32(DW_X + di * 1024, 1024),
                                src_dram[dglob * 128:(dglob + 1) * 128, th * 1024:(th + 1) * 1024],
                                sDX, deps=xdeps if di == 0 else [])
                        x_ld = sDX.n
                        for kg in range(16):
                            d = kg % 2
                            deps = [(sCC, ccB_done)] + inter_all
                            if len(dn_kg_pe) >= 2:
                                deps.append((sPE, dn_kg_pe[-2]))
                            for si in range(2):
                                s_ = 2 * dg + si
                                off = (s_ * 256 + kg * 8 * 2) * TILE
                                a = ccB_out.ap().rearrange("a b -> (a b)")[off:off + 16 * TILE]
                                a = a.rearrange("(t p c) -> p t c", p=128, c=128)
                                dma("sync", ar.bf16(DW_W[d] + si * 1024, 2048)
                                    .rearrange("p (t c) -> p t c", c=128), a, sI[d],
                                    deps=deps if si == 0 else [])
                            iap = inter_buf.ap()[kg * 8:(kg + 1) * 8, :, th * 1024:(th + 1) * 1024]
                            v_ld = dma("sync", ar.bf16(DW_I[d], 8192).rearrange("p (t c) -> p t c", c=1024),
                                       iap.rearrange("t p c -> p t c"), sI[d])
                            plan.wait("tensor", sI[d], v_ld)
                            if kg == 0 and pidx > 0:
                                plan.wait("tensor", sDV, ev_dv)
                            for kc in range(8):
                                kcg = kg * 8 + kc
                                for di in range(4):
                                    si, dh = di // 2, di % 2
                                    wt = ar.bf16(DW_W[d] + si * 1024, 2048)[:, (kc * 2 + dh) * 128:(kc * 2 + dh + 1) * 128]
                                    pT = pA if di < 2 else pB
                                    bank0 = (di % 2) * 1024
                                    for j in range(2):
                                        it = ar.bf16(DW_I[d], 8192)[:, kc * 1024 + j * 512: kc * 1024 + (j + 1) * 512]
                                        last = (kc == 7 and di == 3 and j == 1)
                                        comp("tensor", lambda E, o=pT[:, bank0 + j * 512: bank0 + (j + 1) * 512],
                                             w=wt, i=it, st=(kcg == 0), sp=(kcg == 127):
                                             E.matmul(o, w, i, start=st, stop=sp),
                                             sem=(sPE if last else None))
                            dn_kg_pe.append(sPE.n)
                        dn_pe = sPE.n
                        for di in range(4):
                            dglob = dg * 4 + di
                            pT = pA if di < 2 else pB
                            bank0 = (di % 2) * 1024
                            comp("vector", lambda E, o=ar.f32(DW_X + di * 1024, 1024),
                                 p=pT[:, bank0:bank0 + 1024], x=ar.f32(DW_X + di * 1024, 1024):
                                 E.scalar_tensor_tensor(o, p, 0.5, x, OP.mult, OP.add),
                                 sem=sDV, deps=[(sPE, dn_pe), (sDX, x_ld)])
                            v_ev = sDV.n
                            dma("gpsimd", dst_dram[dglob * 128:(dglob + 1) * 128, th * 1024:(th + 1) * 1024],
                                ar.f32(DW_X + di * 1024, 1024), sDS, deps=[(sDV, v_ev)])
                        ev_dv = sDV.n
                        pass_store.append((sDS, sDS.n))
                return (sDS, sDS.n), pass_store

            cut(3)
            def p05_per_m(mi):
                want = (mi + 1) * len(cast_jobs) // HC
                while P05["ji"] < want:
                    if not emit_p05_job():
                        break

            ffn1_st, ffn1_passes = emit_ffn(d_x, x1_buf.ap(), True,
                                            ln_extra=[job_done[-1], job_done[-2]],
                                            mid=emit_p05_rest, per_m=p05_per_m)
            cut(4)

            ar.cur = A0
            A_LN2 = ar.alloc(16384)
            A_H = ar.alloc(20544)
            W1B = [ar.alloc(1024), ar.alloc(1024)]
            sc2 = {"roll": A_H, "sqb": A_H + 4096, "xb": A_H + 6144,
                   "rows": A_H + 8192, "t1": A_H + 18432}
            plan.wait("sync", ffn1_st[0], ffn1_st[1])
            ln2_done = emit_layernorm(x1_buf.ap(), A_LN2, sc2)

            def h_view(c, col0, ncols, dtype_bf=True):
                off = A_H
                a = ar.bf16(off, 20544 * 2)
                return a[:, c * 2054 + col0: c * 2054 + col0 + ncols]

            def h_view3(col0, ncols):
                a = ar.bf16(A_H, 20544 * 2)[:, 0:20 * 2054]
                return a.rearrange("p (c t) -> p c t", t=2054)[:, :, col0:col0 + ncols]

            w1_pe = []
            w1_ev = []
            for r_ in range(20):
                d = r_ % 2
                deps = P05['done'] + [(sDV, ln2_done)]
                if r_ >= 2:
                    deps.append((sPE, w1_pe[r_ - 2]))
                v = dma("sync", ar.bf16(W1B[d], 2048).rearrange("p (t c) -> p t c", c=128),
                        attn_tile_ap(W1_BASE + r_ * 16, 16), sW[d], deps=deps)
                plan.wait("tensor", sW[d], v)
                if r_ == 0:
                    plan.wait("tensor", sDV, ln2_done)
                pT = pA if r_ % 2 == 0 else pB
                if r_ >= 2:
                    plan.wait("tensor", sSC, w1_ev[r_ - 2])
                for k in range(16):
                    w = ar.bf16(W1B[d], 2048)[:, k * 128:(k + 1) * 128]
                    for tau in range(4):
                        ln_sl = ar.bf16(A_LN2 + k * 1024, 2048)[:, tau * 512:(tau + 1) * 512]
                        last = (k == 15 and tau == 3)
                        comp("tensor", lambda E, o=pT[:, tau * 512:(tau + 1) * 512], w=w, i=ln_sl,
                             st=(k == 0), sp=(k == 15):
                             E.matmul(o, w, i, start=st, stop=sp),
                             sem=(sPE if last else None))
                w1_pe.append(sPE.n)
                for tau in range(4):
                    last = (tau == 3)
                    comp("scalar", lambda E, o=h_view(r_, 3 + tau * 512, 512),
                         i=pT[:, tau * 512:(tau + 1) * 512], b=vec_col(V_B1 + r_):
                         E.activation(o, i, AF.Silu, bias=b), sem=(sSC if last else None),
                         deps=[(sPE, w1_pe[r_])] if tau == 0 else [])
                w1_ev.append(sSC.n)
            w1_done = sSC.n

            E1S = ar.alloc(80)
            e1p = ar.bf16(E1S, 160)
            e1p3 = e1p.rearrange("p (c k) -> p c k", k=8)
            comp("vector", lambda E: E.tensor_copy(e1p3[:, :, 0:3], h_view3(3, 3)),
                 deps=[(sSC, w1_done)])
            comp("vector", lambda E: E.tensor_copy(e1p3[:, :, 3:6], h_view3(2048, 3)), sem=sDV)
            v = dma("gpsimd", e1_in.ap(), e1p, sE1S, deps=[(sDV, sDV.n)])
            plan.wait("gpsimd", sE1S, v)
            plan.ins("gpsimd", lambda E: E.collective_compute(
                "AllGather", OP.bypass, replica_groups=[list(range(8))],
                ins=[e1_in.ap().opt()], outs=[e1_out.ap().opt()]), inc=sCC)
            e1_done = sCC.n
            E1G = ar.alloc(8 * 80)
            v = None
            for j in range(8):
                v = dma("sync", ar.bf16(E1G + j * 80, 160), e1_out.ap()[j * 128:(j + 1) * 128, :],
                        sE1L, deps=[(sCC, e1_done)] if j == 0 else [])
            e1g_ld = sE1L.n
            ACCL = ar.alloc(120)
            accl = ar.f32(ACCL, 60).rearrange("p (c k) -> p c k", k=3)
            accr = ar.f32(ACCL + 60, 60).rearrange("p (c k) -> p c k", k=3)
            for j in range(8):
                g = ar.bf16(E1G + j * 80, 160).rearrange("p (c k) -> p c k", k=8)
                lsel = ar.f32(A_MSK + 16 + j, 1)
                rsel = ar.f32(A_MSK + 24 + j, 1)
                if j == 0:
                    comp("vector", lambda E, a=accl, b=g[:, :, 3:6], s=lsel:
                         E.tensor_scalar(a, b, s, None, OP.mult), deps=[(sE1L, e1g_ld)])
                    comp("vector", lambda E, a=accr, b=g[:, :, 0:3], s=rsel:
                         E.tensor_scalar(a, b, s, None, OP.mult))
                else:
                    comp("vector", lambda E, a=accl, b=g[:, :, 3:6], s=lsel:
                         E.scalar_tensor_tensor(a, b, s, a, OP.mult, OP.add))
                    comp("vector", lambda E, a=accr, b=g[:, :, 0:3], s=rsel:
                         E.scalar_tensor_tensor(a, b, s, a, OP.mult, OP.add))
            comp("vector", lambda E: E.tensor_copy(h_view3(0, 3), accl))
            comp("vector", lambda E: E.tensor_copy(h_view3(2051, 3), accr), sem=sDV)
            halo_done = sDV.n

            cut(5)
            TAIL = A_H + 20544
            assert TAIL + 10240 <= 48000, TAIL
            CVB = [A_LN2, TAIL]
            HCS = [A_LN2 + 10240, A_LN2 + 11264]
            cv_pe = []
            cv_st = []
            for o in range(20):
                d = o % 2
                deps = [(sDV, halo_done)]
                if o >= 2:
                    deps.append((sPE, cv_pe[o - 2]))
                v = dma("sync", ar.bf16(CVB[d], 10240).rearrange("p (t c) -> p t c", c=128),
                        attn_tile_ap(CV_BASE + o * 80, 80), sW[d], deps=deps)
                plan.wait("tensor", sW[d], v)
                plan.wait("tensor", sDV, halo_done)
                pT = pA if o % 2 == 0 else pB
                if o >= 2:
                    plan.wait("tensor", sDV, cv_st[o - 2])
                for tap in range(4):
                    for ic in range(20):
                        w = ar.bf16(CVB[d], 10240)[:, (tap * 20 + ic) * 128:(tap * 20 + ic + 1) * 128]
                        for tau in range(4):
                            i_ap = h_view(ic, tau * 512 + 2 * tap, 512)
                            last = (tap == 3 and ic == 19 and tau == 3)
                            comp("tensor", lambda E, oo=pT[:, tau * 512:(tau + 1) * 512], w=w, i=i_ap,
                                 st=(tap == 0 and ic == 0), sp=(tap == 3 and ic == 19):
                                 E.matmul(oo, w, i, start=st, stop=sp),
                                 sem=(sPE if last else None))
                cv_pe.append(sPE.n)
                comp("vector", lambda E, o_=ar.bf16(HCS[d], 2048), i=pT[:, :], b=vec_col(V_CB + o):
                     E.tensor_scalar(o_, i, b, None, OP.add), sem=sDV, deps=[(sPE, cv_pe[o])])
                v_hc = sDV.n
                dma("gpsimd", hc_buf.ap()[o], ar.bf16(HCS[d], 2048), sHS, deps=[(sDV, v_hc)])
                cv_st.append(v_hc)
            conv_st = sHS.n

            cut(6)
            ar.cur = A0
            P_HC = ar.alloc(5120)
            P_A = ar.alloc(10240)
            P_B = ar.alloc(10240)
            P_C = ar.alloc(10240)
            P_W = [ar.alloc(2560), ar.alloc(2560)]
            P_CAR = ar.alloc(48)
            gates_dv = 0

            hcv = ar.bf16(P_HC, 10240)
            Av = ar.f32(P_A, 10240)
            Bv = ar.f32(P_B, 10240)
            Cv = ar.f32(P_C, 10240)
            HLv = ar.bf16(P_C, 10240)
            ACv = ar.bf16(P_C + 5120, 10240)
            car_hl = ar.bf16(P_CAR, 40)
            car_ac = ar.bf16(P_CAR + 24, 40)

            ew_pe = []
            for tau in range(4):
                deps = [(sHS, conv_st)]
                if tau >= 1:
                    deps.append((sDV, gates_dv))
                    deps.append((sGP, gates_gp))
                for c in range(20):
                    dma("sync", hcv[:, c * 512:(c + 1) * 512],
                        hc_buf.ap()[c, :, tau * 512:(tau + 1) * 512], sHC,
                        deps=deps if c == 0 else [])
                hc_ld = sHC.n
                epe = []
                eev = []
                for e in range(20):
                    d = e % 2
                    depw = []
                    if e >= 2:
                        depw.append((sPE, epe[e - 2]))
                    v = dma("sync", ar.bf16(P_W[d], 2560).rearrange("p (t c) -> p t c", c=128),
                            attn_tile_ap(WA_BASE + e * 20, 20), sW[d], deps=depw)
                    v = dma("sync", ar.bf16(P_W[d] + 1280, 2560).rearrange("p (t c) -> p t c", c=128),
                            attn_tile_ap(WX_BASE + e * 20, 20), sW[d])
                    plan.wait("tensor", sW[d], v)
                    plan.wait("tensor", sHC, hc_ld)
                    if e >= 2:
                        plan.wait("tensor", sSC, eev[e - 2])
                    bnk_r = pA[:, (e % 2) * 512:(e % 2) * 512 + 512]
                    bnk_i = pA[:, 1024 + (e % 2) * 512:1024 + (e % 2) * 512 + 512]
                    for dc in range(20):
                        w = ar.bf16(P_W[d], 2560)[:, dc * 128:(dc + 1) * 128]
                        comp("tensor", lambda E, o=bnk_r, w_=w, i=hcv[:, dc * 512:(dc + 1) * 512],
                             st=(dc == 0), sp=(dc == 19):
                             E.matmul(o, w_, i, start=st, stop=sp))
                    for dc in range(20):
                        w = ar.bf16(P_W[d] + 1280, 2560)[:, dc * 128:(dc + 1) * 128]
                        last = (dc == 19)
                        comp("tensor", lambda E, o=bnk_i, w_=w, i=hcv[:, dc * 512:(dc + 1) * 512],
                             st=(dc == 0), sp=(dc == 19):
                             E.matmul(o, w_, i, start=st, stop=sp),
                             sem=(sPE if last else None))
                    epe.append(sPE.n)
                    comp("scalar", lambda E, o=Av[:, e * 512:(e + 1) * 512], i=bnk_r, b=vec_col(V_BA + e):
                         E.activation(o, i, AF.Sigmoid, bias=b), deps=[(sPE, epe[e])])
                    comp("scalar", lambda E, o=Bv[:, e * 512:(e + 1) * 512], i=bnk_i, b=vec_col(V_BX + e):
                         E.activation(o, i, AF.Sigmoid, bias=b), sem=sSC)
                    eev.append(sSC.n)
                    c = e
                    ch_a = Av[:, c * 512:(c + 1) * 512]
                    ch_b = Bv[:, c * 512:(c + 1) * 512]
                    ch_c = Cv[:, c * 512:(c + 1) * 512]
                    ch_h = hcv[:, c * 512:(c + 1) * 512]
                    depl = [(sSC, eev[c])] + ([sp_ready] if c == 0 else [])
                    comp("vector", lambda E, o=ch_a, s=ar.f32(A_SP + c, 1):
                         E.tensor_scalar(o, o, s, -8.0, OP.mult, OP.mult), sem=sDV, deps=depl)
                    v_lg = sDV.n
                    comp("scalar", lambda E, o=ch_a: E.activation(o, o, AF.Exp),
                         deps=[(sDV, v_lg)])
                    comp("scalar", lambda E, o=ch_c, i=ch_a: E.activation(o, i, AF.Square),
                         sem=sSC)
                    v_sq = sSC.n
                    comp("vector", lambda E, o=ch_c: E.tensor_scalar(o, o, -1.0, 1.0, OP.mult, OP.add),
                         sem=sDV, deps=[(sSC, v_sq)])
                    v_om = sDV.n
                    comp("scalar", lambda E, o=ch_c: E.activation(o, o, AF.Sqrt),
                         sem=sSC, deps=[(sDV, v_om)])
                    v_sr = sSC.n
                    comp("vector", lambda E, o=ch_b, h=ch_h: E.tensor_tensor(o, o, h, OP.mult))
                    comp("vector", lambda E, o=ch_b, m=ch_c: E.tensor_tensor(o, o, m, OP.mult),
                         sem=sDV, deps=[(sSC, v_sr)])
                v_b = sDV.n
                plan.wait("vector", sHLS, sHLS.n)
                for c in range(20):
                    if tau == 0:
                        comp("vector", lambda E, o=HLv[:, c * 512:(c + 1) * 512],
                             a=Av[:, c * 512:(c + 1) * 512], b=Bv[:, c * 512:(c + 1) * 512]:
                             E.tensor_tensor_scan(o, a, b, 0.0, OP.mult, OP.add))
                        comp("vector", lambda E, o=ACv[:, c * 512:(c + 1) * 512],
                             a=Av[:, c * 512:(c + 1) * 512]:
                             E.tensor_tensor_scan(o, a, zeros512, 1.0, OP.mult, OP.add))
                    else:
                        comp("vector", lambda E, o=HLv[:, c * 512:(c + 1) * 512],
                             a=Av[:, c * 512:(c + 1) * 512], b=Bv[:, c * 512:(c + 1) * 512],
                             i=car_hl[:, c:c + 1]:
                             E.tensor_tensor_scan(o, a, b, i, OP.mult, OP.add))
                        comp("vector", lambda E, o=ACv[:, c * 512:(c + 1) * 512],
                             a=Av[:, c * 512:(c + 1) * 512], i=car_ac[:, c:c + 1]:
                             E.tensor_tensor_scan(o, a, zeros512, i, OP.mult, OP.add))
                hl_last = HLv.rearrange("p (c t) -> p c t", t=512)[:, :, 511]
                ac_last = ACv.rearrange("p (c t) -> p c t", t=512)[:, :, 511]
                comp("vector", lambda E, o=car_hl[:, 0:20], i=hl_last: E.tensor_copy(o, i))
                comp("vector", lambda E, o=car_ac[:, 0:20], i=ac_last: E.tensor_copy(o, i),
                     sem=sDV)
                v_scan = sDV.n
                v_scan_gp = sGP.n
                for c in range(20):
                    dma("gpsimd", hl_buf.ap()[c, :, tau * 512:(tau + 1) * 512],
                        HLv[:, c * 512:(c + 1) * 512], sHLS,
                        deps=[(sDV, v_scan), (sGP, v_scan_gp)] if c == 0 else [])
                for c in range(20):
                    dma("gpsimd", ac_buf.ap()[c, :, tau * 512:(tau + 1) * 512],
                        ACv[:, c * 512:(c + 1) * 512], sHLS)
                gates_dv = v_scan
                gates_gp = v_scan_gp
            hlac_st = sHLS.n

            E2S = ar.alloc(20)
            e2p = ar.bf16(E2S, 40).rearrange("p (c k) -> p c k", k=2)
            comp("vector", lambda E: E.tensor_copy(e2p[:, :, 0], car_ac[:, 0:20]))
            comp("vector", lambda E: E.tensor_copy(e2p[:, :, 1], car_hl[:, 0:20]), sem=sDV)
            v = dma("gpsimd", e2_in.ap(), ar.bf16(E2S, 40), sE2S, deps=[(sDV, sDV.n)])
            plan.wait("gpsimd", sE2S, v)
            plan.ins("gpsimd", lambda E: E.collective_compute(
                "AllGather", OP.bypass, replica_groups=[list(range(8))],
                ins=[e2_in.ap().opt()], outs=[e2_out.ap().opt()]), inc=sCC)
            e2_done = sCC.n
            ar.cur = A0
            E2G = ar.alloc(8 * 20)
            for j in range(8):
                dma("sync", ar.bf16(E2G + j * 20, 40), e2_out.ap()[j * 128:(j + 1) * 128, :],
                    sE2L, deps=[(sCC, e2_done), (sHLS, hlac_st)] if j == 0 else [])
            e2g_ld = sE2L.n
            CFOLD = ar.alloc(9 * 20)
            HIN = ar.alloc(20)
            h_in = ar.f32(HIN, 20)
            comp("vector", lambda E: E.memset(ar.f32(CFOLD, 20), 0.0), deps=[(sE2L, e2g_ld)])
            comp("vector", lambda E: E.memset(h_in, 0.0))
            for j in range(7):
                g = ar.bf16(E2G + j * 20, 40).rearrange("p (c k) -> p c k", k=2)
                cj = ar.f32(CFOLD + j * 20, 20)
                cj1 = ar.f32(CFOLD + (j + 1) * 20, 20)
                comp("vector", lambda E, o=cj1, a=g[:, :, 0], c_=cj: E.tensor_tensor(o, a, c_, OP.mult))
                comp("vector", lambda E, o=cj1, b=g[:, :, 1]: E.tensor_tensor(o, o, b, OP.add))
                comp("vector", lambda E, o=cj1, s=ar.f32(A_MSK + j + 1, 1):
                     E.tensor_scalar(o, o, s, None, OP.mult))
            for j in range(1, 8):
                cj = ar.f32(CFOLD + j * 20, 20)
                comp("vector", lambda E, o=h_in, c_=cj, s=ar.f32(A_MSK + 8 + j, 1):
                     E.scalar_tensor_tensor(o, c_, s, o, OP.mult, OP.add),
                     sem=(sDV if j == 7 else None))
            hin_done = sDV.n

            cut(7)
            P3_HL = [ar.alloc(5120), ar.alloc(5120)]
            P3_AC = [ar.alloc(5120), ar.alloc(5120)]
            P3_HC = [ar.alloc(5120), ar.alloc(5120)]
            P3_W = [ar.alloc(1280), ar.alloc(1280)]
            P3_X = [ar.alloc(512), ar.alloc(512)]
            P3_O = [ar.alloc(512), ar.alloc(512)]
            dst_marks = []
            p3_dv = 0
            x2_st = 0
            p3_pe_marks = []
            for tau in range(4):
                tp = tau % 2
                hl_t = ar.bf16(P3_HL[tp], 10240)
                ac_t = ar.bf16(P3_AC[tp], 10240)
                hcor = ar.bf16(P3_HC[tp], 10240)
                deps = [(sHLS, hlac_st), (sDV, hin_done)]
                if tau >= 2:
                    deps.append((sPE, p3_pe_marks[tau - 2]))
                for c in range(20):
                    dma("sync", hl_t[:, c * 512:(c + 1) * 512],
                        hl_buf.ap()[c, :, tau * 512:(tau + 1) * 512], sHL,
                        deps=deps if c == 0 else [])
                for c in range(20):
                    dma("sync", ac_t[:, c * 512:(c + 1) * 512],
                        ac_buf.ap()[c, :, tau * 512:(tau + 1) * 512], sHL)
                v_ld = sHL.n
                for c in range(20):
                    comp("vector", lambda E, o=hcor[:, c * 512:(c + 1) * 512],
                         a=ac_t[:, c * 512:(c + 1) * 512], b=hl_t[:, c * 512:(c + 1) * 512],
                         s=ar.f32(HIN + c, 1):
                         E.scalar_tensor_tensor(o, a, s, b, OP.mult, OP.add),
                         sem=(sDV if c == 19 else None),
                         deps=[(sHL, v_ld), (sDV, hin_done)] if c == 0 else [])
                v_cor = sDV.n
                dpe = []
                dev = []
                for dd in range(16):
                    d = dd % 2
                    depw = []
                    if dd >= 2:
                        depw.append((sPE, dpe[dd - 2]))
                    if len(dst_marks) >= 2:
                        depw.append(dst_marks[-2])
                    v = dma("sync", ar.bf16(P3_W[d], 2560).rearrange("p (t c) -> p t c", c=128),
                            attn_tile_ap(W2_BASE + dd * 20, 20), sW[d], deps=depw)
                    vx = dma("sync", ar.f32(P3_X[d], 512),
                             x1_buf.ap()[dd * 128:(dd + 1) * 128, tau * 512:(tau + 1) * 512], sX[d])
                    plan.wait("tensor", sW[d], v)
                    plan.wait("tensor", sDV, v_cor)
                    if dd >= 4:
                        plan.wait("tensor", sDV, dev[dd - 4])
                    bank = [pA[:, 0:512], pA[:, 512:1024], pB[:, 0:512], pB[:, 512:1024]][dd % 4]
                    for rc_ in range(20):
                        w = ar.bf16(P3_W[d], 2560)[:, rc_ * 128:(rc_ + 1) * 128]
                        last = (rc_ == 19)
                        comp("tensor", lambda E, o=bank, w_=w, i=hcor[:, rc_ * 512:(rc_ + 1) * 512],
                             st=(rc_ == 0), sp=(rc_ == 19):
                             E.matmul(o, w_, i, start=st, stop=sp),
                             sem=(sPE if last else None))
                    dpe.append(sPE.n)
                    comp("vector", lambda E, o=ar.f32(P3_O[d], 512), p=bank,
                         b=vec_col(V_B2 + dd), x=ar.f32(P3_X[d], 512):
                         E.scalar_tensor_tensor(o, p, b, x, OP.add, OP.add),
                         sem=sDV, deps=[(sPE, dpe[dd]), (sX[d], vx)])
                    dev.append(sDV.n)
                    dma("gpsimd", x2_buf.ap()[dd * 128:(dd + 1) * 128, tau * 512:(tau + 1) * 512],
                        ar.f32(P3_O[d], 512), sO[d], deps=[(sDV, dev[dd])])
                    dst_marks.append((sO[d], sO[d].n))
                p3_dv = sDV.n
                p3_pe_marks.append(sPE.n)
            x2_st = [(sO0, sO0.n), (sO1, sO1.n)]

            cut(8)
            for s_, v_ in x2_st:
                plan.wait("sync", s_, v_)
            out_st, _ = emit_ffn(x2_buf.ap(), d_out, False)

        try:
            _emit_all()
        except _Cut:
            pass
        for nm_ in SEM_NAMES:
            plan.wait("sync", sems[nm_], sems[nm_].n)
        if os.environ.get("ATH_DBG"):
            dbgs = [("dbg_x1", x1_buf, [D, T], F32),
                    ("dbg_hc", hc_buf, [RC, 128, T], BF16),
                    ("dbg_hl", hl_buf, [RC, 128, T], BF16),
                    ("dbg_ac", ac_buf, [RC, 128, T], BF16),
                    ("dbg_x2", x2_buf, [D, T], F32),
                    ("dbg_ccB", ccB_out, [16384, 2048], BF16),
                    ("dbg_attnbf", attn_bf, [3040 * TILE], BF16),
                    ("dbg_inter", inter_buf, [H // 128, 128, T], BF16)]
            for nm_, buf_, shp_, dt_ in dbgs:
                o_ = nc.dram_tensor(nm_, shp_, dt_, kind="ExternalOutput").ap()
                sST.inc(16)
                plan.emit("sync", lambda E, a=o_, b=buf_.ap(): E.dma_start(out=a, in_=b).then_inc(sST.h, 16))
            o_ar = nc.dram_tensor("dbg_arena", [128, 48000], F32, kind="ExternalOutput").ap()
            sST.inc(16)
            plan.emit("sync", lambda E, a=o_ar: E.dma_start(out=a, in_=ar_t[:, :]).then_inc(sST.h, 16))
            plan.wait("sync", sST, sST.n)


        with nc.Block() as block:
            @block.sync
            def _(E):
                for fn in plan.q["sync"]:
                    fn(E)

            @block.gpsimd
            def _(E):
                for fn in plan.q["gpsimd"]:
                    fn(E)

            @block.vector
            def _(E):
                for fn in plan.q["vector"]:
                    fn(E)

            @block.scalar
            def _(E):
                for fn in plan.q["scalar"]:
                    fn(E)

            @block.tensor
            def _(E):
                for fn in plan.q["tensor"]:
                    fn(E)

    return nc


_CACHE = {}


def kernel(**inputs):
    x = np.asarray(inputs["L0_attn_x"], np.float32)
    w1 = np.asarray(inputs["L0_attn_w1"], np.float32)
    b1 = np.asarray(inputs["L0_attn_b1"], np.float32)
    conv_w = np.asarray(inputs["L0_attn_conv_w"], np.float32)
    conv_b = np.asarray(inputs["L0_attn_conv_b"], np.float32)
    wa = np.asarray(inputs["L0_attn_wa"], np.float32)
    ba = np.asarray(inputs["L0_attn_ba"], np.float32)
    wx = np.asarray(inputs["L0_attn_wx"], np.float32)
    bx = np.asarray(inputs["L0_attn_bx"], np.float32)
    lam = np.asarray(inputs["L0_attn_lam"], np.float32)
    w2 = np.asarray(inputs["L0_attn_w2"], np.float32)
    b2 = np.asarray(inputs["L0_attn_b2"], np.float32)
    gpk = np.ascontiguousarray(np.asarray(inputs["L0_ffn_gate_W_packed"], np.int32))
    gsc = np.ascontiguousarray(np.asarray(inputs["L0_ffn_gate_scales"], np.float32))
    upk = np.ascontiguousarray(np.asarray(inputs["L0_ffn_up_W_packed"], np.int32))
    usc = np.ascontiguousarray(np.asarray(inputs["L0_ffn_up_scales"], np.float32))
    dpk = np.ascontiguousarray(np.asarray(inputs["L0_ffn_down_W_packed"], np.int32))
    dsc = np.ascontiguousarray(np.asarray(inputs["L0_ffn_down_scales"], np.float32))

    if "nc" not in _CACHE:
        _CACHE["nc"] = build_program()
    nc = _CACHE["nc"]

    w1T = np.ascontiguousarray(w1.T)
    convT = np.ascontiguousarray(conv_w.transpose(2, 1, 0))
    waT = np.ascontiguousarray(wa.T)
    wxT = np.ascontiguousarray(wx.T)
    w2T = np.ascontiguousarray(w2.T)

    def chunkify(v, n):
        return np.ascontiguousarray(v.reshape(n, 128).T)

    vecs = np.zeros((128, 128), np.float32)
    vecs[:, 0:20] = chunkify(b1, 20)
    vecs[:, 20:40] = chunkify(conv_b, 20)
    vecs[:, 40:60] = chunkify(ba, 20)
    vecs[:, 60:80] = chunkify(bx, 20)
    vecs[:, 80:100] = chunkify(lam, 20)
    vecs[:, 100:116] = chunkify(b2, 16)
    consts = np.array([GC3, GC1, 0.5, 7.5, -4.0, -6.0, -1.0, 1.0], np.float32)
    vecs[:, 116:124] = np.tile(consts[None, :], (128, 1))

    ones = np.zeros((128, 130), BF)
    ones[:, 0] = BF(1.0)
    ones[0, 1:129] = BF(1.0)
    ident = np.eye(128, dtype=BF)

    in_maps = []
    for c in range(8):
        b = c // 4
        t0 = 2048 * (c % 4)
        msk = np.zeros((128, 32), np.float32)
        for j in range(8):
            msk[:, j] = 0.0 if j in (0, 4) else 1.0
        msk[:, 8 + c] = 1.0
        if c not in (0, 4):
            msk[:, 16 + (c - 1)] = 1.0
        if c not in (3, 7):
            msk[:, 24 + (c + 1)] = 1.0
        im = {
            "x_fm": np.ascontiguousarray(x[b, t0:t0 + T, :].T),
            "w1T": w1T, "convT": convT, "waT": waT, "wxT": wxT, "w2T": w2T,
            "vecs": vecs, "onesb": ones, "ident": ident, "msk": msk,
            "gpk": np.ascontiguousarray(gpk[c * GSL:(c + 1) * GSL]),
            "gsc": np.ascontiguousarray(gsc[c * GSL:(c + 1) * GSL]),
            "upk": np.ascontiguousarray(upk[c * GSL:(c + 1) * GSL]),
            "usc": np.ascontiguousarray(usc[c * GSL:(c + 1) * GSL]),
            "dpk": np.ascontiguousarray(dpk[c * DSL:(c + 1) * DSL]),
            "dsc": np.ascontiguousarray(dsc[c * DSL:(c + 1) * DSL]),
        }
        in_maps.append(im)

    res = run_bass_kernel_spmd(nc, in_maps, core_ids=list(range(8)))

    out = np.empty((2, 8192, 2048), np.float32)
    for c in range(8):
        b = c // 4
        t0 = 2048 * (c % 4)
        out[b, t0:t0 + T, :] = np.asarray(res.results[c]["out_fm"]).T
    return out
